# version 06
# baseline (speedup 1.0000x reference)
import numpy as np
import ml_dtypes
from contextlib import ExitStack

import concourse.bass as bass
import concourse.mybir as mybir
from concourse.bass_utils import run_bass_kernel_spmd

F32 = mybir.dt.float32
BF16 = mybir.dt.bfloat16
I32 = mybir.dt.int32
U8 = mybir.dt.uint8
ALU = mybir.AluOpType
AF = mybir.ActivationFunctionType
AX = mybir.AxisListType.X

T, NT, D, ND = 2048, 16, 1024, 8
NH, NHP, HD = 16, 8, 64
FF, NF = 3072, 24
C, NCH, SCT, NSC = 128, 16, 512, 4
QMAX = 32767.0
EPS = 1e-5
ARENA_B = 211456

ENGS = ("sync", "tensor", "vector", "scalar", "gpsimd")

_TAP = None
_STOP = None



class Sem:
    def __init__(self, st, nc, name):
        self.h = st.enter_context(nc.semaphore(name))
        self.n = 0

    def bump(self, k=1):
        self.n += k
        return self.n


class Plan:
    def __init__(self):
        self.ops = {e: [] for e in ENGS}
        self.marks = {}

    def mark(self, name):
        self.marks[name] = {e: len(self.ops[e]) for e in ENGS}


def build_nc():
    nc = bass.Bass("TRN2", target_bir_lowering=False)

    def inp(name, shape, dt=F32):
        return nc.dram_tensor(name, shape, dt, kind="ExternalInput").ap()

    x_h = inp("x", [T, D])
    wq_h, wk_h, wv_h, wo_h = (inp(n, [D, D]) for n in ("wq", "wk", "wv", "wo"))
    wg1_h, wg2_h = inp("wg1", [D, HD]), inp("wg2", [HD, D])
    lnw_h, lnb_h = inp("lnw", [128, ND]), inp("lnb", [128, ND])
    wgate_h, wup_h = inp("wgate", [FF, D]), inp("wup", [FF, D])
    wdown_h = inp("wdown", [D, FF])
    cos_h, sin_h = inp("cosb", [128, T], BF16), inp("sinb", [128, T], BF16)
    maskT_h = inp("maskt", [128, 128], BF16)
    p2l_h = inp("p2l", [128, 128], BF16)
    onesc_h = inp("onesc", [128, 1], BF16)
    onesr_h = inp("onesr", [1, 128])
    eo_h = inp("eo", [128, 2])
    out_h = nc.dram_tensor("out", [T, D], F32, kind="ExternalOutput").ap()

    st_in = nc.dram_tensor("st_in", [128, 512], F32).ap()
    st_out = nc.dram_tensor("st_out", [128, 512], F32).ap()
    bar_in = [nc.dram_tensor(f"bar{i}_in", [1, 8], F32).ap() for i in range(4)]
    bar_out = [nc.dram_tensor(f"bar{i}_out", [1, 8], F32).ap() for i in range(4)]
    g_spill = nc.dram_tensor("g_spill", [FF, T], BF16).ap()
    u_spill = nc.dram_tensor("u_spill", [FF, T], BF16).ap()
    x1_spill = nc.dram_tensor("x1_spill", [T, D], F32).ap()
    wgdq = nc.dram_tensor("wgdq", [128, 24576], BF16).ap()
    wudq = nc.dram_tensor("wudq", [128, 24576], BF16).ap()
    wddq = nc.dram_tensor("wddq", [128, 24576], BF16).ap()

    with ExitStack() as est:
        arena = est.enter_context(nc.sbuf_tensor("arena", [128, ARENA_B], U8))
        cur = [0]

        def carve(nbytes, align=64):
            off = (cur[0] + align - 1) // align * align
            cur[0] = off + nbytes
            assert cur[0] <= ARENA_B, f"arena overflow {cur[0]}"
            return off

        def vvw(off, nbytes, dt, rows=128, r0=0):
            return arena[r0:r0 + rows, off:off + nbytes].bitcast(dt)

        cos_b = vvw(carve(4096), 4096, BF16)
        sin_b = vvw(carve(4096), 4096, BF16)
        maskT_b = vvw(carve(256), 256, BF16)
        p2l_b = vvw(carve(256), 256, BF16)
        onesc_b = vvw(carve(64), 2, BF16)
        onesr_f = vvw(carve(512), 512, F32, rows=1)
        lnw_s = vvw(carve(64), 32, F32)
        lnb_s = vvw(carve(64), 32, F32)
        eo_s = vvw(carve(64), 8, F32)
        wg1_b = vvw(carve(1024), 1024, BF16)
        o_g1 = carve(4096)
        g1_b = vvw(o_g1, 4096, BF16, rows=64)
        mx_acc = vvw(carve(128), 128, F32)
        gm_acc = vvw(carve(128), 128, F32)
        um_acc = vvw(carve(128), 128, F32)
        row_s = vvw(carve(512), 512, F32, rows=1)
        sc_s = vvw(carve(64), 32, F32, rows=1)
        res_s = vvw(carve(64), 32, F32, rows=1)
        scl_bc = vvw(carve(64), 48, F32)
        st1_s = vvw(carve(64), 32, F32)
        o_mu = carve(2048)
        mu_row = vvw(o_mu, 2048, F32, rows=1)
        o_rr = carve(2048)
        rr_row = vvw(o_rr, 2048, F32, rows=1)
        tmp1_s = vvw(carve(64), 32, F32, rows=1)
        eps_s = vvw(carve(64), 4, F32)
        kf_j = vvw(carve(2048), 2048, BF16)
        v_j = vvw(carve(2048), 2048, BF16)
        o_scr1 = carve(4096)
        scr1_b = vvw(o_scr1, 4096, BF16)
        scr1_f = vvw(o_scr1, 4096, F32)
        kt_j = vvw(o_scr1 + 2048, 2048, BF16)
        e_sb = vvw(carve(4096), 4096, BF16)
        o_sz = carve(2048)
        s_z = vvw(o_sz, 2048, BF16)
        s_iz = s_z
        o_scrf = carve(4096)
        scrf_f = vvw(o_scrf, 4096, F32)
        scrf_i = vvw(o_scrf, 4096, I32)
        s_fin = vvw(o_scrf + 2048, 2048, F32)
        o_stg = carve(8192)
        stg_f = [vvw(o_stg, 4096, F32), vvw(o_stg + 4096, 4096, F32)]
        kf_z = [vvw(o_stg, 2048, BF16), vvw(o_stg + 2048, 2048, BF16)]
        winA_b = vvw(o_mu, 2048, BF16)
        winB_b = vvw(o_rr, 2048, BF16)
        stg1_i = vvw(o_stg + 4096, 4096, I32)
        bm_s = vvw(carve(512), 384, F32)
        br_s = vvw(carve(512), 384, F32)
        globals()["_DBG"] = dict(scrf=o_scrf, stg=o_stg)

        o_biga = carve(32768)
        o_bigb = carve(32768)
        o_bige = carve(32768)
        o_wk = carve(16384)
        o_wv = carve(16384)
        o_wo = carve(16384)
        o_wcur = carve(16384)
        assert o_wk == o_bige + 32768 and o_wo == o_wv + 16384
        assert o_wcur == o_wo + 16384 and o_bigb == o_biga + 32768

        xT_b = vvw(o_biga, 32768, BF16)
        xqT_b = xT_b
        q_b = vvw(o_bigb, 32768, BF16)
        x_tmb = vvw(o_bige, 32768, BF16)
        og_b = vvw(o_bige, 32768, BF16)
        wk_b = vvw(o_wk, 16384, BF16)
        wv_b = vvw(o_wv, 16384, BF16)
        wo_b = vvw(o_wo, 16384, BF16)
        wq_b = vvw(o_wcur, 16384, BF16)
        wg2_b = vvw(o_wcur, 2048, BF16, rows=64)
        wgT_b = vvw(o_bige, 49152, BF16)
        wuT_b = vvw(o_wv, 49152, BF16)
        g2st = [vvw(o_bigb, 4096, BF16), vvw(o_bigb + 4096, 4096, BF16)]
        oost = [vvw(o_bigb + 8192, 4096, BF16),
                vvw(o_bigb + 12288, 4096, BF16)]
        o_tile = [vvw(o_biga, 6144, BF16), vvw(o_biga + 6144, 6144, BF16)]
        gq_c = vvw(o_bigb + 16384, 4096, BF16)
        sig_c = vvw(o_bigb + 16384 + 4096, 4096, BF16)
        sgq_c = vvw(o_bigb + 16384 + 8192, 4096, BF16)
        xqt_st = [vvw(o_bigb + 16384 + 12288, 2048, BF16),
                  vvw(o_bigb + 16384 + 14336, 2048, BF16)]
        wn6_b = vvw(o_bigb + 16384, 12288, BF16)

        pA = est.enter_context(nc.psum_tensor("pA", [128, 512], F32))
        pB = est.enter_context(nc.psum_tensor("pB", [128, 512], F32))
        pC = est.enter_context(nc.psum_tensor("pC", [128, 512], F32))
        pE1 = est.enter_context(nc.psum_tensor("pE1", [128, 512], F32))
        pE2 = est.enter_context(nc.psum_tensor("pE2", [128, 512], F32))
        pO = est.enter_context(nc.psum_tensor("pO", [128, 1024], F32))
        pS = est.enter_context(nc.psum_tensor("pS", [128, 512], F32))
        rot3 = (pA, pB, pC)

        ld = Sem(est, nc, "ld")
        sld = [Sem(est, nc, "sld0"), Sem(est, nc, "sld1")]
        strx = Sem(est, nc, "strx")
        skt = Sem(est, nc, "skt")
        stq = [Sem(est, nc, "stq0"), Sem(est, nc, "stq1")]
        swg = Sem(est, nc, "swg")
        swu = Sem(est, nc, "swu")
        swd = Sem(est, nc, "swd")
        sot = [Sem(est, nc, "sot0"), Sem(est, nc, "sot1")]
        srb = [Sem(est, nc, "srb0"), Sem(est, nc, "srb1")]
        sxq = [Sem(est, nc, "sxq0"), Sem(est, nc, "sxq1")]
        ssp = [Sem(est, nc, "ssp0"), Sem(est, nc, "ssp1")]
        sou = [Sem(est, nc, "sou0"), Sem(est, nc, "sou1")]
        ldg = Sem(est, nc, "ldg")
        spl = Sem(est, nc, "spl")
        spt = Sem(est, nc, "spt")
        sps = Sem(est, nc, "sps")
        pe = Sem(est, nc, "pe")
        dv = Sem(est, nc, "dv")
        sc = Sem(est, nc, "sc")
        cc = Sem(est, nc, "cc")

        plan = Plan()

        dbg_h = None
        dbg_sem = [None]
        if _TAP is not None:
            dbg_h = nc.dram_tensor("dbg", [128, 16384], F32,
                                   kind="ExternalOutput").ap()
            dbg_sem[0] = Sem(est, nc, "dbgsem")

        def Y(fn): plan.ops["sync"].append(fn)
        def TN(fn): plan.ops["tensor"].append(fn)

        def tap(name, src_ap, wait=None, dram_src=None):
            if _TAP != name:
                return
            ds = dbg_sem[0]
            ds.bump(16)
            Y(lambda q, s=src_ap, w=wait, d=dram_src:
              (w(q) if w else None,
               q.dma_start(out=dbg_h[:, 0:(d if d else s).free_size()
                               ] if False else _tap_out(dbg_h, s, d),
                           in_=(d if d is not None else s)).then_inc(ds.h, 16)))

        def V(fn): plan.ops["vector"].append(fn)
        def S(fn): plan.ops["scalar"].append(fn)
        def G(fn): plan.ops["gpsimd"].append(fn)

        dv.bump()
        V(lambda v: (v.memset(eps_s, EPS),
                     v.memset(sc_s, 0.0),
                     v.memset(res_s, 0.0),
                     v.memset(tmp1_s, 0.0),
                     v.memset(mx_acc, 0.0),
                     v.memset(gm_acc, 0.0),
                     v.memset(um_acc, 0.0),
                     v.memset(scl_bc, 0.0).then_inc(dv.h, 1)))

        for dst, src in ((cos_b, cos_h), (sin_b, sin_h), (maskT_b, maskT_h),
                         (p2l_b, p2l_h), (onesc_b, onesc_h), (onesr_f, onesr_h),
                         (lnw_s, lnw_h), (lnb_s, lnb_h), (eo_s, eo_h)):
            ld.bump(16)
            Y(lambda q, d=dst, s=src: q.dma_start(out=d, in_=s).then_inc(ld.h, 16))
        consts_ld = ld.n

        def load_cast(dst_b, src_h, nchunks, width, wslot):
            for k in range(nchunks):
                sl = sld[k % 2]
                wprev = dv.n - 1
                tgt = sl.bump(16)
                Y(lambda q, k=k, w=wprev, s=src_h, wd=width, sl=sl:
                  (q.wait_ge(dv.h, max(0, w)),
                   q.dma_start(out=stg_f[k % 2][:, :wd],
                               in_=s[128 * k:128 * (k + 1), :]
                               ).then_inc(sl.h, 16)))
                dv.bump()
                V(lambda v, k=k, tgt=tgt, d=dst_b, wd=width, ws=wslot, sl=sl:
                  (v.wait_ge(sl.h, tgt),
                   v.tensor_copy(d[:, ws * k:ws * k + wd],
                                 stg_f[k % 2][:, :wd]).then_inc(dv.h, 1)))

        load_cast(x_tmb, x_h, NT, 1024, 1024)
        x_cast_dv = dv.n

        for j in range(NT):
            strx.bump(16)
            Y(lambda q, j=j, w=x_cast_dv:
              (q.wait_ge(dv.h, w),
               q.dma_start_transpose(
                   out=xT_b[:].rearrange("p (k t) -> p k t", t=2048
                                         )[:, :, 128 * j:128 * (j + 1)],
                   in_=x_tmb[:, 1024 * j:1024 * (j + 1)].rearrange(
                       "t (k h) -> t k h", h=128),
               ).then_inc(strx.h, 16)))
        xT_tr = strx.n
        tap("xT", xT_b, wait=(lambda q: q.wait_ge(strx.h, xT_tr)))

        load_cast(wq_b, wq_h, ND, 1024, 1024)
        load_cast(wk_b, wk_h, ND, 1024, 1024)
        load_cast(wv_b, wv_h, ND, 1024, 1024)
        load_cast(wo_b, wo_h, ND, 1024, 1024)
        load_cast(wg1_b, wg1_h, ND, 64, 64)
        allw_dv = dv.n
        tap("wq", wq_b, wait=(lambda q, w=allw_dv: q.wait_ge(dv.h, w)))

        plan.mark("p0")
        def mm_group(ps_ap, pairs, pre=None):
            n = pe.bump()
            def f(tn, P=list(pairs), pre=pre, ps=ps_ap):
                if pre:
                    pre(tn)
                for i, (l, r) in enumerate(P):
                    ins = tn.matmul(ps, l, r,
                                    start=(i == 0), stop=(i == len(P) - 1))
                ins.then_inc(pe.h, 1)
            TN(f)
            return n

        g1_sc0 = sc.n
        g1_pes = []
        for tt in range(4):
            g1_pes.append(mm_group(
                rot3[tt % 3][0:64, 0:512],
                [(wg1_b[:, 64 * k:64 * (k + 1)],
                  xT_b[:, 2048 * k + 512 * tt:2048 * k + 512 * (tt + 1)])
                 for k in range(ND)],
                pre=(lambda tn, t=tt:
                     (tn.wait_ge(strx.h, xT_tr), tn.wait_ge(dv.h, allw_dv),
                      tn.wait_ge(ld.h, consts_ld),
                      tn.wait_ge(sc.h, g1_sc0 + t - 2) if t >= 3 else None)))
            )
        for tt in range(4):
            sc.bump()
            S(lambda s, t=tt, w=g1_pes[tt]:
              (s.wait_ge(pe.h, w),
               s.copy(g1_b[0:64, 512 * t:512 * (t + 1)],
                      rot3[t % 3][0:64, 0:512]).then_inc(sc.h, 1)))
        g1_sc = sc.n

        q_sc0 = sc.n
        for i in range(32):
            m, tt = divmod(i, 4)
            mm_group(
                rot3[i % 3][:, 0:512],
                [(wq_b[:, 1024 * k + 128 * m:1024 * k + 128 * (m + 1)],
                  xT_b[:, 2048 * k + 512 * tt:2048 * k + 512 * (tt + 1)])
                 for k in range(ND)],
                pre=(lambda tn, i=i:
                     (tn.wait_ge(sc.h, g1_sc),
                      tn.wait_ge(sc.h, q_sc0 + i - 2) if i >= 3 else None)))
            sc.bump()
            S(lambda s, i=i, m=m, tt=tt, w=pe.n:
              (s.wait_ge(pe.h, w),
               s.activation(q_b[:, 2048 * m + 512 * tt:2048 * m + 512 * (tt + 1)],
                            rot3[i % 3][:, 0:512], AF.Silu).then_inc(sc.h, 1)))
        q_sc = sc.n
        tap("qsilu", q_b, wait=(lambda q, w=q_sc: q.wait_ge(sc.h, w)))

        q_pe_done = pe.n
        sl = sld[0]
        wprev = dv.n - 1
        tgt = sl.bump(16)
        Y(lambda q, w=wprev, sl=sl, qp=q_pe_done:
          (q.wait_ge(dv.h, max(0, w)),
           q.wait_ge(pe.h, qp),
           q.dma_start(out=stg_f[0][0:64, :], in_=wg2_h).then_inc(sl.h, 16)))
        dv.bump()
        V(lambda v, tgt=tgt, sl=sl:
          (v.wait_ge(sl.h, tgt),
           v.tensor_copy(wg2_b[0:64, :], stg_f[0][0:64, :]).then_inc(dv.h, 1)))
        wg2_dv = dv.n

        def rope(buf, m_off, tok0, ntok, v_pre=None):
            u = scr1_b[:, 0:ntok]
            cw = cos_b[:, tok0:tok0 + ntok]
            sw = sin_b[:, tok0:tok0 + ntok]
            pe_at = pe.n
            n1 = dv.bump()
            V(lambda v, vp=v_pre, pa=pe_at:
              (vp(v) if vp else None,
               v.wait_ge(pe.h, pa),
               v.tensor_mul(u, buf[:, m_off:m_off + ntok], sw),
               v.tensor_mul(buf[:, m_off:m_off + ntok],
                            buf[:, m_off:m_off + ntok], cw).then_inc(dv.h, 1)))
            tap("u%d" % (m_off // 2048), u,
                wait=(lambda q, w=n1: q.wait_ge(dv.h, w)))
            nj = (ntok + 511) // 512
            add_dvs = []
            for j in range(nj):
                wdt = min(512, ntok - 512 * j)
                ps = (pA, pB)[j % 2]
                pn = pe.bump()
                TN(lambda tn, j=j, ps=ps, wdt=wdt, n1=n1,
                   ag=(add_dvs[j - 2] if j >= 2 else None):
                   (tn.wait_ge(dv.h, n1),
                    tn.wait_ge(dv.h, ag) if ag is not None else None,
                    tn.matmul(ps[:, 0:wdt], p2l_b, u[:, 512 * j:512 * j + wdt],
                              start=True, stop=True).then_inc(pe.h, 1)))
                add_dvs.append(dv.bump())
                V(lambda v, j=j, ps=ps, wdt=wdt, pn=pn, mo=m_off:
                  (v.wait_ge(pe.h, pn),
                   v.tensor_add(buf[:, mo + 512 * j:mo + 512 * j + wdt],
                                buf[:, mo + 512 * j:mo + 512 * j + wdt],
                                ps[:, 0:wdt]).then_inc(dv.h, 1)))

        for m in range(ND):
            rope(q_b, 2048 * m, 0, 2048,
                 v_pre=(lambda v, m=m, s0=q_sc - 32:
                        v.wait_ge(sc.h, s0 + 4 * (m + 1))))
        qrope_dv = dv.n
        tap("q", q_b, wait=(lambda q, w=qrope_dv: q.wait_ge(dv.h, w)))

        plan.mark("p1")

        prep_srcs = []
        for ci in range(NF):
            prep_srcs.append((
                wgate_h[128 * ci:128 * (ci + 1), :],
                wgdq.rearrange("p (hc i) -> p hc i", i=3072)
                [:, :, 128 * ci:128 * (ci + 1)]))
        for ci in range(NF):
            prep_srcs.append((
                wup_h[128 * ci:128 * (ci + 1), :],
                wudq.rearrange("p (hc i) -> p hc i", i=3072)
                [:, :, 128 * ci:128 * (ci + 1)]))
        for hc in range(ND):
            for th in range(3):
                prep_srcs.append((
                    wdown_h[128 * hc:128 * (hc + 1),
                            1024 * th:1024 * (th + 1)],
                    wddq.rearrange("p (j F) -> p j F", F=1024)
                    [:, 8 * th:8 * (th + 1), 128 * hc:128 * (hc + 1)]))
        prep_cur = [0]
        prep_deq = [dv.n]

        def emit_prep_upto(n):
            while prep_cur[0] < min(n, len(prep_srcs)):
                k = prep_cur[0]
                src, dst = prep_srcs[k]
                tl = spl.bump(16)
                G(lambda q, src=src, w=prep_deq[0]:
                  (q.wait_ge(dv.h, w),
                   q.dma_start(out=stg_f[1], in_=src).then_inc(spl.h, 16)))
                ndq = dv.bump()
                prep_deq[0] = ndq
                V(lambda v, tl=tl, tw=spt.n:
                  (v.wait_ge(spl.h, tl),
                   v.wait_ge(spt.h, tw),
                   v.tensor_reduce(
                       bm_s[:, 0:32],
                       stg_f[1].rearrange("p (b c) -> p b c", c=32),
                       AX, ALU.max, apply_absolute_value=True),
                   v.tensor_scalar(bm_s[:, 32:64], bm_s[:, 0:32], 1.0 / 7.0,
                                   1e-12, ALU.mult, ALU.max),
                   v.tensor_scalar(bm_s[:, 32:64], bm_s[:, 0:32], 1.0 / 7.0,
                                   1e-12, ALU.mult, ALU.max),
                   [v.memset(bm_s[:, 64:65], 0.0) for _p in range(4)][-1],
                   v.reciprocal(br_s[:, 0:32], bm_s[:, 32:64]),
                   v.reciprocal(br_s[:, 0:32], bm_s[:, 32:64]),
                   v.memset(bm_s[:, 64:65], 0.0),
                   v.memset(bm_s[:, 64:65], 0.0),
                   v.tensor_tensor(
                       stg1_i.rearrange("p (b c) -> p b c", c=32),
                       stg_f[1].rearrange("p (b c) -> p b c", c=32),
                       br_s[:, 0:32].unsqueeze(2).broadcast_to([128, 32, 32]),
                       ALU.mult),
                   v.scalar_tensor_tensor(
                       winA_b[:].rearrange("p (b c) -> p b c", c=32),
                       stg1_i.rearrange("p (b c) -> p b c", c=32),
                       0.0,
                       bm_s[:, 32:64].unsqueeze(2).broadcast_to([128, 32, 32]),
                       ALU.bypass, ALU.mult).then_inc(dv.h, 1)))
                ntr = spt.bump(16)
                S(lambda s, w=ndq, sw=sps.n:
                  (s.wait_ge(dv.h, w),
                   s.wait_ge(sps.h, sw),
                   s.dma_start_transpose(
                       out=winB_b[:].rearrange("p (j i) -> p j i", i=128),
                       in_=winA_b[:].rearrange("i (j h) -> i j h", h=128),
                   ).then_inc(spt.h, 16)))
                sps.bump(16)
                G(lambda q, dst=dst, w=ntr:
                  (q.wait_ge(spt.h, w),
                   q.dma_start(out=dst,
                               in_=winB_b[:].rearrange("p (j i) -> p j i",
                                                       i=128)
                               ).then_inc(sps.h, 16)))
                prep_cur[0] += 1

        dv.bump()
        V(lambda v:
          (v.memset(kf_z[0][64:128, :], 0.0),
           v.memset(kf_z[1][0:64, :], 0.0),
           v.memset(s_z[:].rearrange("p (hp c) -> p hp c", c=128)
                    [64:128, :, 0:64], 0.0),
           v.memset(s_z[:].rearrange("p (hp c) -> p hp c", c=128)
                    [0:64, :, 64:128], 0.0).then_inc(dv.h, 1)))
        og_sc_prev = [sc.n]
        for ci in range(NCH):
            tok0 = C * ci
            ksc0 = sc.n
            kq_dv = dv.n
            for bank in range(2):
                ps = (pA, pB)[bank]
                n = pe.bump()
                def kproj(tn, bank=bank, ps=ps, kq=kq_dv, tok0=tok0):
                    if bank == 0:
                        tn.wait_ge(dv.h, kq)
                    first = True
                    for mm in range(4):
                        m = 4 * bank + mm
                        for k in range(ND):
                            ins = tn.matmul(
                                ps[:, 128 * mm:128 * (mm + 1)],
                                wk_b[:, 1024 * k + 128 * m:
                                     1024 * k + 128 * (m + 1)],
                                xT_b[:, 2048 * k + tok0:2048 * k + tok0 + 128],
                                start=first,
                                stop=(mm == 3 and k == ND - 1))
                            first = False
                    ins.then_inc(pe.h, 1)
                TN(kproj)
                sc.bump()
                S(lambda s, bank=bank, w=pe.n, ps=ps:
                  (s.wait_ge(pe.h, w),
                   s.activation(kf_j[:, 512 * bank:512 * (bank + 1)],
                                ps[:, 0:512], AF.Silu).then_inc(sc.h, 1)))
            ksil_sc = sc.n
            if ci == 0:
                plan.mark("c0a")
            cw = cos_b[:, tok0:tok0 + 128].unsqueeze(1).broadcast_to(
                [128, ND, 128])
            sw = sin_b[:, tok0:tok0 + 128].unsqueeze(1).broadcast_to(
                [128, ND, 128])
            kf3 = kf_j[:].rearrange("p (j t) -> p j t", t=128)
            u3 = scr1_b[:, 0:1024].rearrange("p (j t) -> p j t", t=128)
            n1 = dv.bump()
            V(lambda v, w=ksil_sc, pa=pe.n, cw=cw, sw=sw, kf3=kf3, u3=u3:
              (v.wait_ge(sc.h, w),
               v.wait_ge(pe.h, pa),
               v.tensor_tensor(u3, kf3, sw, ALU.mult),
               v.tensor_tensor(kf3, kf3, cw, ALU.mult).then_inc(dv.h, 1)))
            if ci == 0:
                plan.mark("c0b")
            prev_add = [None]
            for g2 in range(2):
                pn = pe.bump()
                TN(lambda tn, g2=g2, n1=n1, ag=prev_add[0]:
                   (tn.wait_ge(dv.h, n1),
                    tn.wait_ge(dv.h, ag) if ag is not None else None,
                    [tn.matmul(pC[:, 128 * mm:128 * (mm + 1)], p2l_b,
                               scr1_b[:, 512 * g2 + 128 * mm:
                                      512 * g2 + 128 * (mm + 1)],
                               start=(mm == 0), stop=(mm == 3))
                     for mm in range(4)][-1].then_inc(pe.h, 1)))
                prev_add[0] = dv.bump()
                V(lambda v, g2=g2, w=pn:
                  (v.wait_ge(pe.h, w),
                   v.tensor_add(kf_j[:, 512 * g2:512 * (g2 + 1)],
                                kf_j[:, 512 * g2:512 * (g2 + 1)],
                                pC[:, 0:512]).then_inc(dv.h, 1)))
            krope_dv = dv.n
            kz_dv = dv.bump()
            V(lambda v:
              (v.tensor_copy(kf_z[0][0:64, :], kf_j[0:64, :]),
               v.tensor_copy(kf_z[1][64:128, :], kf_j[64:128, :]
                             ).then_inc(dv.h, 1)))
            if ci == 0:
                plan.mark("c0c")
            kt_tgt = skt.bump(16)
            Y(lambda q, w=krope_dv, pa=pe.n:
              (q.wait_ge(dv.h, w),
               q.wait_ge(pe.h, pa),
               q.dma_start_transpose(
                   out=kt_j[:].rearrange("t (j F) -> t j F", F=128),
                   in_=kf_j[:].rearrange("f (j t) -> f j t", t=128),
               ).then_inc(skt.h, 16)))
            if ci == 0:
                plan.mark("c0d")
            vsc0 = sc.n
            for hh in range(2):
                ps = (pA, pB)[hh]
                mm_group(
                    ps[:, 0:512],
                    [(xT_b[:, 2048 * k + tok0:2048 * k + tok0 + 128],
                      wv_b[:, 1024 * k + 512 * hh:1024 * k + 512 * (hh + 1)])
                     for k in range(ND)],
                    pre=(lambda tn, hh=hh, kq=krope_dv:
                         tn.wait_ge(dv.h, kq) if hh == 0 else None))
                sc.bump()
                S(lambda s, hh=hh, w=pe.n, ps=ps:
                  (s.wait_ge(pe.h, w),
                   s.copy(v_j[:, 512 * hh:512 * (hh + 1)],
                          ps[:, 0:512]).then_inc(sc.h, 1)))
            if ci == 0:
                plan.mark("c0e")
            v_sc = sc.n

            for gidx in range(4):
                psb = (pE1, pE2)[gidx % 2]
                dv_guard = dv.n
                n = pe.bump()
                def ef(tn, gidx=gidx, psb=psb, ci=ci,
                       ktw=kt_tgt, vsc=v_sc, dg=dv_guard, kz=kz_dv):
                    tn.wait_ge(sc.h, vsc)
                    tn.wait_ge(skt.h, ktw)
                    tn.wait_ge(dv.h, kz)
                    if gidx >= 2:
                        tn.wait_ge(dv.h, dg)
                    for hh in range(4):
                        h = 4 * gidx + hh
                        m, par = divmod(h, 2)
                        ins = tn.matmul(
                            psb[:, 128 * hh:128 * (hh + 1)],
                            kf_z[par][:, 128 * m:128 * (m + 1)],
                            q_b[:, 2048 * m + 128 * ci:
                                2048 * m + 128 * (ci + 1)],
                            start=(hh == 0), stop=(hh == 3))
                    ins.then_inc(pe.h, 1)
                TN(ef)
                dv.bump()
                V(lambda v, gidx=gidx, psb=psb, n=n:
                  (v.wait_ge(pe.h, n),
                   v.tensor_tensor(
                       e_sb[:, 512 * gidx:512 * (gidx + 1)].rearrange(
                           "p (j f) -> p j f", f=128),
                       psb[:, 0:512].rearrange("p (j f) -> p j f", f=128),
                       maskT_b.unsqueeze(1).broadcast_to([128, 4, 128]),
                       ALU.mult).then_inc(dv.h, 1)))
            if ci == 0:
                plan.mark("c0f")
            eev_dv = dv.n
            av_pe = pe.bump()
            def avf(tn, ci=ci, ev=eev_dv, ogp=og_sc_prev[0]):
                tn.wait_ge(dv.h, ev)
                tn.wait_ge(sc.h, ogp)
                seen = set()
                plan_mm = []
                for h in range(NH):
                    hp, par = divmod(h, 2)
                    plan_mm.append(("av", h, hp, par, (hp // 4, par)))
                if ci > 0 and not globals().get('_NO_INTER'):
                    for h in range(NH):
                        hp, par = divmod(h, 2)
                        plan_mm.append(("in", h, hp, par, (hp // 4, par)))
                lastidx = {}
                for idx, (_, _, _, _, grp) in enumerate(plan_mm):
                    lastidx[grp] = idx
                for idx, (kind, h, hp, par, grp) in enumerate(plan_mm):
                    first = grp not in seen
                    seen.add(grp)
                    stop = (lastidx[grp] == idx)
                    if kind == "av":
                        ins = tn.matmul(
                            pO[64 * par:64 * (par + 1), 128 * hp:128 * (hp + 1)],
                            v_j[:, 64 * h:64 * (h + 1)],
                            e_sb[:, 128 * h:128 * (h + 1)],
                            start=first, stop=stop,
                            skip_group_check=(par == 1),
                            tile_position=(0, 64 * par))
                    else:
                        ins = tn.matmul(
                            pO[64 * par:64 * (par + 1), 128 * hp:128 * (hp + 1)],
                            s_z[:, 64 * h:64 * (h + 1)],
                            q_b[:, 2048 * hp + 128 * ci:
                                2048 * hp + 128 * (ci + 1)],
                            start=first, stop=stop,
                            skip_group_check=(par == 1),
                            tile_position=(0, 64 * par))
                ins.then_inc(pe.h, 1)
            TN(avf)
            if ci == 0:
                plan.mark("c0g")
            st_pe = pe.bump()
            def stf(tn, ci=ci, sd=dv.n):
                tn.wait_ge(dv.h, sd)
                for h in range(NH):
                    hp, par = divmod(h, 2)
                    ins = tn.matmul(
                        pS[64 * par:64 * (par + 1), 64 * hp:64 * (hp + 1)],
                        kt_j[:, 64 * h:64 * (h + 1)],
                        v_j[:, 64 * h:64 * (h + 1)],
                        start=(h < 2), stop=(h >= NH - 2),
                        skip_group_check=(par == 1),
                        tile_position=(0, 64 * par))
                ins.then_inc(pe.h, 1)
            TN(stf)
            dv.bump()
            if ci == 0:
                V(lambda v, w=st_pe:
                  (v.wait_ge(pe.h, w),
                   v.tensor_copy(s_fin, pS[:, 0:512]),
                   v.tensor_copy(
                       s_z[:].rearrange("p (hp c) -> p hp c", c=128)
                       [0:64, :, 0:64],
                       s_fin[0:64, :].rearrange("p (hp c) -> p hp c", c=64)),
                   v.tensor_copy(
                       s_z[:].rearrange("p (hp c) -> p hp c", c=128)
                       [64:128, :, 64:128],
                       s_fin[64:128, :].rearrange("p (hp c) -> p hp c", c=64)
                   ).then_inc(dv.h, 1)))
            else:
                V(lambda v, w=st_pe:
                  (v.wait_ge(pe.h, w),
                   v.tensor_add(s_fin, s_fin, pS[:, 0:512]),
                   v.tensor_copy(
                       s_z[:].rearrange("p (hp c) -> p hp c", c=128)
                       [0:64, :, 0:64],
                       s_fin[0:64, :].rearrange("p (hp c) -> p hp c", c=64)),
                   v.tensor_copy(
                       s_z[:].rearrange("p (hp c) -> p hp c", c=128)
                       [64:128, :, 64:128],
                       s_fin[64:128, :].rearrange("p (hp c) -> p hp c", c=64)
                   ).then_inc(dv.h, 1)))
            for gg in range(2):
                sc.bump()
                S(lambda s, gg=gg, ci=ci, w=av_pe:
                  (s.wait_ge(pe.h, w),
                   s.copy(og_b[:].rearrange("p (hp t) -> p hp t", t=2048
                                            )[:, 4 * gg:4 * (gg + 1),
                                              128 * ci:128 * (ci + 1)],
                          pO[:, 512 * gg:512 * (gg + 1)].rearrange(
                              "p (j f) -> p j f", f=128)).then_inc(sc.h, 1)))
            og_sc_prev[0] = sc.n
            emit_prep_upto((len(prep_srcs) * (ci + 1)) // NCH)
            plan.mark(f"p3c{ci}")
        loop_sc = sc.n
        loop_pe = pe.n
        tap("og0", og_b, wait=(lambda q, w=loop_sc: q.wait_ge(sc.h, w)))
        tap("S", s_fin, wait=(lambda q, w=dv.n: q.wait_ge(dv.h, w)))

        plan.mark("p3")
        n1 = dv.bump()
        V(lambda v, w=loop_pe, dvp=dv.n - 1:
          (v.tensor_scalar(scrf_f[:, 0:512], s_fin, eo_s[:, 0:1], None,
                           ALU.mult).then_inc(dv.h, 1)))
        ldg.bump(16)
        G(lambda q, w=n1, t=ldg.n:
          (q.wait_ge(dv.h, w),
           q.dma_start(out=st_in, in_=scrf_f[:, 0:512]).then_inc(ldg.h, 16)))
        cc.bump()
        G(lambda q, w=ldg.n, t=cc.n:
          (q.wait_ge(ldg.h, w),
           q.collective_compute(
               "AllReduce", ALU.add,
               replica_groups=[[0, 1], [2, 3], [4, 5], [6, 7]],
               ins=[st_in.opt()], outs=[st_out.opt()]).then_inc(cc.h, 1)))
        ldg.bump(16)
        G(lambda q, w=cc.n, t=ldg.n:
          (q.wait_ge(cc.h, w),
           q.dma_start(out=scrf_f[:, 0:512], in_=st_out).then_inc(ldg.h, 16)))
        sini_dv = dv.bump()
        V(lambda v, w=ldg.n:
          (v.wait_ge(ldg.h, w),
           v.tensor_scalar(
               s_iz[:].rearrange("p (hp c) -> p hp c", c=128)[0:64, :, 0:64],
               scrf_f[0:64, 0:512].rearrange("p (hp c) -> p hp c", c=64),
               eo_s[0:64, 1:2], None, ALU.mult),
           v.tensor_scalar(
               s_iz[:].rearrange("p (hp c) -> p hp c", c=128)
               [64:128, :, 64:128],
               scrf_f[64:128, 0:512].rearrange("p (hp c) -> p hp c", c=64),
               eo_s[64:128, 1:2], None, ALU.mult).then_inc(dv.h, 1)))

        plan.mark("p4")
        first_pref = [True]
        for ci in range(NCH):
            for gg in range(2):
                psb = (pE1, pE2)[gg]
                dv_guard = dv.n
                n = pe.bump()
                def pf(tn, ci=ci, gg=gg, psb=psb, dg=dv_guard, fp=first_pref[0]):
                    tn.wait_ge(dv.h, sini_dv)
                    if fp:
                        tn.wait_ge(sc.h, loop_sc)
                    else:
                        tn.wait_ge(dv.h, dg)
                    for hh in range(4):
                        hp = 4 * gg + hh
                        for par in range(2):
                            h = 2 * hp + par
                            ins = tn.matmul(
                                psb[64 * par:64 * (par + 1),
                                    128 * hh:128 * (hh + 1)],
                                s_iz[:, 64 * h:64 * (h + 1)],
                                q_b[:, 2048 * hp + 128 * ci:
                                    2048 * hp + 128 * (ci + 1)],
                                start=(hh == 0), stop=(hh == 3),
                                skip_group_check=(par == 1),
                                tile_position=(0, 64 * par))
                    ins.then_inc(pe.h, 1)
                TN(pf)
                first_pref[0] = False
                dv.bump()
                V(lambda v, ci=ci, gg=gg, psb=psb, n=n:
                  (v.wait_ge(pe.h, n),
                   v.tensor_add(
                       og_b[:].rearrange("p (hp t) -> p hp t", t=2048
                                         )[:, 4 * gg:4 * (gg + 1),
                                           128 * ci:128 * (ci + 1)],
                       og_b[:].rearrange("p (hp t) -> p hp t", t=2048
                                         )[:, 4 * gg:4 * (gg + 1),
                                           128 * ci:128 * (ci + 1)],
                       psb[:, 0:512].rearrange("p (j f) -> p j f", f=128)
                   ).then_inc(dv.h, 1)))
        prefix_dv = dv.n
        tap("og1", og_b, wait=(lambda q, w=prefix_dv: q.wait_ge(dv.h, w)))
        og1_tap_n = dbg_sem[0].n if _TAP == "og1" else None
        tap("sini", s_iz, wait=(lambda q, w=sini_dv: q.wait_ge(dv.h, w)))

        plan.mark("p5")
        gate_dv_prev = [prefix_dv]
        for m in range(ND):
            gsc0 = sc.n
            gpes = []
            for tt in range(4):
                ps = rot3[tt % 3]
                gpes.append(pe.bump())
                TN(lambda tn, m=m, tt=tt, ps=ps, w=prefix_dv, g0=gsc0,
                   w2=wg2_dv,
                   scw={0: gsc0, 1: gsc0 - 2, 2: gsc0 - 1,
                        3: gsc0 + 1}[tt]:
                   (tn.wait_ge(dv.h, max(w, w2)),
                    tn.wait_ge(sc.h, max(0, scw)),
                    tn.matmul(ps[:, 0:512],
                              wg2_b[0:64, 128 * m:128 * (m + 1)],
                              g1_b[0:64, 512 * tt:512 * (tt + 1)],
                              start=True, stop=True, tile_position=(0, 0)
                              ).then_inc(pe.h, 1)))
            for tt in range(4):
                sc.bump()
                S(lambda s, m=m, tt=tt, w=gpes[tt], gd=gate_dv_prev[0]:
                  (s.wait_ge(pe.h, w),
                   s.wait_ge(dv.h, gd) if tt == 0 else None,
                   s.activation(scr1_b[:, 512 * tt:512 * (tt + 1)],
                                rot3[tt % 3][:, 0:512], AF.Sigmoid
                                ).then_inc(sc.h, 1)))
            gate_dv_prev[0] = dv.bump()
            V(lambda v, m=m, w=sc.n, tn_=og1_tap_n if m == 0 else None:
              (v.wait_ge(sc.h, w),
               v.wait_ge(dbg_sem[0].h, tn_) if tn_ is not None else None,
               v.tensor_mul(og_b[:, 2048 * m:2048 * (m + 1)],
                            og_b[:, 2048 * m:2048 * (m + 1)],
                            scr1_b).then_inc(dv.h, 1)))
        gated_dv = dv.n
        tap("og2", og_b, wait=(lambda q, w=gated_dv: q.wait_ge(dv.h, w)))
        og2_tap_n = dbg_sem[0].n if _TAP == "og2" else None

        plan.mark("p6")
        emit_prep_upto(len(prep_srcs))
        dv.bump()
        V(lambda v, a=spt.n, b=sps.n:
          (v.wait_ge(spt.h, a), v.wait_ge(sps.h, b),
           v.memset(bm_s[:, 64:65], 0.0).then_inc(dv.h, 1)))
        prev_stats_dv = [gated_dv]
        prev_nsq = [0]
        for tt in range(4):
            n_sum = pe.bump()
            TN(lambda tn, tt=tt, w=gated_dv, pv=prev_stats_dv[0]:
               (tn.wait_ge(dv.h, max(w, pv)),
                [tn.matmul(pE1[0:1, 0:512], onesc_b,
                           og_b[:, 2048 * m + 512 * tt:2048 * m + 512 * (tt + 1)],
                           start=(m == 0), stop=(m == ND - 1))
                 for m in range(ND)][-1].then_inc(pe.h, 1)))
            sq_scs = []
            for m in range(ND):
                sq_scs.append(sc.bump())
                S(lambda s, m=m, tt=tt, w=gated_dv,
                  pq=prev_nsq[0], dst=(scr1_b, e_sb)[m // 4]:
                  (s.wait_ge(dv.h, w),
                   s.wait_ge(pe.h, pq) if pq else None,
                   s.activation(dst[:, 512 * (m % 4):512 * (m % 4 + 1)],
                                og_b[:, 2048 * m + 512 * tt:
                                     2048 * m + 512 * (tt + 1)],
                                AF.Square).then_inc(sc.h, 1)))
            n_sq = pe.bump()
            TN(lambda tn, tt=tt, sqs=list(sq_scs), pv=prev_stats_dv[0]:
               (tn.wait_ge(dv.h, pv),
                [(tn.wait_ge(sc.h, sqs[m]),
                  tn.matmul(pE2[0:1, 0:512], onesc_b,
                            (scr1_b, e_sb)[m // 4][:,
                                512 * (m % 4):512 * (m % 4 + 1)],
                            start=(m == 0), stop=(m == ND - 1)))
                 for m in range(ND)][-1][1].then_inc(pe.h, 1)))
            prev_nsq[0] = n_sq
            nst = dv.bump()
            V(lambda v, w2=n_sq:
              (v.wait_ge(pe.h, w2),
               v.tensor_scalar(mu_row[0:1, 0:512], pE1[0:1, 0:512],
                               1.0 / D, None, ALU.mult),
               v.tensor_scalar(rr_row[0:1, 0:512], pE2[0:1, 0:512],
                               1.0 / D, None, ALU.mult),
               v.tensor_mul(scrf_f[0:1, 0:512], mu_row[0:1, 0:512],
                            mu_row[0:1, 0:512]),
               v.tensor_sub(rr_row[0:1, 0:512], rr_row[0:1, 0:512],
                            scrf_f[0:1, 0:512]).then_inc(dv.h, 1)))
            prev_stats_dv[0] = nst
            nsd = sc.bump()
            S(lambda s, w=nst:
              (s.wait_ge(dv.h, w),
               s.activation(rr_row[0:1, 0:512], rr_row[0:1, 0:512],
                            AF.Sqrt, bias=eps_s[0:1, 0:1]).then_inc(sc.h, 1)))
            nrc = dv.bump()
            V(lambda v, w=nsd:
              (v.wait_ge(sc.h, w),
               v.reciprocal(rr_row[0:1, 0:512], rr_row[0:1, 0:512]
                            ).then_inc(dv.h, 1)))
            nbc = pe.bump()
            TN(lambda tn, w=nrc:
               (tn.wait_ge(dv.h, w),
                tn.matmul(pA[:, 0:512], onesr_f, mu_row[0:1, 0:512],
                          start=True, stop=True),
                tn.matmul(pB[:, 0:512], onesr_f, rr_row[0:1, 0:512],
                          start=True, stop=True).then_inc(pe.h, 1)))
            for m in range(ND):
                nap = dv.bump()
                V(lambda v, m=m, tt=tt, w=nbc,
                  tn_=og2_tap_n if (m == 0 and tt == 0) else None:
                  (v.wait_ge(pe.h, w),
                   v.wait_ge(dbg_sem[0].h, tn_) if tn_ is not None else None,
                   v.scalar_tensor_tensor(
                       scrf_f[:, 0:512], pA[:, 0:512], -1.0,
                       og_b[:, 2048 * m + 512 * tt:2048 * m + 512 * (tt + 1)],
                       ALU.mult, ALU.add),
                   v.tensor_mul(scrf_f[:, 512:1024], scrf_f[:, 0:512],
                                pB[:, 0:512]),
                   v.tensor_scalar(og_b[:, 2048 * m + 512 * tt:
                                        2048 * m + 512 * (tt + 1)],
                                   scrf_f[:, 512:1024],
                                   lnw_s[:, m:m + 1], lnb_s[:, m:m + 1],
                                   ALU.mult, ALU.add).then_inc(dv.h, 1)))
            prev_stats_dv[0] = dv.n
        y_dv = dv.n
        tap("y", og_b, wait=(lambda q, w=y_dv: q.wait_ge(dv.h, w)))

        plan.mark("p7")
        for tt in range(NT):
            sl = sld[tt % 2]
            sp = ssp[tt % 2]
            sp_guard = sp.n
            x_tgt = sl.bump(16)
            Y(lambda q, tt=tt, yd=y_dv, spg=sp_guard, sl=sl, sp=sp,
              nv=dv.n:
              (q.wait_ge(dv.h, yd) if tt == 0 else None,
               q.wait_ge(dv.h, nv) if tt == 1 else None,
               q.wait_ge(sp.h, spg) if tt >= 2 else None,
               q.dma_start(out=stg_f[tt % 2],
                           in_=x_h[128 * tt:128 * (tt + 1), :]
                           ).then_inc(sl.h, 16)))
            wo_pes = []
            for hh in range(2):
                ps = (pA, pB)[hh]
                wo_pes.append(pe.bump())
                TN(lambda tn, tt=tt, hh=hh, ps=ps, yd=y_dv, dvw=dv.n:
                   (tn.wait_ge(dv.h, yd),
                    tn.wait_ge(dv.h, dvw) if tt > 0 else None,
                    [tn.matmul(ps[:, 0:512],
                               og_b[:, 2048 * m + 128 * tt:
                                    2048 * m + 128 * (tt + 1)],
                               wo_b[:, 1024 * m + 512 * hh:
                                    1024 * m + 512 * (hh + 1)],
                               start=(m == 0), stop=(m == ND - 1))
                     for m in range(ND)][-1].then_inc(pe.h, 1)))
            nev = dv.bump()
            V(lambda v, tt=tt, w=wo_pes, xt=x_tgt, sl=sl:
              (v.wait_ge(pe.h, w[1]),
               v.wait_ge(sl.h, xt),
               v.scalar_tensor_tensor(scrf_f[:, 0:512], pA[:, 0:512], 0.0,
                                      stg_f[tt % 2][:, 0:512],
                                      ALU.bypass, ALU.add),
               v.scalar_tensor_tensor(scrf_f[:, 512:1024], pB[:, 0:512], 0.0,
                                      stg_f[tt % 2][:, 512:1024],
                                      ALU.bypass, ALU.add),
               v.reduce_sum(st1_s[:, 0:1], scrf_f[:, 0:1024], AX),
               v.tensor_mul(scr1_f[:, 0:1024], scrf_f[:, 0:1024],
                            scrf_f[:, 0:1024]),
               v.reduce_sum(st1_s[:, 1:2], scr1_f[:, 0:1024], AX),
               [(v.tensor_scalar(st1_s[:, 2:3], st1_s[:, 0:1], 1.0 / D,
                                 None, ALU.mult),
                 v.tensor_scalar(st1_s[:, 3:4], st1_s[:, 1:2], 1.0 / D,
                                 None, ALU.mult),
                 v.tensor_mul(st1_s[:, 4:5], st1_s[:, 2:3], st1_s[:, 2:3]),
                 v.tensor_sub(st1_s[:, 7:8], st1_s[:, 3:4], st1_s[:, 4:5]))
                for _r in range(2)][-1][-1].then_inc(dv.h, 1)))
            if tt == 0 and _TAP in ("p8s", "p8x"):
                ds = dbg_sem[0]
                ds.bump(16)
                Y(lambda q, w=nev, ds=ds:
                  (q.wait_ge(dv.h, w),
                   q.dma_start(out=dbg_h[:, 0:1024].bitcast(F32),
                               in_=(scrf_f[:, 0:1024] if _TAP == "p8s"
                                    else stg_f[0][:, 0:1024])
                               ).then_inc(ds.h, 16)))
            nsd = sc.bump()
            S(lambda s, w=nev:
              (s.wait_ge(dv.h, w),
               s.activation(st1_s[:, 5:6], st1_s[:, 7:8], AF.Sqrt,
                            bias=eps_s).then_inc(sc.h, 1)))
            nap = dv.bump()
            V(lambda v, tt=tt, w=nsd, dbt=(tt == 0 and _TAP in ("p8s", "p8x")):
              (v.wait_ge(sc.h, w),
               v.wait_ge(dbg_sem[0].h, dbg_sem[0].n) if dbt else None,
               [v.memset(scr1_f[:, 0:1], 0.0) for _p in range(4)][-1],
               v.reciprocal(st1_s[:, 6:7], st1_s[:, 5:6]),
               v.reciprocal(st1_s[:, 6:7], st1_s[:, 5:6]),
               v.memset(scr1_f[:, 0:1], 0.0),
               v.memset(scr1_f[:, 0:1], 0.0),
               v.tensor_scalar(stg_f[tt % 2][:, 0:1024], scrf_f[:, 0:1024],
                               st1_s[:, 2:3], st1_s[:, 6:7],
                               ALU.subtract, ALU.mult),
               v.tensor_reduce(mx_acc[:, tt:tt + 1], stg_f[tt % 2][:, 0:1024],
                               AX, ALU.max, apply_absolute_value=True
                               ).then_inc(dv.h, 1)))
            sp.bump(16)
            S(lambda s, tt=tt, w=nap, sp=sp:
              (s.wait_ge(dv.h, w),
               s.dma_start(out=x1_spill[128 * tt:128 * (tt + 1), :],
                           in_=stg_f[tt % 2]).then_inc(sp.h, 16)))
        x1_dv = dv.n
        x1_sp = (ssp[0].n, ssp[1].n)
        if _TAP == "x1f":
            ds = dbg_sem[0]
            ds.bump(16)
            Y(lambda q, a=ssp[0].n, b=ssp[1].n, ds=ds:
              (q.wait_ge(ssp[0].h, a), q.wait_ge(ssp[1].h, b),
               q.dma_start(
                   out=dbg_h.rearrange("p (a d) -> a p d", d=1024),
                   in_=x1_spill.rearrange("(a p) d -> a p d", p=128),
               ).then_inc(ds.h, 16)))
        tap("x1", None, wait=(lambda q, a=ssp[0].n, b=ssp[1].n:
                              (q.wait_ge(ssp[0].h, a), q.wait_ge(ssp[1].h, b))),
            dram_src=x1_spill[0:128, :])

        plan.mark("p8")
        def barrier(bidx, srcs, lane_base):
            nred = None
            for i, (acc, K) in enumerate(srcs):
                n1b = dv.bump()
                V(lambda v, acc=acc, K=K:
                  (v.tensor_reduce(acc[:, 31:32], acc[:, 0:K], AX, ALU.max,
                                   apply_absolute_value=True),
                   v.tensor_reduce(acc[:, 31:32], acc[:, 0:K], AX, ALU.max,
                                   apply_absolute_value=True
                                   ).then_inc(dv.h, 1)))
                ldg.bump(16)
                G(lambda q, acc=acc, w=n1b, t=ldg.n - 16:
                  (q.wait_ge(dv.h, w),
                   q.wait_ge(ldg.h, t),
                   q.dma_start(out=row_s, in_=acc[:, 31:32]).then_inc(ldg.h, 16)))
                nred = dv.bump()
                V(lambda v, i=i, w=ldg.n:
                  (v.wait_ge(ldg.h, w),
                   v.tensor_reduce(sc_s[0:1, i:i + 1], row_s, AX, ALU.max
                                   ).then_inc(dv.h, 1)))
            ldg.bump(16)
            G(lambda q, b=bidx, w=nred:
              (q.wait_ge(dv.h, w),
               q.dma_start(out=bar_in[b], in_=sc_s[0:1, 0:8]).then_inc(ldg.h, 16)))
            cc.bump()
            G(lambda q, b=bidx, w=ldg.n:
              (q.wait_ge(ldg.h, w),
               q.collective_compute(
                   "AllReduce", ALU.max, replica_groups=[list(range(8))],
                   ins=[bar_in[b].opt()], outs=[bar_out[b].opt()]
               ).then_inc(cc.h, 1)))
            ldg.bump(16)
            G(lambda q, b=bidx, w=cc.n:
              (q.wait_ge(cc.h, w),
               q.dma_start(out=res_s, in_=bar_out[b]).then_inc(ldg.h, 16)))
            nsc_ = dv.bump()
            V(lambda v, ns=len(srcs), w=ldg.n:
              (v.wait_ge(ldg.h, w),
               [v.tensor_scalar(tmp1_s[0:1, 2 * i:2 * i + 1],
                                res_s[0:1, i:i + 1], 1.0 / QMAX, 1e-12,
                                ALU.mult, ALU.max) for i in range(ns)],
               [v.memset(tmp1_s[0:1, 7:8], 0.0) for _p in range(4)],
               [(v.reciprocal(tmp1_s[0:1, 2 * i + 1:2 * i + 2],
                              tmp1_s[0:1, 2 * i:2 * i + 1]),
                 v.reciprocal(tmp1_s[0:1, 2 * i + 1:2 * i + 2],
                              tmp1_s[0:1, 2 * i:2 * i + 1]))
                for i in range(ns)][-1][1].then_inc(dv.h, 1)))
            nbc = pe.bump()
            TN(lambda tn, ns=len(srcs), w=nsc_:
               (tn.wait_ge(dv.h, w),
                tn.matmul(pC[:, 0:2 * ns], onesr_f, tmp1_s[0:1, 0:2 * ns],
                          start=True, stop=True).then_inc(pe.h, 1)))
            nfin = dv.bump()
            V(lambda v, ns=len(srcs), lb=lane_base, w=nbc:
              (v.wait_ge(pe.h, w),
               v.tensor_copy(scl_bc[:, lb:lb + 2 * ns], pC[:, 0:2 * ns]
                             ).then_inc(dv.h, 1)))
            return nfin

        bar1_dv = barrier(0, [(mx_acc, NT)], 0)
        if _TAP == "bar1d":
            ds = dbg_sem[0]
            ds.bump(96)
            Y(lambda q, w=bar1_dv, ds=ds:
              (q.wait_ge(dv.h, w),
               q.dma_start(out=dbg_h[:, 0:32].bitcast(F32),
                           in_=mx_acc[:, 0:32]).then_inc(ds.h, 16),
               q.dma_start(out=dbg_h[0:1, 32:160].bitcast(F32),
                           in_=row_s).then_inc(ds.h, 16),
               q.dma_start(out=dbg_h[0:1, 160:168].bitcast(F32),
                           in_=sc_s).then_inc(ds.h, 16),
               q.dma_start(out=dbg_h[0:1, 192:200].bitcast(F32),
                           in_=res_s).then_inc(ds.h, 16),
               q.dma_start(out=dbg_h[0:1, 224:232].bitcast(F32),
                           in_=tmp1_s).then_inc(ds.h, 16),
               q.dma_start(out=dbg_h[:, 256:268].bitcast(F32),
                           in_=scl_bc[:, 0:12]).then_inc(ds.h, 16)))

        plan.mark("bar1")
        wm_pe = pe.n
        swg.bump(16)
        Y(lambda q, w=wm_pe, s=sps.n:
          (q.wait_ge(pe.h, w), q.wait_ge(sps.h, s),
           q.dma_start(out=wgT_b[:], in_=wgdq).then_inc(swg.h, 16)))
        swu.bump(16)
        Y(lambda q:
          q.dma_start(out=wuT_b[:], in_=wudq).then_inc(swu.h, 16))
        wgT_tr = swg.n
        wuT_tr = swu.n

        plan.mark("wprep")
        p9_dv0 = dv.n
        for tt in range(NT):
            sx = sxq[tt % 2]
            sq_ = stq[tt % 2]
            tr_guard = sq_.n
            x_tgt = sx.bump(16)
            Y(lambda q, tt=tt, d0=p9_dv0, spx=x1_sp, sx=sx,
              pv=(dv.n - 1 if tt >= 2 else None):
              ((q.wait_ge(ssp[0].h, spx[0]), q.wait_ge(ssp[1].h, spx[1]))
               if tt == 0 else None,
               q.wait_ge(dv.h, d0) if tt < 2 else q.wait_ge(dv.h, pv),
               q.dma_start(out=stg_f[tt % 2],
                           in_=x1_spill[128 * tt:128 * (tt + 1), :]
                           ).then_inc(sx.h, 16)))
            nq = dv.bump()
            V(lambda v, tt=tt, xt=x_tgt, b1=bar1_dv, sx=sx, sq_=sq_,
              tg=tr_guard:
              (v.wait_ge(sx.h, xt),
               v.wait_ge(dv.h, b1) if tt == 0 else None,
               v.wait_ge(sq_.h, tg) if tt >= 2 else None,
               v.tensor_scalar(scrf_i[:, 0:1024], stg_f[tt % 2],
                               scl_bc[:, 1:2], None, ALU.mult),
               v.tensor_scalar(xqt_st[tt % 2], scrf_i[:, 0:1024],
                               scl_bc[:, 0:1], None, ALU.mult).then_inc(dv.h, 1)))
            sq_.bump(16)
            Y(lambda q, tt=tt, w=nq, lp=loop_pe, sq_=sq_:
              (q.wait_ge(dv.h, w),
               q.dma_start_transpose(
                   out=xqT_b[:].rearrange("p (j t) -> p j t", t=2048
                                          )[:, :, 128 * tt:128 * (tt + 1)],
                   in_=xqt_st[tt % 2].rearrange("t (j h) -> t j h", h=128),
               ).then_inc(sq_.h, 16)))
        xq_tr = (stq[0].n, stq[1].n)
        tap("xqT", xqT_b, wait=(lambda q, a=stq[0].n, b=stq[1].n:
                                (q.wait_ge(stq[0].h, a), q.wait_ge(stq[1].h, b))))
        tap("sclx", scl_bc, wait=(lambda q, w=bar1_dv: q.wait_ge(dv.h, w)))

        plan.mark("p9")
        def ffn_mm(wT_b, spill, acc, tsem, wtr):
            evac_base = sc.n
            sp_base = (ssp[0].n, ssp[1].n)
            for m in range(NF):
                stgc = (scr1_b, e_sb)[m % 2]
                sp = ssp[m % 2]
                mpes = []
                for tt in range(4):
                    i = 4 * m + tt
                    ps = rot3[i % 3]
                    mpes.append(pe.bump())
                    TN(lambda tn, m=m, tt=tt, i=i, ps=ps, wT=wT_b,
                       tsem=tsem, wtr=wtr:
                       (tn.wait_ge(tsem.h, wtr),
                        (tn.wait_ge(stq[0].h, xq_tr[0]),
                         tn.wait_ge(stq[1].h, xq_tr[1])) if i == 0 else None,
                        tn.wait_ge(sc.h, evac_base + i - 2) if i >= 3 else None,
                        [tn.matmul(ps[:, 0:512],
                                   wT[:, 3072 * k + 128 * m:
                                      3072 * k + 128 * (m + 1)],
                                   xqT_b[:, 2048 * k + 512 * tt:
                                         2048 * k + 512 * (tt + 1)],
                                   start=(k == 0), stop=(k == ND - 1))
                         for k in range(ND)][-1].then_inc(pe.h, 1)))
                sp_guard = sp.n
                for tt in range(4):
                    i = 4 * m + tt
                    sc.bump()
                    S(lambda s, m=m, tt=tt, i=i, w=mpes[tt], stgc=stgc,
                      sp=sp, spg=sp_guard:
                      (s.wait_ge(pe.h, w),
                       s.wait_ge(sp.h, spg) if (tt == 0 and m >= 2) else None,
                       s.copy(stgc[:, 512 * tt:512 * (tt + 1)],
                              rot3[i % 3][:, 0:512]).then_inc(sc.h, 1)))
                nmx = dv.bump()
                V(lambda v, m=m, w=sc.n, stgc=stgc, acc=acc:
                  (v.wait_ge(sc.h, w),
                   v.tensor_reduce(acc[:, m % 24:m % 24 + 1], stgc, AX,
                                   ALU.max, apply_absolute_value=True
                                   ).then_inc(dv.h, 1)))
                sp.bump(16)
                S(lambda s, m=m, w=nmx, stgc=stgc, spd=spill, sp=sp:
                  (s.wait_ge(dv.h, w),
                   s.dma_start(out=spd[128 * m:128 * (m + 1), :], in_=stgc
                               ).then_inc(sp.h, 16)))

        ffn_mm(wgT_b, g_spill, gm_acc, swg, wgT_tr)
        ffn_mm(wuT_b, u_spill, um_acc, swu, wuT_tr)
        fmm_sp = (ssp[0].n, ssp[1].n)
        tap("graw", None, wait=(lambda q, a=ssp[0].n, b=ssp[1].n:
                                (q.wait_ge(ssp[0].h, a), q.wait_ge(ssp[1].h, b))),
            dram_src=g_spill[0:128, :])
        tap("uraw", None, wait=(lambda q, a=ssp[0].n, b=ssp[1].n:
                                (q.wait_ge(ssp[0].h, a), q.wait_ge(ssp[1].h, b))),
            dram_src=u_spill[0:128, :])
        tap("wgT", wgT_b, wait=(lambda q, w=wgT_tr: q.wait_ge(swg.h, w)))

        bar2_dv = barrier(1, [(gm_acc, 24), (um_acc, 24)], 2)

        plan.mark("f23")
        wd_pe = pe.n
        swd.bump(16)
        Y(lambda q, w=wd_pe, s=sps.n:
          (q.wait_ge(pe.h, w), q.wait_ge(sps.h, s),
           q.dma_start(out=wgT_b[:], in_=wddq).then_inc(swd.h, 16)))
        wdT_tr = swd.n

        plan.mark("wdprep")
        for m in range(NF):
            stgc = (scr1_b, e_sb)[m % 2]
            sr = srb[m % 2]
            sp = ssp[m % 2]
            wprev = dv.n
            tgt = sr.bump(16)
            Y(lambda q, m=m, b2=bar2_dv, w=wprev, stgc=stgc, sr=sr:
              (q.wait_ge(dv.h, b2) if m < 2 else q.wait_ge(dv.h, w - 1),
               q.dma_start(out=stgc,
                           in_=g_spill[128 * m:128 * (m + 1), :]
                           ).then_inc(sr.h, 16)))
            n1 = dv.bump()
            V(lambda v, m=m, tgt=tgt, stgc=stgc, sr=sr, wd=wdT_tr:
              (v.wait_ge(sr.h, tgt),
               v.wait_ge(swd.h, wd) if m == 0 else None,
               [(v.tensor_scalar(scrf_i[:, 0:1024],
                                 stgc[:, 1024 * hf:1024 * (hf + 1)],
                                 scl_bc[:, 3:4], None, ALU.mult),
                 v.tensor_scalar(gq_c[:, 1024 * hf:1024 * (hf + 1)],
                                 scrf_i[:, 0:1024], scl_bc[:, 2:3],
                                 None, ALU.mult)) for hf in range(2)
                ][-1][1].then_inc(dv.h, 1)))
            ns = sc.bump()
            S(lambda s, w=n1:
              (s.wait_ge(dv.h, w),
               s.activation(sig_c, gq_c, AF.Sigmoid).then_inc(sc.h, 1)))
            sp_guard = sp.n
            n2 = dv.bump()
            V(lambda v, m=m, w=ns, sp=sp, spg=sp_guard:
              (v.wait_ge(sc.h, w),
               v.wait_ge(sp.h, spg) if m >= 2 else None,
               [(v.tensor_scalar(scrf_i[:, 0:1024],
                                 sig_c[:, 1024 * hf:1024 * (hf + 1)],
                                 65536.0, None, ALU.mult),
                 v.tensor_scalar(sgq_c[:, 1024 * hf:1024 * (hf + 1)],
                                 scrf_i[:, 0:1024], 65535.0,
                                 1.0 / 65536.0, ALU.min, ALU.mult))
                for hf in range(2)],
               v.tensor_mul(g2st[m % 2], gq_c, sgq_c),
               v.tensor_reduce(gm_acc[:, m % 24:m % 24 + 1],
                               g2st[m % 2], AX,
                               ALU.max, apply_absolute_value=True
                               ).then_inc(dv.h, 1)))
            sp.bump(16)
            S(lambda s, m=m, w=n2, sp=sp:
              (s.wait_ge(dv.h, w),
               s.dma_start(out=g_spill[128 * m:128 * (m + 1), :],
                           in_=g2st[m % 2]).then_inc(sp.h, 16)))
        f4_sp = (ssp[0].n, ssp[1].n)
        tap("g2", None, wait=(lambda q, a=ssp[0].n, b=ssp[1].n:
                              (q.wait_ge(ssp[0].h, a), q.wait_ge(ssp[1].h, b))),
            dram_src=g_spill[0:128, :])

        bar3_dv = barrier(2, [(gm_acc, 24)], 6)

        plan.mark("f4")
        for m in range(NF):
            stgc = (scr1_b, e_sb)[m % 2]
            sr = srb[m % 2]
            sp = ssp[m % 2]
            wprev = dv.n
            tgt0 = sr.n
            sr.bump(32)
            Y(lambda q, m=m, b3=bar3_dv, w=wprev, stgc=stgc, sr=sr,
              fs=f4_sp:
              ((q.wait_ge(dv.h, b3),
                q.wait_ge(ssp[0].h, fs[0]), q.wait_ge(ssp[1].h, fs[1]))
               if m < 2 else q.wait_ge(dv.h, w - 1),
               q.dma_start(out=stgc,
                           in_=u_spill[128 * m:128 * (m + 1), :]
                           ).then_inc(sr.h, 16),
               q.dma_start(out=g2st[m % 2],
                           in_=g_spill[128 * m:128 * (m + 1), :]
                           ).then_inc(sr.h, 16)))
            sp_guard = sp.n
            n2 = dv.bump()
            V(lambda v, m=m, tgt=sr.n, stgc=stgc, sr=sr, sp=sp, spg=sp_guard:
              (v.wait_ge(sr.h, tgt),
               v.wait_ge(sp.h, spg) if m >= 2 else None,
               [(v.tensor_scalar(scrf_i[:, 0:1024],
                                 stgc[:, 1024 * hf:1024 * (hf + 1)],
                                 scl_bc[:, 5:6], None, ALU.mult),
                 v.tensor_scalar(sgq_c[:, 1024 * hf:1024 * (hf + 1)],
                                 scrf_i[:, 0:1024], scl_bc[:, 4:5],
                                 None, ALU.mult),
                 v.tensor_scalar(scrf_i[:, 0:1024],
                                 g2st[m % 2][:, 1024 * hf:1024 * (hf + 1)],
                                 scl_bc[:, 7:8], None, ALU.mult),
                 v.tensor_scalar(gq_c[:, 1024 * hf:1024 * (hf + 1)],
                                 scrf_i[:, 0:1024], scl_bc[:, 6:7],
                                 None, ALU.mult)) for hf in range(2)],
               v.tensor_mul(oost[m % 2], gq_c, sgq_c),
               v.tensor_reduce(um_acc[:, m % 24:m % 24 + 1],
                               oost[m % 2], AX,
                               ALU.max, apply_absolute_value=True
                               ).then_inc(dv.h, 1)))
            sp.bump(16)
            S(lambda s, m=m, w=n2, sp=sp:
              (s.wait_ge(dv.h, w),
               s.dma_start(out=u_spill[128 * m:128 * (m + 1), :],
                           in_=oost[m % 2]).then_inc(sp.h, 16)))
        f5_sp = (ssp[0].n, ssp[1].n)
        tap("o", None, wait=(lambda q, a=ssp[0].n, b=ssp[1].n:
                             (q.wait_ge(ssp[0].h, a), q.wait_ge(ssp[1].h, b))),
            dram_src=u_spill[0:128, :])

        bar4_dv = barrier(3, [(um_acc, 24)], 8)

        plan.mark("f5")
        f6_quant = []
        f6_out = []
        for tt in range(NT):
            so_ = sot[tt % 2]
            sx = sxq[tt % 2]
            su = sou[tt % 2]
            ot_tgt = so_.bump(16)
            Y(lambda q, tt=tt, b4=bar4_dv, fs=f5_sp, so_=so_,
              pv=(f6_quant[tt - 2] if tt >= 2 else None):
              ((q.wait_ge(ssp[0].h, fs[0]), q.wait_ge(ssp[1].h, fs[1]),
                q.wait_ge(dv.h, b4)) if tt == 0 else None,
               q.wait_ge(dv.h, pv) if pv is not None else None,
               q.dma_start(
                   out=o_tile[tt % 2][:].rearrange("p (ci f) -> p ci f", f=128),
                   in_=u_spill.rearrange("(ci p) t -> p ci t", p=128
                                         )[:, :, 128 * tt:128 * (tt + 1)],
               ).then_inc(so_.h, 16)))
            out_guard = su.n
            x_tgt = sx.bump(16)
            Y(lambda q, tt=tt, su=su, sx=sx, og=out_guard:
              (q.wait_ge(su.h, og) if tt >= 2 else None,
               q.dma_start(out=stg_f[tt % 2],
                           in_=x1_spill[128 * tt:128 * (tt + 1), :]
                           ).then_inc(sx.h, 16)))
            nq = dv.bump()
            f6_quant.append(nq)
            V(lambda v, tt=tt, ot=ot_tgt, b4=bar4_dv, so_=so_:
              (v.wait_ge(so_.h, ot),
               v.wait_ge(dv.h, b4) if tt == 0 else None,
               [(v.tensor_scalar(scrf_i[:, 0:1024],
                                 o_tile[tt % 2][:, 1024 * s3:1024 * (s3 + 1)],
                                 scl_bc[:, 9:10], None, ALU.mult),
                 v.tensor_scalar(o_tile[tt % 2][:, 1024 * s3:1024 * (s3 + 1)],
                                 scrf_i[:, 0:1024], scl_bc[:, 8:9],
                                 None, ALU.mult)) for s3 in range(3)
                ][-1][1].then_inc(dv.h, 1)))
            dn_pes = []
            for hh in range(2):
                ps = (pA, pB)[hh]
                dn_pes.append(pe.bump())
                TN(lambda tn, tt=tt, hh=hh, ps=ps, w=nq, dvw=dv.n:
                   (tn.wait_ge(dv.h, w),
                    tn.wait_ge(swd.h, wdT_tr) if tt == 0 else None,
                    tn.wait_ge(dv.h, dvw) if tt > 0 else None,
                    [tn.matmul(ps[:, 0:512],
                               o_tile[tt % 2][:, 128 * ci:128 * (ci + 1)],
                               wgT_b[:, 1024 * ci + 512 * hh:
                                     1024 * ci + 512 * (hh + 1)],
                               start=(ci == 0), stop=(ci == NF - 1))
                     for ci in range(NF)][-1].then_inc(pe.h, 1)))
            nev = dv.bump()
            V(lambda v, tt=tt, w=dn_pes, xt=x_tgt, sx=sx:
              (v.wait_ge(pe.h, w[1]),
               v.wait_ge(sx.h, xt),
               v.scalar_tensor_tensor(scrf_f[:, 0:512], pA[:, 0:512], 0.0,
                                      stg_f[tt % 2][:, 0:512],
                                      ALU.bypass, ALU.add),
               v.scalar_tensor_tensor(scrf_f[:, 512:1024], pB[:, 0:512], 0.0,
                                      stg_f[tt % 2][:, 512:1024],
                                      ALU.bypass, ALU.add),
               v.reduce_sum(st1_s[:, 0:1], scrf_f[:, 0:1024], AX),
               v.tensor_mul(scr1_f[:, 0:1024], scrf_f[:, 0:1024],
                            scrf_f[:, 0:1024]),
               v.reduce_sum(st1_s[:, 1:2], scr1_f[:, 0:1024], AX),
               [(v.tensor_scalar(st1_s[:, 2:3], st1_s[:, 0:1], 1.0 / D,
                                 None, ALU.mult),
                 v.tensor_scalar(st1_s[:, 3:4], st1_s[:, 1:2], 1.0 / D,
                                 None, ALU.mult),
                 v.tensor_mul(st1_s[:, 4:5], st1_s[:, 2:3], st1_s[:, 2:3]),
                 v.tensor_sub(st1_s[:, 7:8], st1_s[:, 3:4], st1_s[:, 4:5]))
                for _r in range(2)][-1][-1].then_inc(dv.h, 1)))
            nsd = sc.bump()
            S(lambda s, w=nev:
              (s.wait_ge(dv.h, w),
               s.activation(st1_s[:, 5:6], st1_s[:, 7:8], AF.Sqrt,
                            bias=eps_s).then_inc(sc.h, 1)))
            nap = dv.bump()
            V(lambda v, tt=tt, w=nsd:
              (v.wait_ge(sc.h, w),
               [v.memset(scr1_f[:, 0:1], 0.0) for _p in range(4)][-1],
               v.reciprocal(st1_s[:, 6:7], st1_s[:, 5:6]),
               v.reciprocal(st1_s[:, 6:7], st1_s[:, 5:6]),
               v.memset(scr1_f[:, 0:1], 0.0),
               v.memset(scr1_f[:, 0:1], 0.0),
               v.tensor_scalar(stg_f[tt % 2][:, 0:1024], scrf_f[:, 0:1024],
                               st1_s[:, 2:3], st1_s[:, 6:7],
                               ALU.subtract, ALU.mult).then_inc(dv.h, 1)))
            su.bump(16)
            Y(lambda q, tt=tt, w=nap, su=su:
              (q.wait_ge(dv.h, w),
               q.dma_start(out=out_h[128 * tt:128 * (tt + 1), :],
                           in_=stg_f[tt % 2]).then_inc(su.h, 16)))
            f6_out.append(su.n)

        Y(lambda q, a=sou[0].n, b=sou[1].n:
          (q.wait_ge(sou[0].h, a), q.wait_ge(sou[1].h, b),
           q.wait_ge(dbg_sem[0].h, dbg_sem[0].n) if _TAP is not None else None))

        globals()["_DBGS"] = dict(
            sld0=(sld[0].h, sld[0].n), sld1=(sld[1].h, sld[1].n),
            ld=(ld.h, ld.n), strx=(strx.h, strx.n), skt=(skt.h, skt.n),
            stq0=(stq[0].h, stq[0].n), stq1=(stq[1].h, stq[1].n),
            swg=(swg.h, swg.n), swu=(swu.h, swu.n), swd=(swd.h, swd.n),
            srb0=(srb[0].h, srb[0].n), srb1=(srb[1].h, srb[1].n),
            sxq0=(sxq[0].h, sxq[0].n), sxq1=(sxq[1].h, sxq[1].n),
            ssp0=(ssp[0].h, ssp[0].n), ssp1=(ssp[1].h, ssp[1].n),
            pe=(pe.h, pe.n), dv=(dv.h, dv.n), sc=(sc.h, sc.n), cc=(cc.h, cc.n),
            ldg=(ldg.h, ldg.n),
        )
        if _STOP is not None:
            cut = plan.marks[_STOP]
            for e in ENGS:
                plan.ops[e] = plan.ops[e][:cut[e]]

        def _phase_bounds(e):
            return sorted((v[e], k) for k, v in plan.marks.items())

        def _emit(eng_dev, e):
            bounds = _phase_bounds(e)
            bi = 0
            for i, fn in enumerate(plan.ops[e]):
                while bi < len(bounds) and i >= bounds[bi][0]:
                    bi += 1
                ph = bounds[bi][1] if bi < len(bounds) else "tail"
                with nc.named_scope(ph + "." + e[0]):
                    fn(eng_dev)

        block = est.enter_context(nc.Block())

        @block.sync
        def _(q):
            _emit(q, "sync")

        @block.tensor
        def _(tn):
            _emit(tn, "tensor")

        @block.vector
        def _(v):
            _emit(v, "vector")

        @block.scalar
        def _(s):
            _emit(s, "scalar")

        @block.gpsimd
        def _(q):
            _emit(q, "gpsimd")

    return nc
def _tap_out(dbg_h, s, d):
    ap = d if d is not None else s
    if len(ap.shape) == 2 and ap.shape[0] <= 128:
        rows, cols = ap.shape
        eltb = {mybir.dt.float32: 4, mybir.dt.bfloat16: 2,
                mybir.dt.int32: 4}[ap.dtype]
        ncols = cols * eltb // 4
        return dbg_h[0:rows, 0:ncols].bitcast(ap.dtype)
    raise ValueError(ap.shape)


_NC = None


def _constants(core):
    b, h = divmod(core, 2)
    pos = np.arange(2048 * h, 2048 * (h + 1), dtype=np.float64)
    inv = 1.0 / (10000.0 ** (np.arange(0, 64, 2, dtype=np.float64) / 64.0))
    ang = pos[:, None] * inv[None, :]
    cosv = np.cos(ang).astype(np.float32)
    sinv = np.sin(ang).astype(np.float32)
    rows = np.arange(128) % 64 % 32
    cos_b = np.ascontiguousarray(cosv[:, rows].T).astype(ml_dtypes.bfloat16)
    sin_b = np.ascontiguousarray(sinv[:, rows].T).astype(ml_dtypes.bfloat16)
    kt = np.arange(128)[:, None]
    qt = np.arange(128)[None, :]
    maskt = (kt <= qt).astype(np.float32).astype(ml_dtypes.bfloat16)
    p2l = np.zeros((128, 128), np.float32)
    for base in (0, 64):
        for i in range(32):
            p2l[base + i + 32, base + i] = -1.0
            p2l[base + i, base + i + 32] = 1.0
    p2l = p2l.astype(ml_dtypes.bfloat16)
    onesc = np.ones((128, 1), ml_dtypes.bfloat16)
    onesr = np.ones((1, 128), np.float32)
    eo = np.zeros((128, 2), np.float32)
    eo[:, 0] = 1.0 if h == 0 else 0.0
    eo[:, 1] = 1.0 - eo[:, 0]
    return dict(cosb=cos_b, sinb=sin_b, maskt=maskt, p2l=p2l, onesc=onesc,
                onesr=onesr, eo=eo)


def kernel(L0_attn_x, L0_attn_Wq, L0_attn_Wk, L0_attn_Wv, L0_attn_Wo,
           L0_attn_Wg1, L0_attn_Wg2, L0_attn_ln_w, L0_attn_ln_b,
           L0_ffn_w_gate, L0_ffn_w_up, L0_ffn_w_down):
    global _NC
    if _NC is None:
        _NC = build_nc()
    x = np.ascontiguousarray(np.asarray(L0_attn_x, np.float32))
    lnw = np.ascontiguousarray(
        np.asarray(L0_attn_ln_w, np.float32).reshape(8, 128).T)
    lnb = np.ascontiguousarray(
        np.asarray(L0_attn_ln_b, np.float32).reshape(8, 128).T)
    base = dict(
        wq=np.ascontiguousarray(L0_attn_Wq, dtype=np.float32),
        wk=np.ascontiguousarray(L0_attn_Wk, dtype=np.float32),
        wv=np.ascontiguousarray(L0_attn_Wv, dtype=np.float32),
        wo=np.ascontiguousarray(L0_attn_Wo, dtype=np.float32),
        wg1=np.ascontiguousarray(L0_attn_Wg1, dtype=np.float32),
        wg2=np.ascontiguousarray(L0_attn_Wg2, dtype=np.float32),
        lnw=lnw, lnb=lnb,
        wgate=np.ascontiguousarray(L0_ffn_w_gate, dtype=np.float32),
        wup=np.ascontiguousarray(L0_ffn_w_up, dtype=np.float32),
        wdown=np.ascontiguousarray(L0_ffn_w_down, dtype=np.float32),
    )
    in_maps = []
    for c in range(8):
        b, h = divmod(c, 2)
        m = dict(base)
        m["x"] = np.ascontiguousarray(x[b, 2048 * h:2048 * (h + 1), :])
        m.update(_constants(c))
        in_maps.append(m)
    res = run_bass_kernel_spmd(_NC, in_maps, core_ids=list(range(8)))
    out = np.empty((4, 4096, 1024), np.float32)
    for c in range(8):
        b, h = divmod(c, 2)
        out[b, 2048 * h:2048 * (h + 1), :] = res.results[c]["out"]
    return out


if __name__ == "__main__":
    build_nc()
    print("build ok")


# version 09
# speedup vs baseline: 1.0978x; 1.0978x over previous
import numpy as np
import ml_dtypes
from contextlib import ExitStack

import concourse.bass as bass
import concourse.mybir as mybir
from concourse.bass_utils import run_bass_kernel_spmd

F32 = mybir.dt.float32
BF16 = mybir.dt.bfloat16
I32 = mybir.dt.int32
U8 = mybir.dt.uint8
ALU = mybir.AluOpType
AF = mybir.ActivationFunctionType
AX = mybir.AxisListType.X

T, NT, D, ND = 2048, 16, 1024, 8
NH, NHP, HD = 16, 8, 64
FF, NF = 3072, 24
C, NCH, SCT, NSC = 128, 16, 512, 4
QMAX = 32767.0
EPS = 1e-5
ARENA_B = 211456

ENGS = ("sync", "tensor", "vector", "scalar", "gpsimd")

_TAP = None
_STOP = None



class Sem:
    def __init__(self, st, nc, name):
        self.h = st.enter_context(nc.semaphore(name))
        self.n = 0

    def bump(self, k=1):
        self.n += k
        return self.n


class Plan:
    def __init__(self):
        self.ops = {e: [] for e in ENGS}
        self.marks = {}

    def mark(self, name):
        self.marks[name] = {e: len(self.ops[e]) for e in ENGS}


def build_nc():
    nc = bass.Bass("TRN2", target_bir_lowering=False)

    def inp(name, shape, dt=F32):
        return nc.dram_tensor(name, shape, dt, kind="ExternalInput").ap()

    x_h = inp("x", [T, D])
    wq_h, wk_h, wv_h, wo_h = (inp(n, [D, D]) for n in ("wq", "wk", "wv", "wo"))
    wg1_h, wg2_h = inp("wg1", [D, HD]), inp("wg2", [HD, D])
    lnw_h, lnb_h = inp("lnw", [128, ND]), inp("lnb", [128, ND])
    wgate_h, wup_h = inp("wgate", [FF, D]), inp("wup", [FF, D])
    wdown_h = inp("wdown", [D, FF])
    cos_h, sin_h = inp("cosb", [128, T], BF16), inp("sinb", [128, T], BF16)
    maskT_h = inp("maskt", [128, 128], BF16)
    p2l_h = inp("p2l", [128, 128], BF16)
    onesc_h = inp("onesc", [128, 1], BF16)
    onesr_h = inp("onesr", [1, 128])
    eo_h = inp("eo", [128, 2])
    out_h = nc.dram_tensor("out", [T, D], F32, kind="ExternalOutput").ap()

    st_in = nc.dram_tensor("st_in", [128, 512], F32).ap()
    st_out = nc.dram_tensor("st_out", [128, 512], F32).ap()
    bar_in = [nc.dram_tensor(f"bar{i}_in", [1, 8], F32).ap() for i in range(4)]
    bar_out = [nc.dram_tensor(f"bar{i}_out", [1, 8], F32).ap() for i in range(4)]
    g_spill = nc.dram_tensor("g_spill", [FF, T], BF16).ap()
    u_spill = nc.dram_tensor("u_spill", [FF, T], BF16).ap()
    x1_spill = nc.dram_tensor("x1_spill", [T, D], F32).ap()

    with ExitStack() as est:
        arena = est.enter_context(nc.sbuf_tensor("arena", [128, ARENA_B], U8))
        cur = [0]

        def carve(nbytes, align=64):
            off = (cur[0] + align - 1) // align * align
            cur[0] = off + nbytes
            assert cur[0] <= ARENA_B, f"arena overflow {cur[0]}"
            return off

        def vvw(off, nbytes, dt, rows=128, r0=0):
            return arena[r0:r0 + rows, off:off + nbytes].bitcast(dt)

        cos_b = vvw(carve(4096), 4096, BF16)
        sin_b = vvw(carve(4096), 4096, BF16)
        maskT_b = vvw(carve(256), 256, BF16)
        p2l_b = vvw(carve(256), 256, BF16)
        onesc_b = vvw(carve(64), 2, BF16)
        onesr_f = vvw(carve(512), 512, F32, rows=1)
        lnw_s = vvw(carve(64), 32, F32)
        lnb_s = vvw(carve(64), 32, F32)
        eo_s = vvw(carve(64), 8, F32)
        wg1_b = vvw(carve(1024), 1024, BF16)
        o_g1 = carve(4096)
        g1_b = vvw(o_g1, 4096, BF16, rows=64)
        mx_acc = vvw(carve(128), 128, F32)
        gm_acc = vvw(carve(128), 128, F32)
        um_acc = vvw(carve(128), 128, F32)
        row_s = vvw(carve(512), 512, F32, rows=1)
        sc_s = vvw(carve(64), 32, F32, rows=1)
        res_s = vvw(carve(64), 32, F32, rows=1)
        scl_bc = vvw(carve(64), 48, F32)
        st1_s = vvw(carve(64), 32, F32)
        mu_row = vvw(carve(2048), 2048, F32, rows=1)
        rr_row = vvw(carve(2048), 2048, F32, rows=1)
        tmp1_s = vvw(carve(64), 32, F32, rows=1)
        eps_s = vvw(carve(64), 4, F32)
        kf_j = vvw(carve(2048), 2048, BF16)
        v_j = vvw(carve(2048), 2048, BF16)
        o_scr1 = carve(4096)
        scr1_b = vvw(o_scr1, 4096, BF16)
        scr1_f = vvw(o_scr1, 4096, F32)
        kt_j = vvw(o_scr1 + 2048, 2048, BF16)
        e_sb = vvw(carve(4096), 4096, BF16)
        o_sz = carve(2048)
        s_z = vvw(o_sz, 2048, BF16)
        s_iz = s_z
        o_scrf = carve(4096)
        scrf_f = vvw(o_scrf, 4096, F32)
        scrf_i = vvw(o_scrf, 4096, I32)
        s_fin = vvw(o_scrf + 2048, 2048, F32)
        o_stg = carve(8192)
        stg_f = [vvw(o_stg, 4096, F32), vvw(o_stg + 4096, 4096, F32)]
        kf_z = [vvw(o_stg, 2048, BF16), vvw(o_stg + 2048, 2048, BF16)]
        wd_st = [vvw(o_stg, 2048, BF16), vvw(o_stg + 2048, 2048, BF16)]
        bm_s = vvw(carve(512), 384, F32)
        br_s = vvw(carve(512), 384, F32)
        globals()["_DBG"] = dict(scrf=o_scrf, stg=o_stg)

        o_biga = carve(32768)
        o_bigb = carve(32768)
        o_bige = carve(32768)
        o_wk = carve(16384)
        o_wv = carve(16384)
        o_wo = carve(16384)
        o_wcur = carve(16384)
        assert o_wk == o_bige + 32768 and o_wo == o_wv + 16384
        assert o_wcur == o_wo + 16384 and o_bigb == o_biga + 32768

        xT_b = vvw(o_biga, 32768, BF16)
        xqT_b = xT_b
        q_b = vvw(o_bigb, 32768, BF16)
        x_tmb = vvw(o_bige, 32768, BF16)
        og_b = vvw(o_bige, 32768, BF16)
        wk_b = vvw(o_wk, 16384, BF16)
        wv_b = vvw(o_wv, 16384, BF16)
        wo_b = vvw(o_wo, 16384, BF16)
        wq_b = vvw(o_wcur, 16384, BF16)
        wg2_b = vvw(o_wcur, 2048, BF16, rows=64)
        wgT_b = vvw(o_bige, 49152, BF16)
        wuT_b = vvw(o_wv, 49152, BF16)
        g2st = [vvw(o_bigb, 4096, BF16), vvw(o_bigb + 4096, 4096, BF16)]
        oost = [vvw(o_bigb + 8192, 4096, BF16),
                vvw(o_bigb + 12288, 4096, BF16)]
        o_tile = [vvw(o_biga, 6144, BF16), vvw(o_biga + 6144, 6144, BF16)]
        gq_c = vvw(o_bigb + 16384, 4096, BF16)
        sig_c = vvw(o_bigb + 16384 + 4096, 4096, BF16)
        sgq_c = vvw(o_bigb + 16384 + 8192, 4096, BF16)
        xqt_st = [vvw(o_bigb + 16384 + 12288, 2048, BF16),
                  vvw(o_bigb + 16384 + 14336, 2048, BF16)]
        wn6_b = vvw(o_bigb + 16384, 12288, BF16)

        pA = est.enter_context(nc.psum_tensor("pA", [128, 512], F32))
        pB = est.enter_context(nc.psum_tensor("pB", [128, 512], F32))
        pC = est.enter_context(nc.psum_tensor("pC", [128, 512], F32))
        pE1 = est.enter_context(nc.psum_tensor("pE1", [128, 512], F32))
        pE2 = est.enter_context(nc.psum_tensor("pE2", [128, 512], F32))
        pO = est.enter_context(nc.psum_tensor("pO", [128, 1024], F32))
        pS = est.enter_context(nc.psum_tensor("pS", [128, 512], F32))
        rot3 = (pA, pB, pC)

        ld = Sem(est, nc, "ld")
        sld = [Sem(est, nc, "sld0"), Sem(est, nc, "sld1")]
        strx = Sem(est, nc, "strx")
        skt = Sem(est, nc, "skt")
        stq = [Sem(est, nc, "stq0"), Sem(est, nc, "stq1")]
        swg = Sem(est, nc, "swg")
        swu = Sem(est, nc, "swu")
        swd = Sem(est, nc, "swd")
        sot = [Sem(est, nc, "sot0"), Sem(est, nc, "sot1")]
        srb = [Sem(est, nc, "srb0"), Sem(est, nc, "srb1")]
        sxq = [Sem(est, nc, "sxq0"), Sem(est, nc, "sxq1")]
        ssp = [Sem(est, nc, "ssp0"), Sem(est, nc, "ssp1")]
        sou = [Sem(est, nc, "sou0"), Sem(est, nc, "sou1")]
        ldg = Sem(est, nc, "ldg")
        pe = Sem(est, nc, "pe")
        dv = Sem(est, nc, "dv")
        sc = Sem(est, nc, "sc")
        cc = Sem(est, nc, "cc")

        plan = Plan()

        dbg_h = None
        dbg_sem = [None]
        if _TAP is not None:
            dbg_h = nc.dram_tensor("dbg", [128, 16384], F32,
                                   kind="ExternalOutput").ap()
            dbg_sem[0] = Sem(est, nc, "dbgsem")

        def Y(fn): plan.ops["sync"].append(fn)
        def TN(fn): plan.ops["tensor"].append(fn)

        def tap(name, src_ap, wait=None, dram_src=None):
            if _TAP != name:
                return
            ds = dbg_sem[0]
            ds.bump(16)
            Y(lambda q, s=src_ap, w=wait, d=dram_src:
              (w(q) if w else None,
               q.dma_start(out=dbg_h[:, 0:(d if d else s).free_size()
                               ] if False else _tap_out(dbg_h, s, d),
                           in_=(d if d is not None else s)).then_inc(ds.h, 16)))

        def V(fn): plan.ops["vector"].append(fn)
        def S(fn): plan.ops["scalar"].append(fn)
        def G(fn): plan.ops["gpsimd"].append(fn)

        dv.bump()
        V(lambda v: (v.memset(eps_s, EPS),
                     v.memset(sc_s, 0.0),
                     v.memset(res_s, 0.0),
                     v.memset(tmp1_s, 0.0),
                     v.memset(mx_acc, 0.0),
                     v.memset(gm_acc, 0.0),
                     v.memset(um_acc, 0.0),
                     v.memset(scl_bc, 0.0).then_inc(dv.h, 1)))

        for dst, src in ((cos_b, cos_h), (sin_b, sin_h), (maskT_b, maskT_h),
                         (p2l_b, p2l_h), (onesc_b, onesc_h), (onesr_f, onesr_h),
                         (lnw_s, lnw_h), (lnb_s, lnb_h), (eo_s, eo_h)):
            ld.bump(16)
            Y(lambda q, d=dst, s=src: q.dma_start(out=d, in_=s).then_inc(ld.h, 16))
        consts_ld = ld.n

        def load_cast(dst_b, src_h, nchunks, width, wslot):
            for k in range(nchunks):
                sl = sld[k % 2]
                wprev = dv.n - 1
                tgt = sl.bump(16)
                Y(lambda q, k=k, w=wprev, s=src_h, wd=width, sl=sl:
                  (q.wait_ge(dv.h, max(0, w)),
                   q.dma_start(out=stg_f[k % 2][:, :wd],
                               in_=s[128 * k:128 * (k + 1), :]
                               ).then_inc(sl.h, 16)))
                dv.bump()
                V(lambda v, k=k, tgt=tgt, d=dst_b, wd=width, ws=wslot, sl=sl:
                  (v.wait_ge(sl.h, tgt),
                   v.tensor_copy(d[:, ws * k:ws * k + wd],
                                 stg_f[k % 2][:, :wd]).then_inc(dv.h, 1)))

        load_cast(x_tmb, x_h, NT, 1024, 1024)
        x_cast_dv = dv.n

        for j in range(NT):
            strx.bump(16)
            Y(lambda q, j=j, w=x_cast_dv:
              (q.wait_ge(dv.h, w),
               q.dma_start_transpose(
                   out=xT_b[:].rearrange("p (k t) -> p k t", t=2048
                                         )[:, :, 128 * j:128 * (j + 1)],
                   in_=x_tmb[:, 1024 * j:1024 * (j + 1)].rearrange(
                       "t (k h) -> t k h", h=128),
               ).then_inc(strx.h, 16)))
        xT_tr = strx.n
        tap("xT", xT_b, wait=(lambda q: q.wait_ge(strx.h, xT_tr)))

        load_cast(wq_b, wq_h, ND, 1024, 1024)
        load_cast(wk_b, wk_h, ND, 1024, 1024)
        load_cast(wv_b, wv_h, ND, 1024, 1024)
        load_cast(wo_b, wo_h, ND, 1024, 1024)
        load_cast(wg1_b, wg1_h, ND, 64, 64)
        allw_dv = dv.n
        tap("wq", wq_b, wait=(lambda q, w=allw_dv: q.wait_ge(dv.h, w)))

        plan.mark("p0")
        def mm_group(ps_ap, pairs, pre=None):
            n = pe.bump()
            def f(tn, P=list(pairs), pre=pre, ps=ps_ap):
                if pre:
                    pre(tn)
                for i, (l, r) in enumerate(P):
                    ins = tn.matmul(ps, l, r,
                                    start=(i == 0), stop=(i == len(P) - 1))
                ins.then_inc(pe.h, 1)
            TN(f)
            return n

        g1_sc0 = sc.n
        g1_pes = []
        for tt in range(4):
            g1_pes.append(mm_group(
                rot3[tt % 3][0:64, 0:512],
                [(wg1_b[:, 64 * k:64 * (k + 1)],
                  xT_b[:, 2048 * k + 512 * tt:2048 * k + 512 * (tt + 1)])
                 for k in range(ND)],
                pre=(lambda tn, t=tt:
                     (tn.wait_ge(strx.h, xT_tr), tn.wait_ge(dv.h, allw_dv),
                      tn.wait_ge(ld.h, consts_ld),
                      tn.wait_ge(sc.h, g1_sc0 + t - 2) if t >= 3 else None)))
            )
        for tt in range(4):
            sc.bump()
            S(lambda s, t=tt, w=g1_pes[tt]:
              (s.wait_ge(pe.h, w),
               s.copy(g1_b[0:64, 512 * t:512 * (t + 1)],
                      rot3[t % 3][0:64, 0:512]).then_inc(sc.h, 1)))
        g1_sc = sc.n

        q_sc0 = sc.n
        for i in range(32):
            m, tt = divmod(i, 4)
            mm_group(
                rot3[i % 3][:, 0:512],
                [(wq_b[:, 1024 * k + 128 * m:1024 * k + 128 * (m + 1)],
                  xT_b[:, 2048 * k + 512 * tt:2048 * k + 512 * (tt + 1)])
                 for k in range(ND)],
                pre=(lambda tn, i=i:
                     (tn.wait_ge(sc.h, g1_sc),
                      tn.wait_ge(sc.h, q_sc0 + i - 2) if i >= 3 else None)))
            sc.bump()
            S(lambda s, i=i, m=m, tt=tt, w=pe.n:
              (s.wait_ge(pe.h, w),
               s.activation(q_b[:, 2048 * m + 512 * tt:2048 * m + 512 * (tt + 1)],
                            rot3[i % 3][:, 0:512], AF.Silu).then_inc(sc.h, 1)))
        q_sc = sc.n
        tap("qsilu", q_b, wait=(lambda q, w=q_sc: q.wait_ge(sc.h, w)))

        q_pe_done = pe.n
        sl = sld[0]
        wprev = dv.n - 1
        tgt = sl.bump(16)
        Y(lambda q, w=wprev, sl=sl, qp=q_pe_done:
          (q.wait_ge(dv.h, max(0, w)),
           q.wait_ge(pe.h, qp),
           q.dma_start(out=stg_f[0][0:64, :], in_=wg2_h).then_inc(sl.h, 16)))
        dv.bump()
        V(lambda v, tgt=tgt, sl=sl:
          (v.wait_ge(sl.h, tgt),
           v.tensor_copy(wg2_b[0:64, :], stg_f[0][0:64, :]).then_inc(dv.h, 1)))
        wg2_dv = dv.n

        def rope(buf, m_off, tok0, ntok, v_pre=None):
            u = scr1_b[:, 0:ntok]
            cw = cos_b[:, tok0:tok0 + ntok]
            sw = sin_b[:, tok0:tok0 + ntok]
            pe_at = pe.n
            n1 = dv.bump()
            V(lambda v, vp=v_pre, pa=pe_at:
              (vp(v) if vp else None,
               v.wait_ge(pe.h, pa),
               v.tensor_mul(u, buf[:, m_off:m_off + ntok], sw),
               v.tensor_mul(buf[:, m_off:m_off + ntok],
                            buf[:, m_off:m_off + ntok], cw).then_inc(dv.h, 1)))
            tap("u%d" % (m_off // 2048), u,
                wait=(lambda q, w=n1: q.wait_ge(dv.h, w)))
            nj = (ntok + 511) // 512
            add_dvs = []
            for j in range(nj):
                wdt = min(512, ntok - 512 * j)
                ps = (pA, pB)[j % 2]
                pn = pe.bump()
                TN(lambda tn, j=j, ps=ps, wdt=wdt, n1=n1,
                   ag=(add_dvs[j - 2] if j >= 2 else None):
                   (tn.wait_ge(dv.h, n1),
                    tn.wait_ge(dv.h, ag) if ag is not None else None,
                    tn.matmul(ps[:, 0:wdt], p2l_b, u[:, 512 * j:512 * j + wdt],
                              start=True, stop=True).then_inc(pe.h, 1)))
                add_dvs.append(dv.bump())
                V(lambda v, j=j, ps=ps, wdt=wdt, pn=pn, mo=m_off:
                  (v.wait_ge(pe.h, pn),
                   v.tensor_add(buf[:, mo + 512 * j:mo + 512 * j + wdt],
                                buf[:, mo + 512 * j:mo + 512 * j + wdt],
                                ps[:, 0:wdt]).then_inc(dv.h, 1)))

        for m in range(ND):
            rope(q_b, 2048 * m, 0, 2048,
                 v_pre=(lambda v, m=m, s0=q_sc - 32:
                        v.wait_ge(sc.h, s0 + 4 * (m + 1))))
        qrope_dv = dv.n
        tap("q", q_b, wait=(lambda q, w=qrope_dv: q.wait_ge(dv.h, w)))

        plan.mark("p1")
        dv.bump()
        V(lambda v:
          (v.memset(kf_z[0][64:128, :], 0.0),
           v.memset(kf_z[1][0:64, :], 0.0),
           v.memset(s_z[:].rearrange("p (hp c) -> p hp c", c=128)
                    [64:128, :, 0:64], 0.0),
           v.memset(s_z[:].rearrange("p (hp c) -> p hp c", c=128)
                    [0:64, :, 64:128], 0.0).then_inc(dv.h, 1)))
        og_sc_prev = [sc.n]
        for ci in range(NCH):
            tok0 = C * ci
            ksc0 = sc.n
            kq_dv = dv.n
            for bank in range(2):
                ps = (pA, pB)[bank]
                n = pe.bump()
                def kproj(tn, bank=bank, ps=ps, kq=kq_dv, tok0=tok0):
                    if bank == 0:
                        tn.wait_ge(dv.h, kq)
                    first = True
                    for mm in range(4):
                        m = 4 * bank + mm
                        for k in range(ND):
                            ins = tn.matmul(
                                ps[:, 128 * mm:128 * (mm + 1)],
                                wk_b[:, 1024 * k + 128 * m:
                                     1024 * k + 128 * (m + 1)],
                                xT_b[:, 2048 * k + tok0:2048 * k + tok0 + 128],
                                start=first,
                                stop=(mm == 3 and k == ND - 1))
                            first = False
                    ins.then_inc(pe.h, 1)
                TN(kproj)
                sc.bump()
                S(lambda s, bank=bank, w=pe.n, ps=ps:
                  (s.wait_ge(pe.h, w),
                   s.activation(kf_j[:, 512 * bank:512 * (bank + 1)],
                                ps[:, 0:512], AF.Silu).then_inc(sc.h, 1)))
            ksil_sc = sc.n
            if ci == 0:
                plan.mark("c0a")
            cw = cos_b[:, tok0:tok0 + 128].unsqueeze(1).broadcast_to(
                [128, ND, 128])
            sw = sin_b[:, tok0:tok0 + 128].unsqueeze(1).broadcast_to(
                [128, ND, 128])
            kf3 = kf_j[:].rearrange("p (j t) -> p j t", t=128)
            u3 = scr1_b[:, 0:1024].rearrange("p (j t) -> p j t", t=128)
            n1 = dv.bump()
            V(lambda v, w=ksil_sc, pa=pe.n, cw=cw, sw=sw, kf3=kf3, u3=u3:
              (v.wait_ge(sc.h, w),
               v.wait_ge(pe.h, pa),
               v.tensor_tensor(u3, kf3, sw, ALU.mult),
               v.tensor_tensor(kf3, kf3, cw, ALU.mult).then_inc(dv.h, 1)))
            if ci == 0:
                plan.mark("c0b")
            prev_add = [None]
            for g2 in range(2):
                pn = pe.bump()
                TN(lambda tn, g2=g2, n1=n1, ag=prev_add[0]:
                   (tn.wait_ge(dv.h, n1),
                    tn.wait_ge(dv.h, ag) if ag is not None else None,
                    [tn.matmul(pC[:, 128 * mm:128 * (mm + 1)], p2l_b,
                               scr1_b[:, 512 * g2 + 128 * mm:
                                      512 * g2 + 128 * (mm + 1)],
                               start=(mm == 0), stop=(mm == 3))
                     for mm in range(4)][-1].then_inc(pe.h, 1)))
                prev_add[0] = dv.bump()
                V(lambda v, g2=g2, w=pn:
                  (v.wait_ge(pe.h, w),
                   v.tensor_add(kf_j[:, 512 * g2:512 * (g2 + 1)],
                                kf_j[:, 512 * g2:512 * (g2 + 1)],
                                pC[:, 0:512]).then_inc(dv.h, 1)))
            krope_dv = dv.n
            kz_dv = dv.bump()
            V(lambda v:
              (v.tensor_copy(kf_z[0][0:64, :], kf_j[0:64, :]),
               v.tensor_copy(kf_z[1][64:128, :], kf_j[64:128, :]
                             ).then_inc(dv.h, 1)))
            if ci == 0:
                plan.mark("c0c")
            kt_tgt = skt.bump(16)
            Y(lambda q, w=krope_dv, pa=pe.n:
              (q.wait_ge(dv.h, w),
               q.wait_ge(pe.h, pa),
               q.dma_start_transpose(
                   out=kt_j[:].rearrange("t (j F) -> t j F", F=128),
                   in_=kf_j[:].rearrange("f (j t) -> f j t", t=128),
               ).then_inc(skt.h, 16)))
            if ci == 0:
                plan.mark("c0d")
            vsc0 = sc.n
            for hh in range(2):
                ps = (pA, pB)[hh]
                mm_group(
                    ps[:, 0:512],
                    [(xT_b[:, 2048 * k + tok0:2048 * k + tok0 + 128],
                      wv_b[:, 1024 * k + 512 * hh:1024 * k + 512 * (hh + 1)])
                     for k in range(ND)],
                    pre=(lambda tn, hh=hh, kq=krope_dv:
                         tn.wait_ge(dv.h, kq) if hh == 0 else None))
                sc.bump()
                S(lambda s, hh=hh, w=pe.n, ps=ps:
                  (s.wait_ge(pe.h, w),
                   s.copy(v_j[:, 512 * hh:512 * (hh + 1)],
                          ps[:, 0:512]).then_inc(sc.h, 1)))
            if ci == 0:
                plan.mark("c0e")
            v_sc = sc.n

            for gidx in range(4):
                psb = (pE1, pE2)[gidx % 2]
                dv_guard = dv.n
                n = pe.bump()
                def ef(tn, gidx=gidx, psb=psb, ci=ci,
                       ktw=kt_tgt, vsc=v_sc, dg=dv_guard, kz=kz_dv):
                    tn.wait_ge(sc.h, vsc)
                    tn.wait_ge(skt.h, ktw)
                    tn.wait_ge(dv.h, kz)
                    if gidx >= 2:
                        tn.wait_ge(dv.h, dg)
                    for hh in range(4):
                        h = 4 * gidx + hh
                        m, par = divmod(h, 2)
                        ins = tn.matmul(
                            psb[:, 128 * hh:128 * (hh + 1)],
                            kf_z[par][:, 128 * m:128 * (m + 1)],
                            q_b[:, 2048 * m + 128 * ci:
                                2048 * m + 128 * (ci + 1)],
                            start=(hh == 0), stop=(hh == 3))
                    ins.then_inc(pe.h, 1)
                TN(ef)
                dv.bump()
                V(lambda v, gidx=gidx, psb=psb, n=n:
                  (v.wait_ge(pe.h, n),
                   v.tensor_tensor(
                       e_sb[:, 512 * gidx:512 * (gidx + 1)].rearrange(
                           "p (j f) -> p j f", f=128),
                       psb[:, 0:512].rearrange("p (j f) -> p j f", f=128),
                       maskT_b.unsqueeze(1).broadcast_to([128, 4, 128]),
                       ALU.mult).then_inc(dv.h, 1)))
            if ci == 0:
                plan.mark("c0f")
            eev_dv = dv.n
            av_pe = pe.bump()
            def avf(tn, ci=ci, ev=eev_dv, ogp=og_sc_prev[0]):
                tn.wait_ge(dv.h, ev)
                tn.wait_ge(sc.h, ogp)
                seen = set()
                plan_mm = []
                for h in range(NH):
                    hp, par = divmod(h, 2)
                    plan_mm.append(("av", h, hp, par, (hp // 4, par)))
                if ci > 0 and not globals().get('_NO_INTER'):
                    for h in range(NH):
                        hp, par = divmod(h, 2)
                        plan_mm.append(("in", h, hp, par, (hp // 4, par)))
                lastidx = {}
                for idx, (_, _, _, _, grp) in enumerate(plan_mm):
                    lastidx[grp] = idx
                for idx, (kind, h, hp, par, grp) in enumerate(plan_mm):
                    first = grp not in seen
                    seen.add(grp)
                    stop = (lastidx[grp] == idx)
                    if kind == "av":
                        ins = tn.matmul(
                            pO[64 * par:64 * (par + 1), 128 * hp:128 * (hp + 1)],
                            v_j[:, 64 * h:64 * (h + 1)],
                            e_sb[:, 128 * h:128 * (h + 1)],
                            start=first, stop=stop,
                            skip_group_check=(par == 1),
                            tile_position=(0, 64 * par))
                    else:
                        ins = tn.matmul(
                            pO[64 * par:64 * (par + 1), 128 * hp:128 * (hp + 1)],
                            s_z[:, 64 * h:64 * (h + 1)],
                            q_b[:, 2048 * hp + 128 * ci:
                                2048 * hp + 128 * (ci + 1)],
                            start=first, stop=stop,
                            skip_group_check=(par == 1),
                            tile_position=(0, 64 * par))
                ins.then_inc(pe.h, 1)
            TN(avf)
            if ci == 0:
                plan.mark("c0g")
            st_pe = pe.bump()
            def stf(tn, ci=ci, sd=dv.n):
                tn.wait_ge(dv.h, sd)
                for h in range(NH):
                    hp, par = divmod(h, 2)
                    ins = tn.matmul(
                        pS[64 * par:64 * (par + 1), 64 * hp:64 * (hp + 1)],
                        kt_j[:, 64 * h:64 * (h + 1)],
                        v_j[:, 64 * h:64 * (h + 1)],
                        start=(h < 2), stop=(h >= NH - 2),
                        skip_group_check=(par == 1),
                        tile_position=(0, 64 * par))
                ins.then_inc(pe.h, 1)
            TN(stf)
            dv.bump()
            if ci == 0:
                V(lambda v, w=st_pe:
                  (v.wait_ge(pe.h, w),
                   v.tensor_copy(s_fin, pS[:, 0:512]),
                   v.tensor_copy(
                       s_z[:].rearrange("p (hp c) -> p hp c", c=128)
                       [0:64, :, 0:64],
                       s_fin[0:64, :].rearrange("p (hp c) -> p hp c", c=64)),
                   v.tensor_copy(
                       s_z[:].rearrange("p (hp c) -> p hp c", c=128)
                       [64:128, :, 64:128],
                       s_fin[64:128, :].rearrange("p (hp c) -> p hp c", c=64)
                   ).then_inc(dv.h, 1)))
            else:
                V(lambda v, w=st_pe:
                  (v.wait_ge(pe.h, w),
                   v.tensor_add(s_fin, s_fin, pS[:, 0:512]),
                   v.tensor_copy(
                       s_z[:].rearrange("p (hp c) -> p hp c", c=128)
                       [0:64, :, 0:64],
                       s_fin[0:64, :].rearrange("p (hp c) -> p hp c", c=64)),
                   v.tensor_copy(
                       s_z[:].rearrange("p (hp c) -> p hp c", c=128)
                       [64:128, :, 64:128],
                       s_fin[64:128, :].rearrange("p (hp c) -> p hp c", c=64)
                   ).then_inc(dv.h, 1)))
            for gg in range(2):
                sc.bump()
                S(lambda s, gg=gg, ci=ci, w=av_pe:
                  (s.wait_ge(pe.h, w),
                   s.copy(og_b[:].rearrange("p (hp t) -> p hp t", t=2048
                                            )[:, 4 * gg:4 * (gg + 1),
                                              128 * ci:128 * (ci + 1)],
                          pO[:, 512 * gg:512 * (gg + 1)].rearrange(
                              "p (j f) -> p j f", f=128)).then_inc(sc.h, 1)))
            og_sc_prev[0] = sc.n
            plan.mark(f"p3c{ci}")
        loop_sc = sc.n
        loop_pe = pe.n
        tap("og0", og_b, wait=(lambda q, w=loop_sc: q.wait_ge(sc.h, w)))
        tap("S", s_fin, wait=(lambda q, w=dv.n: q.wait_ge(dv.h, w)))

        plan.mark("p3")
        n1 = dv.bump()
        V(lambda v, w=loop_pe, dvp=dv.n - 1:
          (v.tensor_scalar(scrf_f[:, 0:512], s_fin, eo_s[:, 0:1], None,
                           ALU.mult).then_inc(dv.h, 1)))
        ldg.bump(16)
        G(lambda q, w=n1, t=ldg.n:
          (q.wait_ge(dv.h, w),
           q.dma_start(out=st_in, in_=scrf_f[:, 0:512]).then_inc(ldg.h, 16)))
        cc.bump()
        G(lambda q, w=ldg.n, t=cc.n:
          (q.wait_ge(ldg.h, w),
           q.collective_compute(
               "AllReduce", ALU.add,
               replica_groups=[[0, 1], [2, 3], [4, 5], [6, 7]],
               ins=[st_in.opt()], outs=[st_out.opt()]).then_inc(cc.h, 1)))
        ldg.bump(16)
        G(lambda q, w=cc.n, t=ldg.n:
          (q.wait_ge(cc.h, w),
           q.dma_start(out=scrf_f[:, 0:512], in_=st_out).then_inc(ldg.h, 16)))
        sini_dv = dv.bump()
        V(lambda v, w=ldg.n:
          (v.wait_ge(ldg.h, w),
           v.tensor_scalar(
               s_iz[:].rearrange("p (hp c) -> p hp c", c=128)[0:64, :, 0:64],
               scrf_f[0:64, 0:512].rearrange("p (hp c) -> p hp c", c=64),
               eo_s[0:64, 1:2], None, ALU.mult),
           v.tensor_scalar(
               s_iz[:].rearrange("p (hp c) -> p hp c", c=128)
               [64:128, :, 64:128],
               scrf_f[64:128, 0:512].rearrange("p (hp c) -> p hp c", c=64),
               eo_s[64:128, 1:2], None, ALU.mult).then_inc(dv.h, 1)))

        plan.mark("p4")
        first_pref = [True]
        for ci in range(NCH):
            for gg in range(2):
                psb = (pE1, pE2)[gg]
                dv_guard = dv.n
                n = pe.bump()
                def pf(tn, ci=ci, gg=gg, psb=psb, dg=dv_guard, fp=first_pref[0]):
                    tn.wait_ge(dv.h, sini_dv)
                    if fp:
                        tn.wait_ge(sc.h, loop_sc)
                    else:
                        tn.wait_ge(dv.h, dg)
                    for hh in range(4):
                        hp = 4 * gg + hh
                        for par in range(2):
                            h = 2 * hp + par
                            ins = tn.matmul(
                                psb[64 * par:64 * (par + 1),
                                    128 * hh:128 * (hh + 1)],
                                s_iz[:, 64 * h:64 * (h + 1)],
                                q_b[:, 2048 * hp + 128 * ci:
                                    2048 * hp + 128 * (ci + 1)],
                                start=(hh == 0), stop=(hh == 3),
                                skip_group_check=(par == 1),
                                tile_position=(0, 64 * par))
                    ins.then_inc(pe.h, 1)
                TN(pf)
                first_pref[0] = False
                dv.bump()
                V(lambda v, ci=ci, gg=gg, psb=psb, n=n:
                  (v.wait_ge(pe.h, n),
                   v.tensor_add(
                       og_b[:].rearrange("p (hp t) -> p hp t", t=2048
                                         )[:, 4 * gg:4 * (gg + 1),
                                           128 * ci:128 * (ci + 1)],
                       og_b[:].rearrange("p (hp t) -> p hp t", t=2048
                                         )[:, 4 * gg:4 * (gg + 1),
                                           128 * ci:128 * (ci + 1)],
                       psb[:, 0:512].rearrange("p (j f) -> p j f", f=128)
                   ).then_inc(dv.h, 1)))
        prefix_dv = dv.n
        tap("og1", og_b, wait=(lambda q, w=prefix_dv: q.wait_ge(dv.h, w)))
        og1_tap_n = dbg_sem[0].n if _TAP == "og1" else None
        tap("sini", s_iz, wait=(lambda q, w=sini_dv: q.wait_ge(dv.h, w)))

        plan.mark("p5")
        gate_dv_prev = [prefix_dv]
        for m in range(ND):
            gsc0 = sc.n
            gpes = []
            for tt in range(4):
                ps = rot3[tt % 3]
                gpes.append(pe.bump())
                TN(lambda tn, m=m, tt=tt, ps=ps, w=prefix_dv, g0=gsc0,
                   w2=wg2_dv,
                   scw={0: gsc0, 1: gsc0 - 2, 2: gsc0 - 1,
                        3: gsc0 + 1}[tt]:
                   (tn.wait_ge(dv.h, max(w, w2)),
                    tn.wait_ge(sc.h, max(0, scw)),
                    tn.matmul(ps[:, 0:512],
                              wg2_b[0:64, 128 * m:128 * (m + 1)],
                              g1_b[0:64, 512 * tt:512 * (tt + 1)],
                              start=True, stop=True, tile_position=(0, 0)
                              ).then_inc(pe.h, 1)))
            for tt in range(4):
                sc.bump()
                S(lambda s, m=m, tt=tt, w=gpes[tt], gd=gate_dv_prev[0]:
                  (s.wait_ge(pe.h, w),
                   s.wait_ge(dv.h, gd) if tt == 0 else None,
                   s.activation(scr1_b[:, 512 * tt:512 * (tt + 1)],
                                rot3[tt % 3][:, 0:512], AF.Sigmoid
                                ).then_inc(sc.h, 1)))
            gate_dv_prev[0] = dv.bump()
            V(lambda v, m=m, w=sc.n, tn_=og1_tap_n if m == 0 else None:
              (v.wait_ge(sc.h, w),
               v.wait_ge(dbg_sem[0].h, tn_) if tn_ is not None else None,
               v.tensor_mul(og_b[:, 2048 * m:2048 * (m + 1)],
                            og_b[:, 2048 * m:2048 * (m + 1)],
                            scr1_b).then_inc(dv.h, 1)))
        gated_dv = dv.n
        tap("og2", og_b, wait=(lambda q, w=gated_dv: q.wait_ge(dv.h, w)))
        og2_tap_n = dbg_sem[0].n if _TAP == "og2" else None

        plan.mark("p6")
        prev_stats_dv = [gated_dv]
        prev_nsq = [0]
        for tt in range(4):
            n_sum = pe.bump()
            TN(lambda tn, tt=tt, w=gated_dv, pv=prev_stats_dv[0]:
               (tn.wait_ge(dv.h, max(w, pv)),
                [tn.matmul(pE1[0:1, 0:512], onesc_b,
                           og_b[:, 2048 * m + 512 * tt:2048 * m + 512 * (tt + 1)],
                           start=(m == 0), stop=(m == ND - 1))
                 for m in range(ND)][-1].then_inc(pe.h, 1)))
            sq_scs = []
            for m in range(ND):
                sq_scs.append(sc.bump())
                S(lambda s, m=m, tt=tt, w=gated_dv,
                  pq=prev_nsq[0], dst=(scr1_b, e_sb)[m // 4]:
                  (s.wait_ge(dv.h, w),
                   s.wait_ge(pe.h, pq) if pq else None,
                   s.activation(dst[:, 512 * (m % 4):512 * (m % 4 + 1)],
                                og_b[:, 2048 * m + 512 * tt:
                                     2048 * m + 512 * (tt + 1)],
                                AF.Square).then_inc(sc.h, 1)))
            n_sq = pe.bump()
            TN(lambda tn, tt=tt, sqs=list(sq_scs), pv=prev_stats_dv[0]:
               (tn.wait_ge(dv.h, pv),
                [(tn.wait_ge(sc.h, sqs[m]),
                  tn.matmul(pE2[0:1, 0:512], onesc_b,
                            (scr1_b, e_sb)[m // 4][:,
                                512 * (m % 4):512 * (m % 4 + 1)],
                            start=(m == 0), stop=(m == ND - 1)))
                 for m in range(ND)][-1][1].then_inc(pe.h, 1)))
            prev_nsq[0] = n_sq
            nst = dv.bump()
            V(lambda v, w2=n_sq:
              (v.wait_ge(pe.h, w2),
               v.tensor_scalar(mu_row[0:1, 0:512], pE1[0:1, 0:512],
                               1.0 / D, None, ALU.mult),
               v.tensor_scalar(rr_row[0:1, 0:512], pE2[0:1, 0:512],
                               1.0 / D, None, ALU.mult),
               v.tensor_mul(scrf_f[0:1, 0:512], mu_row[0:1, 0:512],
                            mu_row[0:1, 0:512]),
               v.tensor_sub(rr_row[0:1, 0:512], rr_row[0:1, 0:512],
                            scrf_f[0:1, 0:512]).then_inc(dv.h, 1)))
            prev_stats_dv[0] = nst
            nsd = sc.bump()
            S(lambda s, w=nst:
              (s.wait_ge(dv.h, w),
               s.activation(rr_row[0:1, 0:512], rr_row[0:1, 0:512],
                            AF.Sqrt, bias=eps_s[0:1, 0:1]).then_inc(sc.h, 1)))
            nrc = dv.bump()
            V(lambda v, w=nsd:
              (v.wait_ge(sc.h, w),
               v.reciprocal(rr_row[0:1, 0:512], rr_row[0:1, 0:512]
                            ).then_inc(dv.h, 1)))
            nbc = pe.bump()
            TN(lambda tn, w=nrc:
               (tn.wait_ge(dv.h, w),
                tn.matmul(pA[:, 0:512], onesr_f, mu_row[0:1, 0:512],
                          start=True, stop=True),
                tn.matmul(pB[:, 0:512], onesr_f, rr_row[0:1, 0:512],
                          start=True, stop=True).then_inc(pe.h, 1)))
            for m in range(ND):
                nap = dv.bump()
                V(lambda v, m=m, tt=tt, w=nbc,
                  tn_=og2_tap_n if (m == 0 and tt == 0) else None:
                  (v.wait_ge(pe.h, w),
                   v.wait_ge(dbg_sem[0].h, tn_) if tn_ is not None else None,
                   v.scalar_tensor_tensor(
                       scrf_f[:, 0:512], pA[:, 0:512], -1.0,
                       og_b[:, 2048 * m + 512 * tt:2048 * m + 512 * (tt + 1)],
                       ALU.mult, ALU.add),
                   v.tensor_mul(scrf_f[:, 512:1024], scrf_f[:, 0:512],
                                pB[:, 0:512]),
                   v.tensor_scalar(og_b[:, 2048 * m + 512 * tt:
                                        2048 * m + 512 * (tt + 1)],
                                   scrf_f[:, 512:1024],
                                   lnw_s[:, m:m + 1], lnb_s[:, m:m + 1],
                                   ALU.mult, ALU.add).then_inc(dv.h, 1)))
            prev_stats_dv[0] = dv.n
        y_dv = dv.n
        tap("y", og_b, wait=(lambda q, w=y_dv: q.wait_ge(dv.h, w)))

        plan.mark("p7")
        for tt in range(NT):
            sl = sld[tt % 2]
            sp = ssp[tt % 2]
            sp_guard = sp.n
            x_tgt = sl.bump(16)
            Y(lambda q, tt=tt, yd=y_dv, spg=sp_guard, sl=sl, sp=sp,
              nv=dv.n:
              (q.wait_ge(dv.h, yd) if tt == 0 else None,
               q.wait_ge(dv.h, nv) if tt == 1 else None,
               q.wait_ge(sp.h, spg) if tt >= 2 else None,
               q.dma_start(out=stg_f[tt % 2],
                           in_=x_h[128 * tt:128 * (tt + 1), :]
                           ).then_inc(sl.h, 16)))
            wo_pes = []
            for hh in range(2):
                ps = (pA, pB)[hh]
                wo_pes.append(pe.bump())
                TN(lambda tn, tt=tt, hh=hh, ps=ps, yd=y_dv, dvw=dv.n:
                   (tn.wait_ge(dv.h, yd),
                    tn.wait_ge(dv.h, dvw) if tt > 0 else None,
                    [tn.matmul(ps[:, 0:512],
                               og_b[:, 2048 * m + 128 * tt:
                                    2048 * m + 128 * (tt + 1)],
                               wo_b[:, 1024 * m + 512 * hh:
                                    1024 * m + 512 * (hh + 1)],
                               start=(m == 0), stop=(m == ND - 1))
                     for m in range(ND)][-1].then_inc(pe.h, 1)))
            nev = dv.bump()
            V(lambda v, tt=tt, w=wo_pes, xt=x_tgt, sl=sl:
              (v.wait_ge(pe.h, w[1]),
               v.wait_ge(sl.h, xt),
               v.scalar_tensor_tensor(scrf_f[:, 0:512], pA[:, 0:512], 0.0,
                                      stg_f[tt % 2][:, 0:512],
                                      ALU.bypass, ALU.add),
               v.scalar_tensor_tensor(scrf_f[:, 512:1024], pB[:, 0:512], 0.0,
                                      stg_f[tt % 2][:, 512:1024],
                                      ALU.bypass, ALU.add),
               v.reduce_sum(st1_s[:, 0:1], scrf_f[:, 0:1024], AX),
               v.tensor_mul(scr1_f[:, 0:1024], scrf_f[:, 0:1024],
                            scrf_f[:, 0:1024]),
               v.reduce_sum(st1_s[:, 1:2], scr1_f[:, 0:1024], AX),
               [(v.tensor_scalar(st1_s[:, 2:3], st1_s[:, 0:1], 1.0 / D,
                                 None, ALU.mult),
                 v.tensor_scalar(st1_s[:, 3:4], st1_s[:, 1:2], 1.0 / D,
                                 None, ALU.mult),
                 v.tensor_mul(st1_s[:, 4:5], st1_s[:, 2:3], st1_s[:, 2:3]),
                 v.tensor_sub(st1_s[:, 7:8], st1_s[:, 3:4], st1_s[:, 4:5]))
                for _r in range(2)][-1][-1].then_inc(dv.h, 1)))
            if tt == 0 and _TAP in ("p8s", "p8x"):
                ds = dbg_sem[0]
                ds.bump(16)
                Y(lambda q, w=nev, ds=ds:
                  (q.wait_ge(dv.h, w),
                   q.dma_start(out=dbg_h[:, 0:1024].bitcast(F32),
                               in_=(scrf_f[:, 0:1024] if _TAP == "p8s"
                                    else stg_f[0][:, 0:1024])
                               ).then_inc(ds.h, 16)))
            nsd = sc.bump()
            S(lambda s, w=nev:
              (s.wait_ge(dv.h, w),
               s.activation(st1_s[:, 5:6], st1_s[:, 7:8], AF.Sqrt,
                            bias=eps_s).then_inc(sc.h, 1)))
            nap = dv.bump()
            V(lambda v, tt=tt, w=nsd, dbt=(tt == 0 and _TAP in ("p8s", "p8x")):
              (v.wait_ge(sc.h, w),
               v.wait_ge(dbg_sem[0].h, dbg_sem[0].n) if dbt else None,
               [v.memset(scr1_f[:, 0:1], 0.0) for _p in range(4)][-1],
               v.reciprocal(st1_s[:, 6:7], st1_s[:, 5:6]),
               v.reciprocal(st1_s[:, 6:7], st1_s[:, 5:6]),
               v.memset(scr1_f[:, 0:1], 0.0),
               v.memset(scr1_f[:, 0:1], 0.0),
               v.tensor_scalar(stg_f[tt % 2][:, 0:1024], scrf_f[:, 0:1024],
                               st1_s[:, 2:3], st1_s[:, 6:7],
                               ALU.subtract, ALU.mult),
               v.tensor_reduce(mx_acc[:, tt:tt + 1], stg_f[tt % 2][:, 0:1024],
                               AX, ALU.max, apply_absolute_value=True
                               ).then_inc(dv.h, 1)))
            sp.bump(16)
            S(lambda s, tt=tt, w=nap, sp=sp:
              (s.wait_ge(dv.h, w),
               s.dma_start(out=x1_spill[128 * tt:128 * (tt + 1), :],
                           in_=stg_f[tt % 2]).then_inc(sp.h, 16)))
        x1_dv = dv.n
        x1_sp = (ssp[0].n, ssp[1].n)
        if _TAP == "x1f":
            ds = dbg_sem[0]
            ds.bump(16)
            Y(lambda q, a=ssp[0].n, b=ssp[1].n, ds=ds:
              (q.wait_ge(ssp[0].h, a), q.wait_ge(ssp[1].h, b),
               q.dma_start(
                   out=dbg_h.rearrange("p (a d) -> a p d", d=1024),
                   in_=x1_spill.rearrange("(a p) d -> a p d", p=128),
               ).then_inc(ds.h, 16)))
        tap("x1", None, wait=(lambda q, a=ssp[0].n, b=ssp[1].n:
                              (q.wait_ge(ssp[0].h, a), q.wait_ge(ssp[1].h, b))),
            dram_src=x1_spill[0:128, :])

        plan.mark("p8")
        def barrier(bidx, srcs, lane_base):
            nred = None
            for i, (acc, K) in enumerate(srcs):
                n1b = dv.bump()
                V(lambda v, acc=acc, K=K:
                  (v.tensor_reduce(acc[:, 31:32], acc[:, 0:K], AX, ALU.max,
                                   apply_absolute_value=True),
                   v.tensor_reduce(acc[:, 31:32], acc[:, 0:K], AX, ALU.max,
                                   apply_absolute_value=True
                                   ).then_inc(dv.h, 1)))
                ldg.bump(16)
                G(lambda q, acc=acc, w=n1b, t=ldg.n - 16:
                  (q.wait_ge(dv.h, w),
                   q.wait_ge(ldg.h, t),
                   q.dma_start(out=row_s, in_=acc[:, 31:32]).then_inc(ldg.h, 16)))
                nred = dv.bump()
                V(lambda v, i=i, w=ldg.n:
                  (v.wait_ge(ldg.h, w),
                   v.tensor_reduce(sc_s[0:1, i:i + 1], row_s, AX, ALU.max
                                   ).then_inc(dv.h, 1)))
            ldg.bump(16)
            G(lambda q, b=bidx, w=nred:
              (q.wait_ge(dv.h, w),
               q.dma_start(out=bar_in[b], in_=sc_s[0:1, 0:8]).then_inc(ldg.h, 16)))
            cc.bump()
            G(lambda q, b=bidx, w=ldg.n:
              (q.wait_ge(ldg.h, w),
               q.collective_compute(
                   "AllReduce", ALU.max, replica_groups=[list(range(8))],
                   ins=[bar_in[b].opt()], outs=[bar_out[b].opt()]
               ).then_inc(cc.h, 1)))
            ldg.bump(16)
            G(lambda q, b=bidx, w=cc.n:
              (q.wait_ge(cc.h, w),
               q.dma_start(out=res_s, in_=bar_out[b]).then_inc(ldg.h, 16)))
            nsc_ = dv.bump()
            V(lambda v, ns=len(srcs), w=ldg.n:
              (v.wait_ge(ldg.h, w),
               [v.tensor_scalar(tmp1_s[0:1, 2 * i:2 * i + 1],
                                res_s[0:1, i:i + 1], 1.0 / QMAX, 1e-12,
                                ALU.mult, ALU.max) for i in range(ns)],
               [v.memset(tmp1_s[0:1, 7:8], 0.0) for _p in range(4)],
               [(v.reciprocal(tmp1_s[0:1, 2 * i + 1:2 * i + 2],
                              tmp1_s[0:1, 2 * i:2 * i + 1]),
                 v.reciprocal(tmp1_s[0:1, 2 * i + 1:2 * i + 2],
                              tmp1_s[0:1, 2 * i:2 * i + 1]))
                for i in range(ns)][-1][1].then_inc(dv.h, 1)))
            nbc = pe.bump()
            TN(lambda tn, ns=len(srcs), w=nsc_:
               (tn.wait_ge(dv.h, w),
                tn.matmul(pC[:, 0:2 * ns], onesr_f, tmp1_s[0:1, 0:2 * ns],
                          start=True, stop=True).then_inc(pe.h, 1)))
            nfin = dv.bump()
            V(lambda v, ns=len(srcs), lb=lane_base, w=nbc:
              (v.wait_ge(pe.h, w),
               v.tensor_copy(scl_bc[:, lb:lb + 2 * ns], pC[:, 0:2 * ns]
                             ).then_inc(dv.h, 1)))
            return nfin

        bar1_dv = barrier(0, [(mx_acc, NT)], 0)
        if _TAP == "bar1d":
            ds = dbg_sem[0]
            ds.bump(96)
            Y(lambda q, w=bar1_dv, ds=ds:
              (q.wait_ge(dv.h, w),
               q.dma_start(out=dbg_h[:, 0:32].bitcast(F32),
                           in_=mx_acc[:, 0:32]).then_inc(ds.h, 16),
               q.dma_start(out=dbg_h[0:1, 32:160].bitcast(F32),
                           in_=row_s).then_inc(ds.h, 16),
               q.dma_start(out=dbg_h[0:1, 160:168].bitcast(F32),
                           in_=sc_s).then_inc(ds.h, 16),
               q.dma_start(out=dbg_h[0:1, 192:200].bitcast(F32),
                           in_=res_s).then_inc(ds.h, 16),
               q.dma_start(out=dbg_h[0:1, 224:232].bitcast(F32),
                           in_=tmp1_s).then_inc(ds.h, 16),
               q.dma_start(out=dbg_h[:, 256:268].bitcast(F32),
                           in_=scl_bc[:, 0:12]).then_inc(ds.h, 16)))

        plan.mark("bar1")

        plan.mark("wprep")
        p9_dv0 = dv.n
        for tt in range(NT):
            sx = sxq[tt % 2]
            sq_ = stq[tt % 2]
            tr_guard = sq_.n
            x_tgt = sx.bump(16)
            Y(lambda q, tt=tt, d0=p9_dv0, spx=x1_sp, sx=sx,
              pv=(dv.n - 1 if tt >= 2 else None):
              ((q.wait_ge(ssp[0].h, spx[0]), q.wait_ge(ssp[1].h, spx[1]))
               if tt == 0 else None,
               q.wait_ge(dv.h, d0) if tt < 2 else q.wait_ge(dv.h, pv),
               q.dma_start(out=stg_f[tt % 2],
                           in_=x1_spill[128 * tt:128 * (tt + 1), :]
                           ).then_inc(sx.h, 16)))
            nq = dv.bump()
            V(lambda v, tt=tt, xt=x_tgt, b1=bar1_dv, sx=sx, sq_=sq_,
              tg=tr_guard:
              (v.wait_ge(sx.h, xt),
               v.wait_ge(dv.h, b1) if tt == 0 else None,
               v.wait_ge(sq_.h, tg) if tt >= 2 else None,
               v.tensor_scalar(scrf_i[:, 0:1024], stg_f[tt % 2],
                               scl_bc[:, 1:2], None, ALU.mult),
               v.tensor_scalar(xqt_st[tt % 2], scrf_i[:, 0:1024],
                               scl_bc[:, 0:1], None, ALU.mult).then_inc(dv.h, 1)))
            sq_.bump(16)
            Y(lambda q, tt=tt, w=nq, lp=loop_pe, sq_=sq_:
              (q.wait_ge(dv.h, w),
               q.dma_start_transpose(
                   out=xqT_b[:].rearrange("p (j t) -> p j t", t=2048
                                          )[:, :, 128 * tt:128 * (tt + 1)],
                   in_=xqt_st[tt % 2].rearrange("t (j h) -> t j h", h=128),
               ).then_inc(sq_.h, 16)))
        xq_tr = (stq[0].n, stq[1].n)
        tap("xqT", xqT_b, wait=(lambda q, a=stq[0].n, b=stq[1].n:
                                (q.wait_ge(stq[0].h, a), q.wait_ge(stq[1].h, b))))
        tap("sclx", scl_bc, wait=(lambda q, w=bar1_dv: q.wait_ge(dv.h, w)))

        plan.mark("p9")
        pe_at = pe.n
        sc_at = sc.n
        p9_last = dv.n
        evac_sc0 = sc.n
        prep_prev_dv = [p9_last, p9_last]

        def prep_chunk(ci, src_h, dst_b, tsem, half):
            sl = sld[half]
            tgt = sl.bump(16)
            Y(lambda q, w=prep_prev_dv[half], s=src_h, ci=ci, sl=sl:
              (q.wait_ge(dv.h, w),
               q.dma_start(out=stg_f[half],
                           in_=s[128 * ci:128 * (ci + 1), :]
                           ).then_inc(sl.h, 16)))
            ncast = dv.bump()
            prep_prev_dv[half] = ncast
            wn0 = 2048 * half + 1024 * (ci % 2)
            V(lambda v, ci=ci, tgt=tgt, sl=sl, wn0=wn0, tsem=tsem,
              tg=16 * (ci - 1), first=(ci == 0), half=half:
              (v.wait_ge(sl.h, tgt),
               (v.wait_ge(pe.h, pe_at), v.wait_ge(sc.h, sc_at))
               if first else None,
               v.wait_ge(tsem.h, tg) if ci >= 2 else None,
               v.tensor_reduce(
                   bm_s[:, 0:32],
                   stg_f[half].rearrange("p (b c) -> p b c", c=32),
                   AX, ALU.max, apply_absolute_value=True),
               v.tensor_scalar(bm_s[:, 32:64], bm_s[:, 0:32],
                               1.0 / 7.0, 1e-12, ALU.mult, ALU.max),
               v.tensor_scalar(bm_s[:, 32:64], bm_s[:, 0:32],
                               1.0 / 7.0, 1e-12, ALU.mult, ALU.max),
               [v.memset(bm_s[:, 64:65], 0.0) for _p in range(4)][-1],
               v.reciprocal(br_s[:, 0:32], bm_s[:, 32:64]),
               v.reciprocal(br_s[:, 0:32], bm_s[:, 32:64]),
               v.memset(bm_s[:, 64:65], 0.0),
               v.memset(bm_s[:, 64:65], 0.0),
               v.tensor_tensor(
                   scrf_i[:, 0:1024].rearrange("p (b c) -> p b c", c=32),
                   stg_f[half].rearrange("p (b c) -> p b c", c=32),
                   br_s[:, 0:32].unsqueeze(2).broadcast_to([128, 32, 32]),
                   ALU.mult),
               v.scalar_tensor_tensor(
                   wn6_b[:, wn0:wn0 + 1024].rearrange(
                       "p (b c) -> p b c", c=32),
                   scrf_i[:, 0:1024].rearrange("p (b c) -> p b c", c=32),
                   0.0,
                   bm_s[:, 32:64].unsqueeze(2).broadcast_to([128, 32, 32]),
                   ALU.bypass, ALU.mult).then_inc(dv.h, 1)))
            tsem.bump(16)
            Y(lambda q, ci=ci, w=ncast, d=dst_b, wn0=wn0, tsem=tsem,
              first=(ci == 0):
              (q.wait_ge(dv.h, w),
               (q.wait_ge(pe.h, pe_at), q.wait_ge(sc.h, sc_at))
               if first else None,
               q.dma_start_transpose(
                   out=d[:].rearrange("p (hc i) -> p hc i", i=3072
                                      )[:, :, 128 * ci:128 * (ci + 1)],
                   in_=wn6_b[:, wn0:wn0 + 1024].rearrange(
                       "i (hc h) -> i hc h", h=128),
               ).then_inc(tsem.h, 16)))

        gcnt = [0]

        def f23_m(m, wT_b, acc, stgc, spsem, tsem, spill):
            mgs = []
            mpes = []
            for tt in range(4):
                g = gcnt[0]
                gcnt[0] += 1
                mgs.append(g)
                ps = rot3[g % 3]
                mpes.append(pe.bump())
                TN(lambda tn, m=m, tt=tt, g=g, ps=ps, wT=wT_b, tsem=tsem:
                   (tn.wait_ge(tsem.h, 16 * (m + 1)),
                    (tn.wait_ge(stq[0].h, xq_tr[0]),
                     tn.wait_ge(stq[1].h, xq_tr[1])) if g == 0 else None,
                    tn.wait_ge(sc.h, evac_sc0 + g - 2) if g >= 3 else None,
                    [tn.matmul(ps[:, 0:512],
                               wT[:, 3072 * k + 128 * m:
                                  3072 * k + 128 * (m + 1)],
                               xqT_b[:, 2048 * k + 512 * tt:
                                     2048 * k + 512 * (tt + 1)],
                               start=(k == 0), stop=(k == ND - 1))
                     for k in range(ND)][-1].then_inc(pe.h, 1)))
            spg = spsem.n
            for tt in range(4):
                sc.bump()
                S(lambda s, m=m, tt=tt, g=mgs[tt], w=mpes[tt], stgc=stgc,
                  sp=spsem, spg=spg:
                  (s.wait_ge(pe.h, w),
                   s.wait_ge(sp.h, spg) if (tt == 0 and m >= 1) else None,
                   s.copy(stgc[:, 512 * tt:512 * (tt + 1)],
                          rot3[g % 3][:, 0:512]).then_inc(sc.h, 1)))
            nmx = dv.bump()
            V(lambda v, m=m, w=sc.n, stgc=stgc, acc=acc:
              (v.wait_ge(sc.h, w),
               v.tensor_reduce(acc[:, m % 24:m % 24 + 1], stgc, AX,
                               ALU.max, apply_absolute_value=True
                               ).then_inc(dv.h, 1)))
            spsem.bump(16)
            S(lambda s, m=m, w=nmx, stgc=stgc, spd=spill, sp=spsem:
              (s.wait_ge(dv.h, w),
               s.dma_start(out=spd[128 * m:128 * (m + 1), :], in_=stgc
                           ).then_inc(sp.h, 16)))

        for m in range(NF):
            prep_chunk(m, wgate_h, wgT_b, swg, 0)
            prep_chunk(m, wup_h, wuT_b, swu, 1)
            f23_m(m, wgT_b, gm_acc, scr1_b, ssp[0], swg, g_spill)
            f23_m(m, wuT_b, um_acc, e_sb, ssp[1], swu, u_spill)
        wgT_tr = swg.n
        wuT_tr = swu.n
        fmm_sp = (ssp[0].n, ssp[1].n)
        tap("graw", None, wait=(lambda q, a=ssp[0].n, b=ssp[1].n:
                                (q.wait_ge(ssp[0].h, a), q.wait_ge(ssp[1].h, b))),
            dram_src=g_spill[0:128, :])
        tap("uraw", None, wait=(lambda q, a=ssp[0].n, b=ssp[1].n:
                                (q.wait_ge(ssp[0].h, a), q.wait_ge(ssp[1].h, b))),
            dram_src=u_spill[0:128, :])
        tap("wgT", wgT_b, wait=(lambda q, w=wgT_tr: q.wait_ge(swg.h, w)))

        bar2_dv = barrier(1, [(gm_acc, 24), (um_acc, 24)], 2)

        plan.mark("f23")
        wd_pe0 = pe.n
        wd_prev = [dv.n]

        def wd_item(i):
            hc, th = divmod(i, 3)
            sl = sld[0]
            tgt = sl.bump(16)
            Y(lambda q, i=i, hc=hc, th=th, w=wd_prev[0], fs=fmm_sp, sl=sl:
              ((q.wait_ge(ssp[0].h, fs[0]), q.wait_ge(ssp[1].h, fs[1]))
               if i == 0 else None,
               q.wait_ge(dv.h, w),
               q.dma_start(out=stg_f[1],
                           in_=wdown_h[128 * hc:128 * (hc + 1),
                                       1024 * th:1024 * (th + 1)]
                           ).then_inc(sl.h, 16)))
            ndq = dv.bump()
            wd_prev[0] = ndq
            V(lambda v, i=i, tgt=sl.n, sl=sl, tg=16 * (i - 1):
              (v.wait_ge(sl.h, tgt),
               v.wait_ge(swd.h, tg) if i >= 2 else None,
               v.tensor_reduce(
                   bm_s[:, 0:32],
                   stg_f[1].rearrange("p (b c) -> p b c", c=32),
                   AX, ALU.max, apply_absolute_value=True),
               v.tensor_scalar(bm_s[:, 32:64], bm_s[:, 0:32], 1.0 / 7.0,
                               1e-12, ALU.mult, ALU.max),
               v.tensor_scalar(bm_s[:, 32:64], bm_s[:, 0:32], 1.0 / 7.0,
                               1e-12, ALU.mult, ALU.max),
               [v.memset(bm_s[:, 64:65], 0.0) for _p in range(4)][-1],
               v.reciprocal(br_s[:, 0:32], bm_s[:, 32:64]),
               v.reciprocal(br_s[:, 0:32], bm_s[:, 32:64]),
               v.memset(bm_s[:, 64:65], 0.0),
               v.memset(bm_s[:, 64:65], 0.0),
               v.tensor_tensor(
                   scrf_i[:, 0:1024].rearrange("p (b c) -> p b c", c=32),
                   stg_f[1].rearrange("p (b c) -> p b c", c=32),
                   br_s[:, 0:32].unsqueeze(2).broadcast_to([128, 32, 32]),
                   ALU.mult),
               v.scalar_tensor_tensor(
                   wd_st[i % 2][:].rearrange("p (b c) -> p b c", c=32),
                   scrf_i[:, 0:1024].rearrange("p (b c) -> p b c", c=32),
                   0.0,
                   bm_s[:, 32:64].unsqueeze(2).broadcast_to([128, 32, 32]),
                   ALU.bypass, ALU.mult).then_inc(dv.h, 1)))
            swd.bump(16)
            Y(lambda q, i=i, hc=hc, th=th, w=ndq, pa=wd_pe0:
              (q.wait_ge(dv.h, w),
               q.wait_ge(pe.h, pa) if i == 0 else None,
               q.dma_start_transpose(
                   out=wgT_b[:, 0:24576].rearrange(
                       "p (j F) -> p j F", F=1024
                   )[:, 8 * th:8 * (th + 1), 128 * hc:128 * (hc + 1)],
                   in_=wd_st[i % 2][:].rearrange("h (j i2) -> h j i2",
                                                 i2=128),
               ).then_inc(swd.h, 16)))

        plan.mark("wdprep")
        for m in range(NF):
            wd_item(m)
            stgc = (scr1_b, e_sb)[m % 2]
            sr = srb[m % 2]
            sp = ssp[m % 2]
            wprev = dv.n
            tgt = sr.bump(16)
            Y(lambda q, m=m, b2=bar2_dv, w=wprev, stgc=stgc, sr=sr:
              (q.wait_ge(dv.h, b2) if m < 2 else q.wait_ge(dv.h, w - 1),
               q.dma_start(out=stgc,
                           in_=g_spill[128 * m:128 * (m + 1), :]
                           ).then_inc(sr.h, 16)))
            n1 = dv.bump()
            V(lambda v, m=m, tgt=tgt, stgc=stgc, sr=sr:
              (v.wait_ge(sr.h, tgt),
               [(v.tensor_scalar(scrf_i[:, 0:1024],
                                 stgc[:, 1024 * hf:1024 * (hf + 1)],
                                 scl_bc[:, 3:4], None, ALU.mult),
                 v.tensor_scalar(gq_c[:, 1024 * hf:1024 * (hf + 1)],
                                 scrf_i[:, 0:1024], scl_bc[:, 2:3],
                                 None, ALU.mult)) for hf in range(2)
                ][-1][1].then_inc(dv.h, 1)))
            ns = sc.bump()
            S(lambda s, w=n1:
              (s.wait_ge(dv.h, w),
               s.activation(sig_c, gq_c, AF.Sigmoid).then_inc(sc.h, 1)))
            sp_guard = sp.n
            n2 = dv.bump()
            V(lambda v, m=m, w=ns, sp=sp, spg=sp_guard:
              (v.wait_ge(sc.h, w),
               v.wait_ge(sp.h, spg) if m >= 2 else None,
               [(v.tensor_scalar(scrf_i[:, 0:1024],
                                 sig_c[:, 1024 * hf:1024 * (hf + 1)],
                                 65536.0, None, ALU.mult),
                 v.tensor_scalar(sgq_c[:, 1024 * hf:1024 * (hf + 1)],
                                 scrf_i[:, 0:1024], 65535.0,
                                 1.0 / 65536.0, ALU.min, ALU.mult))
                for hf in range(2)],
               v.tensor_mul(g2st[m % 2], gq_c, sgq_c),
               v.tensor_reduce(gm_acc[:, m % 24:m % 24 + 1],
                               g2st[m % 2], AX,
                               ALU.max, apply_absolute_value=True
                               ).then_inc(dv.h, 1)))
            sp.bump(16)
            S(lambda s, m=m, w=n2, sp=sp:
              (s.wait_ge(dv.h, w),
               s.dma_start(out=g_spill[128 * m:128 * (m + 1), :],
                           in_=g2st[m % 2]).then_inc(sp.h, 16)))
        f4_sp = (ssp[0].n, ssp[1].n)
        tap("g2", None, wait=(lambda q, a=ssp[0].n, b=ssp[1].n:
                              (q.wait_ge(ssp[0].h, a), q.wait_ge(ssp[1].h, b))),
            dram_src=g_spill[0:128, :])

        wdT_tr = swd.n

        bar3_dv = barrier(2, [(gm_acc, 24)], 6)

        plan.mark("f4")
        for m in range(NF):
            stgc = (scr1_b, e_sb)[m % 2]
            sr = srb[m % 2]
            sp = ssp[m % 2]
            wprev = dv.n
            tgt0 = sr.n
            sr.bump(32)
            Y(lambda q, m=m, b3=bar3_dv, w=wprev, stgc=stgc, sr=sr,
              fs=f4_sp:
              ((q.wait_ge(dv.h, b3),
                q.wait_ge(ssp[0].h, fs[0]), q.wait_ge(ssp[1].h, fs[1]))
               if m < 2 else q.wait_ge(dv.h, w - 1),
               q.dma_start(out=stgc,
                           in_=u_spill[128 * m:128 * (m + 1), :]
                           ).then_inc(sr.h, 16),
               q.dma_start(out=g2st[m % 2],
                           in_=g_spill[128 * m:128 * (m + 1), :]
                           ).then_inc(sr.h, 16)))
            sp_guard = sp.n
            n2 = dv.bump()
            V(lambda v, m=m, tgt=sr.n, stgc=stgc, sr=sr, sp=sp, spg=sp_guard:
              (v.wait_ge(sr.h, tgt),
               v.wait_ge(sp.h, spg) if m >= 2 else None,
               [(v.tensor_scalar(scrf_i[:, 0:1024],
                                 stgc[:, 1024 * hf:1024 * (hf + 1)],
                                 scl_bc[:, 5:6], None, ALU.mult),
                 v.tensor_scalar(sgq_c[:, 1024 * hf:1024 * (hf + 1)],
                                 scrf_i[:, 0:1024], scl_bc[:, 4:5],
                                 None, ALU.mult),
                 v.tensor_scalar(scrf_i[:, 0:1024],
                                 g2st[m % 2][:, 1024 * hf:1024 * (hf + 1)],
                                 scl_bc[:, 7:8], None, ALU.mult),
                 v.tensor_scalar(gq_c[:, 1024 * hf:1024 * (hf + 1)],
                                 scrf_i[:, 0:1024], scl_bc[:, 6:7],
                                 None, ALU.mult)) for hf in range(2)],
               v.tensor_mul(oost[m % 2], gq_c, sgq_c),
               v.tensor_reduce(um_acc[:, m % 24:m % 24 + 1],
                               oost[m % 2], AX,
                               ALU.max, apply_absolute_value=True
                               ).then_inc(dv.h, 1)))
            sp.bump(16)
            S(lambda s, m=m, w=n2, sp=sp:
              (s.wait_ge(dv.h, w),
               s.dma_start(out=u_spill[128 * m:128 * (m + 1), :],
                           in_=oost[m % 2]).then_inc(sp.h, 16)))
        f5_sp = (ssp[0].n, ssp[1].n)
        tap("o", None, wait=(lambda q, a=ssp[0].n, b=ssp[1].n:
                             (q.wait_ge(ssp[0].h, a), q.wait_ge(ssp[1].h, b))),
            dram_src=u_spill[0:128, :])

        bar4_dv = barrier(3, [(um_acc, 24)], 8)

        plan.mark("f5")
        f6_quant = []
        f6_out = []
        for tt in range(NT):
            so_ = sot[tt % 2]
            sx = sxq[tt % 2]
            su = sou[tt % 2]
            ot_tgt = so_.bump(16)
            Y(lambda q, tt=tt, b4=bar4_dv, fs=f5_sp, so_=so_,
              pv=(f6_quant[tt - 2] if tt >= 2 else None):
              ((q.wait_ge(ssp[0].h, fs[0]), q.wait_ge(ssp[1].h, fs[1]),
                q.wait_ge(dv.h, b4)) if tt == 0 else None,
               q.wait_ge(dv.h, pv) if pv is not None else None,
               q.dma_start(
                   out=o_tile[tt % 2][:].rearrange("p (ci f) -> p ci f", f=128),
                   in_=u_spill.rearrange("(ci p) t -> p ci t", p=128
                                         )[:, :, 128 * tt:128 * (tt + 1)],
               ).then_inc(so_.h, 16)))
            out_guard = su.n
            x_tgt = sx.bump(16)
            Y(lambda q, tt=tt, su=su, sx=sx, og=out_guard:
              (q.wait_ge(su.h, og) if tt >= 2 else None,
               q.dma_start(out=stg_f[tt % 2],
                           in_=x1_spill[128 * tt:128 * (tt + 1), :]
                           ).then_inc(sx.h, 16)))
            nq = dv.bump()
            f6_quant.append(nq)
            V(lambda v, tt=tt, ot=ot_tgt, b4=bar4_dv, so_=so_:
              (v.wait_ge(so_.h, ot),
               v.wait_ge(dv.h, b4) if tt == 0 else None,
               [(v.tensor_scalar(scrf_i[:, 0:1024],
                                 o_tile[tt % 2][:, 1024 * s3:1024 * (s3 + 1)],
                                 scl_bc[:, 9:10], None, ALU.mult),
                 v.tensor_scalar(o_tile[tt % 2][:, 1024 * s3:1024 * (s3 + 1)],
                                 scrf_i[:, 0:1024], scl_bc[:, 8:9],
                                 None, ALU.mult)) for s3 in range(3)
                ][-1][1].then_inc(dv.h, 1)))
            dn_pes = []
            for hh in range(2):
                ps = (pA, pB)[hh]
                dn_pes.append(pe.bump())
                TN(lambda tn, tt=tt, hh=hh, ps=ps, w=nq, dvw=dv.n:
                   (tn.wait_ge(dv.h, w),
                    tn.wait_ge(swd.h, wdT_tr) if tt == 0 else None,
                    tn.wait_ge(dv.h, dvw) if tt > 0 else None,
                    [tn.matmul(ps[:, 0:512],
                               o_tile[tt % 2][:, 128 * ci:128 * (ci + 1)],
                               wgT_b[:, 1024 * ci + 512 * hh:
                                     1024 * ci + 512 * (hh + 1)],
                               start=(ci == 0), stop=(ci == NF - 1))
                     for ci in range(NF)][-1].then_inc(pe.h, 1)))
            nev = dv.bump()
            V(lambda v, tt=tt, w=dn_pes, xt=x_tgt, sx=sx:
              (v.wait_ge(pe.h, w[1]),
               v.wait_ge(sx.h, xt),
               v.scalar_tensor_tensor(scrf_f[:, 0:512], pA[:, 0:512], 0.0,
                                      stg_f[tt % 2][:, 0:512],
                                      ALU.bypass, ALU.add),
               v.scalar_tensor_tensor(scrf_f[:, 512:1024], pB[:, 0:512], 0.0,
                                      stg_f[tt % 2][:, 512:1024],
                                      ALU.bypass, ALU.add),
               v.reduce_sum(st1_s[:, 0:1], scrf_f[:, 0:1024], AX),
               v.tensor_mul(scr1_f[:, 0:1024], scrf_f[:, 0:1024],
                            scrf_f[:, 0:1024]),
               v.reduce_sum(st1_s[:, 1:2], scr1_f[:, 0:1024], AX),
               [(v.tensor_scalar(st1_s[:, 2:3], st1_s[:, 0:1], 1.0 / D,
                                 None, ALU.mult),
                 v.tensor_scalar(st1_s[:, 3:4], st1_s[:, 1:2], 1.0 / D,
                                 None, ALU.mult),
                 v.tensor_mul(st1_s[:, 4:5], st1_s[:, 2:3], st1_s[:, 2:3]),
                 v.tensor_sub(st1_s[:, 7:8], st1_s[:, 3:4], st1_s[:, 4:5]))
                for _r in range(2)][-1][-1].then_inc(dv.h, 1)))
            nsd = sc.bump()
            S(lambda s, w=nev:
              (s.wait_ge(dv.h, w),
               s.activation(st1_s[:, 5:6], st1_s[:, 7:8], AF.Sqrt,
                            bias=eps_s).then_inc(sc.h, 1)))
            nap = dv.bump()
            V(lambda v, tt=tt, w=nsd:
              (v.wait_ge(sc.h, w),
               [v.memset(scr1_f[:, 0:1], 0.0) for _p in range(4)][-1],
               v.reciprocal(st1_s[:, 6:7], st1_s[:, 5:6]),
               v.reciprocal(st1_s[:, 6:7], st1_s[:, 5:6]),
               v.memset(scr1_f[:, 0:1], 0.0),
               v.memset(scr1_f[:, 0:1], 0.0),
               v.tensor_scalar(stg_f[tt % 2][:, 0:1024], scrf_f[:, 0:1024],
                               st1_s[:, 2:3], st1_s[:, 6:7],
                               ALU.subtract, ALU.mult).then_inc(dv.h, 1)))
            su.bump(16)
            Y(lambda q, tt=tt, w=nap, su=su:
              (q.wait_ge(dv.h, w),
               q.dma_start(out=out_h[128 * tt:128 * (tt + 1), :],
                           in_=stg_f[tt % 2]).then_inc(su.h, 16)))
            f6_out.append(su.n)

        Y(lambda q, a=sou[0].n, b=sou[1].n:
          (q.wait_ge(sou[0].h, a), q.wait_ge(sou[1].h, b),
           q.wait_ge(dbg_sem[0].h, dbg_sem[0].n) if _TAP is not None else None))

        globals()["_DBGS"] = dict(
            sld0=(sld[0].h, sld[0].n), sld1=(sld[1].h, sld[1].n),
            ld=(ld.h, ld.n), strx=(strx.h, strx.n), skt=(skt.h, skt.n),
            stq0=(stq[0].h, stq[0].n), stq1=(stq[1].h, stq[1].n),
            swg=(swg.h, swg.n), swu=(swu.h, swu.n), swd=(swd.h, swd.n),
            srb0=(srb[0].h, srb[0].n), srb1=(srb[1].h, srb[1].n),
            sxq0=(sxq[0].h, sxq[0].n), sxq1=(sxq[1].h, sxq[1].n),
            ssp0=(ssp[0].h, ssp[0].n), ssp1=(ssp[1].h, ssp[1].n),
            pe=(pe.h, pe.n), dv=(dv.h, dv.n), sc=(sc.h, sc.n), cc=(cc.h, cc.n),
            ldg=(ldg.h, ldg.n),
        )
        if _STOP is not None:
            cut = plan.marks[_STOP]
            for e in ENGS:
                plan.ops[e] = plan.ops[e][:cut[e]]

        def _phase_bounds(e):
            return sorted((v[e], k) for k, v in plan.marks.items())

        _esuf = {"sync": "y", "tensor": "t", "vector": "v",
                 "scalar": "c", "gpsimd": "g"}

        def _emit(eng_dev, e):
            bounds = _phase_bounds(e)
            bi = 0
            for i, fn in enumerate(plan.ops[e]):
                while bi < len(bounds) and i >= bounds[bi][0]:
                    bi += 1
                ph = bounds[bi][1] if bi < len(bounds) else "tail"
                with nc.named_scope(ph + "." + _esuf[e]):
                    fn(eng_dev)

        block = est.enter_context(nc.Block())

        @block.sync
        def _(q):
            _emit(q, "sync")

        @block.tensor
        def _(tn):
            _emit(tn, "tensor")

        @block.vector
        def _(v):
            _emit(v, "vector")

        @block.scalar
        def _(s):
            _emit(s, "scalar")

        @block.gpsimd
        def _(q):
            _emit(q, "gpsimd")

    return nc
def _tap_out(dbg_h, s, d):
    ap = d if d is not None else s
    if len(ap.shape) == 2 and ap.shape[0] <= 128:
        rows, cols = ap.shape
        eltb = {mybir.dt.float32: 4, mybir.dt.bfloat16: 2,
                mybir.dt.int32: 4}[ap.dtype]
        ncols = cols * eltb // 4
        return dbg_h[0:rows, 0:ncols].bitcast(ap.dtype)
    raise ValueError(ap.shape)


_NC = None


def _constants(core):
    b, h = divmod(core, 2)
    pos = np.arange(2048 * h, 2048 * (h + 1), dtype=np.float64)
    inv = 1.0 / (10000.0 ** (np.arange(0, 64, 2, dtype=np.float64) / 64.0))
    ang = pos[:, None] * inv[None, :]
    cosv = np.cos(ang).astype(np.float32)
    sinv = np.sin(ang).astype(np.float32)
    rows = np.arange(128) % 64 % 32
    cos_b = np.ascontiguousarray(cosv[:, rows].T).astype(ml_dtypes.bfloat16)
    sin_b = np.ascontiguousarray(sinv[:, rows].T).astype(ml_dtypes.bfloat16)
    kt = np.arange(128)[:, None]
    qt = np.arange(128)[None, :]
    maskt = (kt <= qt).astype(np.float32).astype(ml_dtypes.bfloat16)
    p2l = np.zeros((128, 128), np.float32)
    for base in (0, 64):
        for i in range(32):
            p2l[base + i + 32, base + i] = -1.0
            p2l[base + i, base + i + 32] = 1.0
    p2l = p2l.astype(ml_dtypes.bfloat16)
    onesc = np.ones((128, 1), ml_dtypes.bfloat16)
    onesr = np.ones((1, 128), np.float32)
    eo = np.zeros((128, 2), np.float32)
    eo[:, 0] = 1.0 if h == 0 else 0.0
    eo[:, 1] = 1.0 - eo[:, 0]
    return dict(cosb=cos_b, sinb=sin_b, maskt=maskt, p2l=p2l, onesc=onesc,
                onesr=onesr, eo=eo)


def kernel(L0_attn_x, L0_attn_Wq, L0_attn_Wk, L0_attn_Wv, L0_attn_Wo,
           L0_attn_Wg1, L0_attn_Wg2, L0_attn_ln_w, L0_attn_ln_b,
           L0_ffn_w_gate, L0_ffn_w_up, L0_ffn_w_down):
    global _NC
    if _NC is None:
        _NC = build_nc()
    x = np.ascontiguousarray(np.asarray(L0_attn_x, np.float32))
    lnw = np.ascontiguousarray(
        np.asarray(L0_attn_ln_w, np.float32).reshape(8, 128).T)
    lnb = np.ascontiguousarray(
        np.asarray(L0_attn_ln_b, np.float32).reshape(8, 128).T)
    base = dict(
        wq=np.ascontiguousarray(L0_attn_Wq, dtype=np.float32),
        wk=np.ascontiguousarray(L0_attn_Wk, dtype=np.float32),
        wv=np.ascontiguousarray(L0_attn_Wv, dtype=np.float32),
        wo=np.ascontiguousarray(L0_attn_Wo, dtype=np.float32),
        wg1=np.ascontiguousarray(L0_attn_Wg1, dtype=np.float32),
        wg2=np.ascontiguousarray(L0_attn_Wg2, dtype=np.float32),
        lnw=lnw, lnb=lnb,
        wgate=np.ascontiguousarray(L0_ffn_w_gate, dtype=np.float32),
        wup=np.ascontiguousarray(L0_ffn_w_up, dtype=np.float32),
        wdown=np.ascontiguousarray(L0_ffn_w_down, dtype=np.float32),
    )
    in_maps = []
    for c in range(8):
        b, h = divmod(c, 2)
        m = dict(base)
        m["x"] = np.ascontiguousarray(x[b, 2048 * h:2048 * (h + 1), :])
        m.update(_constants(c))
        in_maps.append(m)
    res = run_bass_kernel_spmd(_NC, in_maps, core_ids=list(range(8)))
    out = np.empty((4, 4096, 1024), np.float32)
    for c in range(8):
        b, h = divmod(c, 2)
        out[b, 2048 * h:2048 * (h + 1), :] = res.results[c]["out"]
    return out


if __name__ == "__main__":
    build_nc()
    print("build ok")


# version 10
# speedup vs baseline: 1.0982x; 1.0003x over previous
import numpy as np
import ml_dtypes
from contextlib import ExitStack

import concourse.bass as bass
import concourse.mybir as mybir
from concourse.bass_utils import run_bass_kernel_spmd

F32 = mybir.dt.float32
BF16 = mybir.dt.bfloat16
I32 = mybir.dt.int32
U8 = mybir.dt.uint8
ALU = mybir.AluOpType
AF = mybir.ActivationFunctionType
AX = mybir.AxisListType.X

T, NT, D, ND = 2048, 16, 1024, 8
NH, NHP, HD = 16, 8, 64
FF, NF = 3072, 24
C, NCH, SCT, NSC = 128, 16, 512, 4
QMAX = 32767.0
EPS = 1e-5
ARENA_B = 211456

ENGS = ("sync", "tensor", "vector", "scalar", "gpsimd")

_TAP = None
_STOP = None



class Sem:
    def __init__(self, st, nc, name):
        self.h = st.enter_context(nc.semaphore(name))
        self.n = 0

    def bump(self, k=1):
        self.n += k
        return self.n


class Plan:
    def __init__(self):
        self.ops = {e: [] for e in ENGS}
        self.marks = {}

    def mark(self, name):
        self.marks[name] = {e: len(self.ops[e]) for e in ENGS}


def build_nc():
    nc = bass.Bass("TRN2", target_bir_lowering=False)

    def inp(name, shape, dt=F32):
        return nc.dram_tensor(name, shape, dt, kind="ExternalInput").ap()

    x_h = inp("x", [T, D])
    wq_h, wk_h, wv_h, wo_h = (inp(n, [D, D]) for n in ("wq", "wk", "wv", "wo"))
    wg1_h, wg2_h = inp("wg1", [D, HD]), inp("wg2", [HD, D])
    lnw_h, lnb_h = inp("lnw", [128, ND]), inp("lnb", [128, ND])
    wgate_h, wup_h = inp("wgate", [FF, D]), inp("wup", [FF, D])
    wdown_h = inp("wdown", [D, FF])
    cos_h, sin_h = inp("cosb", [128, T], BF16), inp("sinb", [128, T], BF16)
    maskT_h = inp("maskt", [128, 128], BF16)
    p2l_h = inp("p2l", [128, 128], BF16)
    onesc_h = inp("onesc", [128, 1], BF16)
    onesr_h = inp("onesr", [1, 128])
    eo_h = inp("eo", [128, 2])
    out_h = nc.dram_tensor("out", [T, D], F32, kind="ExternalOutput").ap()

    st_in = nc.dram_tensor("st_in", [128, 512], F32).ap()
    st_out = nc.dram_tensor("st_out", [128, 512], F32).ap()
    bar_in = [nc.dram_tensor(f"bar{i}_in", [1, 8], F32).ap() for i in range(4)]
    bar_out = [nc.dram_tensor(f"bar{i}_out", [1, 8], F32).ap() for i in range(4)]
    g_spill = nc.dram_tensor("g_spill", [FF, T], BF16).ap()
    u_spill = nc.dram_tensor("u_spill", [FF, T], BF16).ap()
    x1_spill = nc.dram_tensor("x1_spill", [T, D], F32).ap()

    with ExitStack() as est:
        arena = est.enter_context(nc.sbuf_tensor("arena", [128, ARENA_B], U8))
        cur = [0]

        def carve(nbytes, align=64):
            off = (cur[0] + align - 1) // align * align
            cur[0] = off + nbytes
            assert cur[0] <= ARENA_B, f"arena overflow {cur[0]}"
            return off

        def vvw(off, nbytes, dt, rows=128, r0=0):
            return arena[r0:r0 + rows, off:off + nbytes].bitcast(dt)

        cos_b = vvw(carve(4096), 4096, BF16)
        sin_b = vvw(carve(4096), 4096, BF16)
        maskT_b = vvw(carve(256), 256, BF16)
        p2l_b = vvw(carve(256), 256, BF16)
        onesc_b = vvw(carve(64), 2, BF16)
        onesr_f = vvw(carve(512), 512, F32, rows=1)
        lnw_s = vvw(carve(64), 32, F32)
        lnb_s = vvw(carve(64), 32, F32)
        eo_s = vvw(carve(64), 8, F32)
        wg1_b = vvw(carve(1024), 1024, BF16)
        o_g1 = carve(4096)
        g1_b = vvw(o_g1, 4096, BF16, rows=64)
        mx_acc = vvw(carve(128), 128, F32)
        gm_acc = vvw(carve(128), 128, F32)
        um_acc = vvw(carve(128), 128, F32)
        row_s = vvw(carve(512), 512, F32, rows=1)
        sc_s = vvw(carve(64), 32, F32, rows=1)
        res_s = vvw(carve(64), 32, F32, rows=1)
        scl_bc = vvw(carve(64), 48, F32)
        st1_s = vvw(carve(64), 32, F32)
        o_mu = carve(2048)
        mu_row = vvw(o_mu, 2048, F32, rows=1)
        o_rr = carve(2048)
        rr_row = vvw(o_rr, 2048, F32, rows=1)
        tmp1_s = vvw(carve(64), 32, F32, rows=1)
        eps_s = vvw(carve(64), 4, F32)
        kf_j = vvw(carve(2048), 2048, BF16)
        v_j = vvw(carve(2048), 2048, BF16)
        o_scr1 = carve(4096)
        scr1_b = vvw(o_scr1, 4096, BF16)
        scr1_f = vvw(o_scr1, 4096, F32)
        kt_j = vvw(o_scr1 + 2048, 2048, BF16)
        e_sb = vvw(carve(4096), 4096, BF16)
        o_sz = carve(2048)
        s_z = vvw(o_sz, 2048, BF16)
        s_iz = s_z
        o_scrf = carve(4096)
        scrf_f = vvw(o_scrf, 4096, F32)
        scrf_i = vvw(o_scrf, 4096, I32)
        s_fin = vvw(o_scrf + 2048, 2048, F32)
        o_stg = carve(8192)
        stg_f = [vvw(o_stg, 4096, F32), vvw(o_stg + 4096, 4096, F32)]
        kf_z = [vvw(o_stg, 2048, BF16), vvw(o_stg + 2048, 2048, BF16)]
        wd_st = None
        wd_st = [vvw(o_mu, 2048, BF16), vvw(o_rr, 2048, BF16)]
        bm_s = vvw(carve(512), 384, F32)
        br_s = vvw(carve(512), 384, F32)
        globals()["_DBG"] = dict(scrf=o_scrf, stg=o_stg)

        o_biga = carve(32768)
        o_bigb = carve(32768)
        o_bige = carve(32768)
        o_wk = carve(16384)
        o_wv = carve(16384)
        o_wo = carve(16384)
        o_wcur = carve(16384)
        assert o_wk == o_bige + 32768 and o_wo == o_wv + 16384
        assert o_wcur == o_wo + 16384 and o_bigb == o_biga + 32768

        xT_b = vvw(o_biga, 32768, BF16)
        xqT_b = xT_b
        q_b = vvw(o_bigb, 32768, BF16)
        x_tmb = vvw(o_bige, 32768, BF16)
        og_b = vvw(o_bige, 32768, BF16)
        wk_b = vvw(o_wk, 16384, BF16)
        wv_b = vvw(o_wv, 16384, BF16)
        wo_b = vvw(o_wo, 16384, BF16)
        wq_b = vvw(o_wcur, 16384, BF16)
        wg2_b = vvw(o_wcur, 2048, BF16, rows=64)
        wgT_b = vvw(o_bige, 49152, BF16)
        wuT_b = vvw(o_wv, 49152, BF16)
        g2st = [vvw(o_bigb, 4096, BF16), vvw(o_bigb + 4096, 4096, BF16)]
        oost = [vvw(o_bigb + 8192, 4096, BF16),
                vvw(o_bigb + 12288, 4096, BF16)]
        o_tile = [vvw(o_biga, 6144, BF16), vvw(o_biga + 6144, 6144, BF16)]
        gq_c = vvw(o_bigb + 16384, 4096, BF16)
        sig_c = vvw(o_bigb + 16384 + 4096, 4096, BF16)
        sgq_c = vvw(o_bigb + 16384 + 8192, 4096, BF16)
        xqt_st = [vvw(o_bigb + 16384 + 12288, 2048, BF16),
                  vvw(o_bigb + 16384 + 14336, 2048, BF16)]
        wn6_b = vvw(o_bigb + 16384, 12288, BF16)

        pA = est.enter_context(nc.psum_tensor("pA", [128, 512], F32))
        pB = est.enter_context(nc.psum_tensor("pB", [128, 512], F32))
        pC = est.enter_context(nc.psum_tensor("pC", [128, 512], F32))
        pE1 = est.enter_context(nc.psum_tensor("pE1", [128, 512], F32))
        pE2 = est.enter_context(nc.psum_tensor("pE2", [128, 512], F32))
        pO = est.enter_context(nc.psum_tensor("pO", [128, 1024], F32))
        pS = est.enter_context(nc.psum_tensor("pS", [128, 512], F32))
        rot3 = (pA, pB, pC)

        ld = Sem(est, nc, "ld")
        sld = [Sem(est, nc, "sld0"), Sem(est, nc, "sld1")]
        strx = Sem(est, nc, "strx")
        skt = Sem(est, nc, "skt")
        stq = [Sem(est, nc, "stq0"), Sem(est, nc, "stq1")]
        swg = Sem(est, nc, "swg")
        swu = Sem(est, nc, "swu")
        swd = Sem(est, nc, "swd")
        sot = [Sem(est, nc, "sot0"), Sem(est, nc, "sot1")]
        srb = [Sem(est, nc, "srb0"), Sem(est, nc, "srb1")]
        sxq = [Sem(est, nc, "sxq0"), Sem(est, nc, "sxq1")]
        ssp = [Sem(est, nc, "ssp0"), Sem(est, nc, "ssp1")]
        sou = [Sem(est, nc, "sou0"), Sem(est, nc, "sou1")]
        ldg = Sem(est, nc, "ldg")
        pe = Sem(est, nc, "pe")
        dv = Sem(est, nc, "dv")
        sc = Sem(est, nc, "sc")
        cc = Sem(est, nc, "cc")

        plan = Plan()

        dbg_h = None
        dbg_sem = [None]
        if _TAP is not None:
            dbg_h = nc.dram_tensor("dbg", [128, 16384], F32,
                                   kind="ExternalOutput").ap()
            dbg_sem[0] = Sem(est, nc, "dbgsem")

        def Y(fn): plan.ops["sync"].append(fn)
        def TN(fn): plan.ops["tensor"].append(fn)

        def tap(name, src_ap, wait=None, dram_src=None):
            if _TAP != name:
                return
            ds = dbg_sem[0]
            ds.bump(16)
            Y(lambda q, s=src_ap, w=wait, d=dram_src:
              (w(q) if w else None,
               q.dma_start(out=dbg_h[:, 0:(d if d else s).free_size()
                               ] if False else _tap_out(dbg_h, s, d),
                           in_=(d if d is not None else s)).then_inc(ds.h, 16)))

        def V(fn): plan.ops["vector"].append(fn)
        def S(fn): plan.ops["scalar"].append(fn)
        def G(fn): plan.ops["gpsimd"].append(fn)

        dv.bump()
        V(lambda v: (v.memset(eps_s, EPS),
                     v.memset(sc_s, 0.0),
                     v.memset(res_s, 0.0),
                     v.memset(tmp1_s, 0.0),
                     v.memset(mx_acc, 0.0),
                     v.memset(gm_acc, 0.0),
                     v.memset(um_acc, 0.0),
                     v.memset(scl_bc, 0.0).then_inc(dv.h, 1)))

        for dst, src in ((cos_b, cos_h), (sin_b, sin_h), (maskT_b, maskT_h),
                         (p2l_b, p2l_h), (onesc_b, onesc_h), (onesr_f, onesr_h),
                         (lnw_s, lnw_h), (lnb_s, lnb_h), (eo_s, eo_h)):
            ld.bump(16)
            Y(lambda q, d=dst, s=src: q.dma_start(out=d, in_=s).then_inc(ld.h, 16))
        consts_ld = ld.n

        def load_cast(dst_b, src_h, nchunks, width, wslot):
            for k in range(nchunks):
                sl = sld[k % 2]
                wprev = dv.n - 1
                tgt = sl.bump(16)
                Y(lambda q, k=k, w=wprev, s=src_h, wd=width, sl=sl:
                  (q.wait_ge(dv.h, max(0, w)),
                   q.dma_start(out=stg_f[k % 2][:, :wd],
                               in_=s[128 * k:128 * (k + 1), :]
                               ).then_inc(sl.h, 16)))
                dv.bump()
                V(lambda v, k=k, tgt=tgt, d=dst_b, wd=width, ws=wslot, sl=sl:
                  (v.wait_ge(sl.h, tgt),
                   v.tensor_copy(d[:, ws * k:ws * k + wd],
                                 stg_f[k % 2][:, :wd]).then_inc(dv.h, 1)))

        load_cast(x_tmb, x_h, NT, 1024, 1024)
        x_cast_dv = dv.n

        for j in range(NT):
            strx.bump(16)
            Y(lambda q, j=j, w=x_cast_dv:
              (q.wait_ge(dv.h, w),
               q.dma_start_transpose(
                   out=xT_b[:].rearrange("p (k t) -> p k t", t=2048
                                         )[:, :, 128 * j:128 * (j + 1)],
                   in_=x_tmb[:, 1024 * j:1024 * (j + 1)].rearrange(
                       "t (k h) -> t k h", h=128),
               ).then_inc(strx.h, 16)))
        xT_tr = strx.n
        tap("xT", xT_b, wait=(lambda q: q.wait_ge(strx.h, xT_tr)))

        load_cast(wq_b, wq_h, ND, 1024, 1024)
        load_cast(wk_b, wk_h, ND, 1024, 1024)
        load_cast(wv_b, wv_h, ND, 1024, 1024)
        load_cast(wo_b, wo_h, ND, 1024, 1024)
        load_cast(wg1_b, wg1_h, ND, 64, 64)
        allw_dv = dv.n
        tap("wq", wq_b, wait=(lambda q, w=allw_dv: q.wait_ge(dv.h, w)))

        plan.mark("p0")
        def mm_group(ps_ap, pairs, pre=None):
            n = pe.bump()
            def f(tn, P=list(pairs), pre=pre, ps=ps_ap):
                if pre:
                    pre(tn)
                for i, (l, r) in enumerate(P):
                    ins = tn.matmul(ps, l, r,
                                    start=(i == 0), stop=(i == len(P) - 1))
                ins.then_inc(pe.h, 1)
            TN(f)
            return n

        g1_sc0 = sc.n
        g1_pes = []
        for tt in range(4):
            g1_pes.append(mm_group(
                rot3[tt % 3][0:64, 0:512],
                [(wg1_b[:, 64 * k:64 * (k + 1)],
                  xT_b[:, 2048 * k + 512 * tt:2048 * k + 512 * (tt + 1)])
                 for k in range(ND)],
                pre=(lambda tn, t=tt:
                     (tn.wait_ge(strx.h, xT_tr), tn.wait_ge(dv.h, allw_dv),
                      tn.wait_ge(ld.h, consts_ld),
                      tn.wait_ge(sc.h, g1_sc0 + t - 2) if t >= 3 else None)))
            )
        for tt in range(4):
            sc.bump()
            S(lambda s, t=tt, w=g1_pes[tt]:
              (s.wait_ge(pe.h, w),
               s.copy(g1_b[0:64, 512 * t:512 * (t + 1)],
                      rot3[t % 3][0:64, 0:512]).then_inc(sc.h, 1)))
        g1_sc = sc.n

        q_sc0 = sc.n
        for i in range(32):
            m, tt = divmod(i, 4)
            mm_group(
                rot3[i % 3][:, 0:512],
                [(wq_b[:, 1024 * k + 128 * m:1024 * k + 128 * (m + 1)],
                  xT_b[:, 2048 * k + 512 * tt:2048 * k + 512 * (tt + 1)])
                 for k in range(ND)],
                pre=(lambda tn, i=i:
                     (tn.wait_ge(sc.h, g1_sc),
                      tn.wait_ge(sc.h, q_sc0 + i - 2) if i >= 3 else None)))
            sc.bump()
            S(lambda s, i=i, m=m, tt=tt, w=pe.n:
              (s.wait_ge(pe.h, w),
               s.activation(q_b[:, 2048 * m + 512 * tt:2048 * m + 512 * (tt + 1)],
                            rot3[i % 3][:, 0:512], AF.Silu).then_inc(sc.h, 1)))
        q_sc = sc.n
        tap("qsilu", q_b, wait=(lambda q, w=q_sc: q.wait_ge(sc.h, w)))

        q_pe_done = pe.n
        sl = sld[0]
        wprev = dv.n - 1
        tgt = sl.bump(16)
        Y(lambda q, w=wprev, sl=sl, qp=q_pe_done:
          (q.wait_ge(dv.h, max(0, w)),
           q.wait_ge(pe.h, qp),
           q.dma_start(out=stg_f[0][0:64, :], in_=wg2_h).then_inc(sl.h, 16)))
        dv.bump()
        V(lambda v, tgt=tgt, sl=sl:
          (v.wait_ge(sl.h, tgt),
           v.tensor_copy(wg2_b[0:64, :], stg_f[0][0:64, :]).then_inc(dv.h, 1)))
        wg2_dv = dv.n

        def rope(buf, m_off, tok0, ntok, v_pre=None):
            u = scr1_b[:, 0:ntok]
            cw = cos_b[:, tok0:tok0 + ntok]
            sw = sin_b[:, tok0:tok0 + ntok]
            pe_at = pe.n
            n1 = dv.bump()
            V(lambda v, vp=v_pre, pa=pe_at:
              (vp(v) if vp else None,
               v.wait_ge(pe.h, pa),
               v.tensor_mul(u, buf[:, m_off:m_off + ntok], sw),
               v.tensor_mul(buf[:, m_off:m_off + ntok],
                            buf[:, m_off:m_off + ntok], cw).then_inc(dv.h, 1)))
            tap("u%d" % (m_off // 2048), u,
                wait=(lambda q, w=n1: q.wait_ge(dv.h, w)))
            nj = (ntok + 511) // 512
            add_dvs = []
            for j in range(nj):
                wdt = min(512, ntok - 512 * j)
                ps = (pA, pB)[j % 2]
                pn = pe.bump()
                TN(lambda tn, j=j, ps=ps, wdt=wdt, n1=n1,
                   ag=(add_dvs[j - 2] if j >= 2 else None):
                   (tn.wait_ge(dv.h, n1),
                    tn.wait_ge(dv.h, ag) if ag is not None else None,
                    tn.matmul(ps[:, 0:wdt], p2l_b, u[:, 512 * j:512 * j + wdt],
                              start=True, stop=True).then_inc(pe.h, 1)))
                add_dvs.append(dv.bump())
                V(lambda v, j=j, ps=ps, wdt=wdt, pn=pn, mo=m_off:
                  (v.wait_ge(pe.h, pn),
                   v.tensor_add(buf[:, mo + 512 * j:mo + 512 * j + wdt],
                                buf[:, mo + 512 * j:mo + 512 * j + wdt],
                                ps[:, 0:wdt]).then_inc(dv.h, 1)))

        for m in range(ND):
            rope(q_b, 2048 * m, 0, 2048,
                 v_pre=(lambda v, m=m, s0=q_sc - 32:
                        v.wait_ge(sc.h, s0 + 4 * (m + 1))))
        qrope_dv = dv.n
        tap("q", q_b, wait=(lambda q, w=qrope_dv: q.wait_ge(dv.h, w)))

        plan.mark("p1")
        dv.bump()
        V(lambda v:
          (v.memset(kf_z[0][64:128, :], 0.0),
           v.memset(kf_z[1][0:64, :], 0.0),
           v.memset(s_z[:].rearrange("p (hp c) -> p hp c", c=128)
                    [64:128, :, 0:64], 0.0),
           v.memset(s_z[:].rearrange("p (hp c) -> p hp c", c=128)
                    [0:64, :, 64:128], 0.0).then_inc(dv.h, 1)))
        og_sc_prev = [sc.n]
        for ci in range(NCH):
            tok0 = C * ci
            ksc0 = sc.n
            kq_dv = dv.n
            for bank in range(2):
                ps = (pA, pB)[bank]
                n = pe.bump()
                def kproj(tn, bank=bank, ps=ps, kq=kq_dv, tok0=tok0):
                    if bank == 0:
                        tn.wait_ge(dv.h, kq)
                    first = True
                    for mm in range(4):
                        m = 4 * bank + mm
                        for k in range(ND):
                            ins = tn.matmul(
                                ps[:, 128 * mm:128 * (mm + 1)],
                                wk_b[:, 1024 * k + 128 * m:
                                     1024 * k + 128 * (m + 1)],
                                xT_b[:, 2048 * k + tok0:2048 * k + tok0 + 128],
                                start=first,
                                stop=(mm == 3 and k == ND - 1))
                            first = False
                    ins.then_inc(pe.h, 1)
                TN(kproj)
                sc.bump()
                S(lambda s, bank=bank, w=pe.n, ps=ps:
                  (s.wait_ge(pe.h, w),
                   s.activation(kf_j[:, 512 * bank:512 * (bank + 1)],
                                ps[:, 0:512], AF.Silu).then_inc(sc.h, 1)))
            ksil_sc = sc.n
            if ci == 0:
                plan.mark("c0a")
            cw = cos_b[:, tok0:tok0 + 128].unsqueeze(1).broadcast_to(
                [128, ND, 128])
            sw = sin_b[:, tok0:tok0 + 128].unsqueeze(1).broadcast_to(
                [128, ND, 128])
            kf3 = kf_j[:].rearrange("p (j t) -> p j t", t=128)
            u3 = scr1_b[:, 0:1024].rearrange("p (j t) -> p j t", t=128)
            n1 = dv.bump()
            V(lambda v, w=ksil_sc, pa=pe.n, cw=cw, sw=sw, kf3=kf3, u3=u3:
              (v.wait_ge(sc.h, w),
               v.wait_ge(pe.h, pa),
               v.tensor_tensor(u3, kf3, sw, ALU.mult),
               v.tensor_tensor(kf3, kf3, cw, ALU.mult).then_inc(dv.h, 1)))
            if ci == 0:
                plan.mark("c0b")
            prev_add = [None]
            for g2 in range(2):
                pn = pe.bump()
                TN(lambda tn, g2=g2, n1=n1, ag=prev_add[0]:
                   (tn.wait_ge(dv.h, n1),
                    tn.wait_ge(dv.h, ag) if ag is not None else None,
                    [tn.matmul(pC[:, 128 * mm:128 * (mm + 1)], p2l_b,
                               scr1_b[:, 512 * g2 + 128 * mm:
                                      512 * g2 + 128 * (mm + 1)],
                               start=(mm == 0), stop=(mm == 3))
                     for mm in range(4)][-1].then_inc(pe.h, 1)))
                prev_add[0] = dv.bump()
                V(lambda v, g2=g2, w=pn:
                  (v.wait_ge(pe.h, w),
                   v.tensor_add(kf_j[:, 512 * g2:512 * (g2 + 1)],
                                kf_j[:, 512 * g2:512 * (g2 + 1)],
                                pC[:, 0:512]).then_inc(dv.h, 1)))
            krope_dv = dv.n
            kz_dv = dv.bump()
            V(lambda v:
              (v.tensor_copy(kf_z[0][0:64, :], kf_j[0:64, :]),
               v.tensor_copy(kf_z[1][64:128, :], kf_j[64:128, :]
                             ).then_inc(dv.h, 1)))
            if ci == 0:
                plan.mark("c0c")
            kt_tgt = skt.bump(16)
            Y(lambda q, w=krope_dv, pa=pe.n:
              (q.wait_ge(dv.h, w),
               q.wait_ge(pe.h, pa),
               q.dma_start_transpose(
                   out=kt_j[:].rearrange("t (j F) -> t j F", F=128),
                   in_=kf_j[:].rearrange("f (j t) -> f j t", t=128),
               ).then_inc(skt.h, 16)))
            if ci == 0:
                plan.mark("c0d")
            vsc0 = sc.n
            for hh in range(2):
                ps = (pA, pB)[hh]
                mm_group(
                    ps[:, 0:512],
                    [(xT_b[:, 2048 * k + tok0:2048 * k + tok0 + 128],
                      wv_b[:, 1024 * k + 512 * hh:1024 * k + 512 * (hh + 1)])
                     for k in range(ND)],
                    pre=(lambda tn, hh=hh, kq=krope_dv:
                         tn.wait_ge(dv.h, kq) if hh == 0 else None))
                sc.bump()
                S(lambda s, hh=hh, w=pe.n, ps=ps:
                  (s.wait_ge(pe.h, w),
                   s.copy(v_j[:, 512 * hh:512 * (hh + 1)],
                          ps[:, 0:512]).then_inc(sc.h, 1)))
            if ci == 0:
                plan.mark("c0e")
            v_sc = sc.n

            for gidx in range(4):
                psb = (pE1, pE2)[gidx % 2]
                dv_guard = dv.n
                n = pe.bump()
                def ef(tn, gidx=gidx, psb=psb, ci=ci,
                       ktw=kt_tgt, vsc=v_sc, dg=dv_guard, kz=kz_dv):
                    tn.wait_ge(sc.h, vsc)
                    tn.wait_ge(skt.h, ktw)
                    tn.wait_ge(dv.h, kz)
                    if gidx >= 2:
                        tn.wait_ge(dv.h, dg)
                    for hh in range(4):
                        h = 4 * gidx + hh
                        m, par = divmod(h, 2)
                        ins = tn.matmul(
                            psb[:, 128 * hh:128 * (hh + 1)],
                            kf_z[par][:, 128 * m:128 * (m + 1)],
                            q_b[:, 2048 * m + 128 * ci:
                                2048 * m + 128 * (ci + 1)],
                            start=(hh == 0), stop=(hh == 3))
                    ins.then_inc(pe.h, 1)
                TN(ef)
                dv.bump()
                V(lambda v, gidx=gidx, psb=psb, n=n:
                  (v.wait_ge(pe.h, n),
                   v.tensor_tensor(
                       e_sb[:, 512 * gidx:512 * (gidx + 1)].rearrange(
                           "p (j f) -> p j f", f=128),
                       psb[:, 0:512].rearrange("p (j f) -> p j f", f=128),
                       maskT_b.unsqueeze(1).broadcast_to([128, 4, 128]),
                       ALU.mult).then_inc(dv.h, 1)))
            if ci == 0:
                plan.mark("c0f")
            eev_dv = dv.n
            av_pe = pe.bump()
            def avf(tn, ci=ci, ev=eev_dv, ogp=og_sc_prev[0]):
                tn.wait_ge(dv.h, ev)
                tn.wait_ge(sc.h, ogp)
                seen = set()
                plan_mm = []
                for h in range(NH):
                    hp, par = divmod(h, 2)
                    plan_mm.append(("av", h, hp, par, (hp // 4, par)))
                if ci > 0 and not globals().get('_NO_INTER'):
                    for h in range(NH):
                        hp, par = divmod(h, 2)
                        plan_mm.append(("in", h, hp, par, (hp // 4, par)))
                lastidx = {}
                for idx, (_, _, _, _, grp) in enumerate(plan_mm):
                    lastidx[grp] = idx
                for idx, (kind, h, hp, par, grp) in enumerate(plan_mm):
                    first = grp not in seen
                    seen.add(grp)
                    stop = (lastidx[grp] == idx)
                    if kind == "av":
                        ins = tn.matmul(
                            pO[64 * par:64 * (par + 1), 128 * hp:128 * (hp + 1)],
                            v_j[:, 64 * h:64 * (h + 1)],
                            e_sb[:, 128 * h:128 * (h + 1)],
                            start=first, stop=stop,
                            skip_group_check=(par == 1),
                            tile_position=(0, 64 * par))
                    else:
                        ins = tn.matmul(
                            pO[64 * par:64 * (par + 1), 128 * hp:128 * (hp + 1)],
                            s_z[:, 64 * h:64 * (h + 1)],
                            q_b[:, 2048 * hp + 128 * ci:
                                2048 * hp + 128 * (ci + 1)],
                            start=first, stop=stop,
                            skip_group_check=(par == 1),
                            tile_position=(0, 64 * par))
                ins.then_inc(pe.h, 1)
            TN(avf)
            if ci == 0:
                plan.mark("c0g")
            st_pe = pe.bump()
            def stf(tn, ci=ci, sd=dv.n):
                tn.wait_ge(dv.h, sd)
                for h in range(NH):
                    hp, par = divmod(h, 2)
                    ins = tn.matmul(
                        pS[64 * par:64 * (par + 1), 64 * hp:64 * (hp + 1)],
                        kt_j[:, 64 * h:64 * (h + 1)],
                        v_j[:, 64 * h:64 * (h + 1)],
                        start=(h < 2), stop=(h >= NH - 2),
                        skip_group_check=(par == 1),
                        tile_position=(0, 64 * par))
                ins.then_inc(pe.h, 1)
            TN(stf)
            dv.bump()
            if ci == 0:
                V(lambda v, w=st_pe:
                  (v.wait_ge(pe.h, w),
                   v.tensor_copy(s_fin, pS[:, 0:512]),
                   v.tensor_copy(
                       s_z[:].rearrange("p (hp c) -> p hp c", c=128)
                       [0:64, :, 0:64],
                       s_fin[0:64, :].rearrange("p (hp c) -> p hp c", c=64)),
                   v.tensor_copy(
                       s_z[:].rearrange("p (hp c) -> p hp c", c=128)
                       [64:128, :, 64:128],
                       s_fin[64:128, :].rearrange("p (hp c) -> p hp c", c=64)
                   ).then_inc(dv.h, 1)))
            else:
                V(lambda v, w=st_pe:
                  (v.wait_ge(pe.h, w),
                   v.tensor_add(s_fin, s_fin, pS[:, 0:512]),
                   v.tensor_copy(
                       s_z[:].rearrange("p (hp c) -> p hp c", c=128)
                       [0:64, :, 0:64],
                       s_fin[0:64, :].rearrange("p (hp c) -> p hp c", c=64)),
                   v.tensor_copy(
                       s_z[:].rearrange("p (hp c) -> p hp c", c=128)
                       [64:128, :, 64:128],
                       s_fin[64:128, :].rearrange("p (hp c) -> p hp c", c=64)
                   ).then_inc(dv.h, 1)))
            for gg in range(2):
                sc.bump()
                S(lambda s, gg=gg, ci=ci, w=av_pe:
                  (s.wait_ge(pe.h, w),
                   s.copy(og_b[:].rearrange("p (hp t) -> p hp t", t=2048
                                            )[:, 4 * gg:4 * (gg + 1),
                                              128 * ci:128 * (ci + 1)],
                          pO[:, 512 * gg:512 * (gg + 1)].rearrange(
                              "p (j f) -> p j f", f=128)).then_inc(sc.h, 1)))
            og_sc_prev[0] = sc.n
            plan.mark(f"p3c{ci}")
        loop_sc = sc.n
        loop_pe = pe.n
        tap("og0", og_b, wait=(lambda q, w=loop_sc: q.wait_ge(sc.h, w)))
        tap("S", s_fin, wait=(lambda q, w=dv.n: q.wait_ge(dv.h, w)))

        plan.mark("p3")
        n1 = dv.bump()
        V(lambda v, w=loop_pe, dvp=dv.n - 1:
          (v.tensor_scalar(scrf_f[:, 0:512], s_fin, eo_s[:, 0:1], None,
                           ALU.mult).then_inc(dv.h, 1)))
        ldg.bump(16)
        G(lambda q, w=n1, t=ldg.n:
          (q.wait_ge(dv.h, w),
           q.dma_start(out=st_in, in_=scrf_f[:, 0:512]).then_inc(ldg.h, 16)))
        cc.bump()
        G(lambda q, w=ldg.n, t=cc.n:
          (q.wait_ge(ldg.h, w),
           q.collective_compute(
               "AllReduce", ALU.add,
               replica_groups=[[0, 1], [2, 3], [4, 5], [6, 7]],
               ins=[st_in.opt()], outs=[st_out.opt()]).then_inc(cc.h, 1)))
        ldg.bump(16)
        G(lambda q, w=cc.n, t=ldg.n:
          (q.wait_ge(cc.h, w),
           q.dma_start(out=scrf_f[:, 0:512], in_=st_out).then_inc(ldg.h, 16)))
        sini_dv = dv.bump()
        V(lambda v, w=ldg.n:
          (v.wait_ge(ldg.h, w),
           v.tensor_scalar(
               s_iz[:].rearrange("p (hp c) -> p hp c", c=128)[0:64, :, 0:64],
               scrf_f[0:64, 0:512].rearrange("p (hp c) -> p hp c", c=64),
               eo_s[0:64, 1:2], None, ALU.mult),
           v.tensor_scalar(
               s_iz[:].rearrange("p (hp c) -> p hp c", c=128)
               [64:128, :, 64:128],
               scrf_f[64:128, 0:512].rearrange("p (hp c) -> p hp c", c=64),
               eo_s[64:128, 1:2], None, ALU.mult).then_inc(dv.h, 1)))

        plan.mark("p4")
        first_pref = [True]
        for ci in range(NCH):
            for gg in range(2):
                psb = (pE1, pE2)[gg]
                dv_guard = dv.n
                n = pe.bump()
                def pf(tn, ci=ci, gg=gg, psb=psb, dg=dv_guard, fp=first_pref[0]):
                    tn.wait_ge(dv.h, sini_dv)
                    if fp:
                        tn.wait_ge(sc.h, loop_sc)
                    else:
                        tn.wait_ge(dv.h, dg)
                    for hh in range(4):
                        hp = 4 * gg + hh
                        for par in range(2):
                            h = 2 * hp + par
                            ins = tn.matmul(
                                psb[64 * par:64 * (par + 1),
                                    128 * hh:128 * (hh + 1)],
                                s_iz[:, 64 * h:64 * (h + 1)],
                                q_b[:, 2048 * hp + 128 * ci:
                                    2048 * hp + 128 * (ci + 1)],
                                start=(hh == 0), stop=(hh == 3),
                                skip_group_check=(par == 1),
                                tile_position=(0, 64 * par))
                    ins.then_inc(pe.h, 1)
                TN(pf)
                first_pref[0] = False
                dv.bump()
                V(lambda v, ci=ci, gg=gg, psb=psb, n=n:
                  (v.wait_ge(pe.h, n),
                   v.tensor_add(
                       og_b[:].rearrange("p (hp t) -> p hp t", t=2048
                                         )[:, 4 * gg:4 * (gg + 1),
                                           128 * ci:128 * (ci + 1)],
                       og_b[:].rearrange("p (hp t) -> p hp t", t=2048
                                         )[:, 4 * gg:4 * (gg + 1),
                                           128 * ci:128 * (ci + 1)],
                       psb[:, 0:512].rearrange("p (j f) -> p j f", f=128)
                   ).then_inc(dv.h, 1)))
        prefix_dv = dv.n
        tap("og1", og_b, wait=(lambda q, w=prefix_dv: q.wait_ge(dv.h, w)))
        og1_tap_n = dbg_sem[0].n if _TAP == "og1" else None
        tap("sini", s_iz, wait=(lambda q, w=sini_dv: q.wait_ge(dv.h, w)))

        plan.mark("p5")
        gate_dv_prev = [prefix_dv]
        for m in range(ND):
            gsc0 = sc.n
            gpes = []
            for tt in range(4):
                ps = rot3[tt % 3]
                gpes.append(pe.bump())
                TN(lambda tn, m=m, tt=tt, ps=ps, w=prefix_dv, g0=gsc0,
                   w2=wg2_dv,
                   scw={0: gsc0, 1: gsc0 - 2, 2: gsc0 - 1,
                        3: gsc0 + 1}[tt]:
                   (tn.wait_ge(dv.h, max(w, w2)),
                    tn.wait_ge(sc.h, max(0, scw)),
                    tn.matmul(ps[:, 0:512],
                              wg2_b[0:64, 128 * m:128 * (m + 1)],
                              g1_b[0:64, 512 * tt:512 * (tt + 1)],
                              start=True, stop=True, tile_position=(0, 0)
                              ).then_inc(pe.h, 1)))
            for tt in range(4):
                sc.bump()
                S(lambda s, m=m, tt=tt, w=gpes[tt], gd=gate_dv_prev[0]:
                  (s.wait_ge(pe.h, w),
                   s.wait_ge(dv.h, gd) if tt == 0 else None,
                   s.activation(scr1_b[:, 512 * tt:512 * (tt + 1)],
                                rot3[tt % 3][:, 0:512], AF.Sigmoid
                                ).then_inc(sc.h, 1)))
            gate_dv_prev[0] = dv.bump()
            V(lambda v, m=m, w=sc.n, tn_=og1_tap_n if m == 0 else None:
              (v.wait_ge(sc.h, w),
               v.wait_ge(dbg_sem[0].h, tn_) if tn_ is not None else None,
               v.tensor_mul(og_b[:, 2048 * m:2048 * (m + 1)],
                            og_b[:, 2048 * m:2048 * (m + 1)],
                            scr1_b).then_inc(dv.h, 1)))
        gated_dv = dv.n
        tap("og2", og_b, wait=(lambda q, w=gated_dv: q.wait_ge(dv.h, w)))
        og2_tap_n = dbg_sem[0].n if _TAP == "og2" else None

        plan.mark("p6")
        prev_stats_dv = [gated_dv]
        prev_nsq = [0]
        for tt in range(4):
            n_sum = pe.bump()
            TN(lambda tn, tt=tt, w=gated_dv, pv=prev_stats_dv[0]:
               (tn.wait_ge(dv.h, max(w, pv)),
                [tn.matmul(pE1[0:1, 0:512], onesc_b,
                           og_b[:, 2048 * m + 512 * tt:2048 * m + 512 * (tt + 1)],
                           start=(m == 0), stop=(m == ND - 1))
                 for m in range(ND)][-1].then_inc(pe.h, 1)))
            sq_scs = []
            for m in range(ND):
                sq_scs.append(sc.bump())
                S(lambda s, m=m, tt=tt, w=gated_dv,
                  pq=prev_nsq[0], dst=(scr1_b, e_sb)[m // 4]:
                  (s.wait_ge(dv.h, w),
                   s.wait_ge(pe.h, pq) if pq else None,
                   s.activation(dst[:, 512 * (m % 4):512 * (m % 4 + 1)],
                                og_b[:, 2048 * m + 512 * tt:
                                     2048 * m + 512 * (tt + 1)],
                                AF.Square).then_inc(sc.h, 1)))
            n_sq = pe.bump()
            TN(lambda tn, tt=tt, sqs=list(sq_scs), pv=prev_stats_dv[0]:
               (tn.wait_ge(dv.h, pv),
                [(tn.wait_ge(sc.h, sqs[m]),
                  tn.matmul(pE2[0:1, 0:512], onesc_b,
                            (scr1_b, e_sb)[m // 4][:,
                                512 * (m % 4):512 * (m % 4 + 1)],
                            start=(m == 0), stop=(m == ND - 1)))
                 for m in range(ND)][-1][1].then_inc(pe.h, 1)))
            prev_nsq[0] = n_sq
            nst = dv.bump()
            V(lambda v, w2=n_sq:
              (v.wait_ge(pe.h, w2),
               v.tensor_scalar(mu_row[0:1, 0:512], pE1[0:1, 0:512],
                               1.0 / D, None, ALU.mult),
               v.tensor_scalar(rr_row[0:1, 0:512], pE2[0:1, 0:512],
                               1.0 / D, None, ALU.mult),
               v.tensor_mul(scrf_f[0:1, 0:512], mu_row[0:1, 0:512],
                            mu_row[0:1, 0:512]),
               v.tensor_sub(rr_row[0:1, 0:512], rr_row[0:1, 0:512],
                            scrf_f[0:1, 0:512]).then_inc(dv.h, 1)))
            prev_stats_dv[0] = nst
            nsd = sc.bump()
            S(lambda s, w=nst:
              (s.wait_ge(dv.h, w),
               s.activation(rr_row[0:1, 0:512], rr_row[0:1, 0:512],
                            AF.Sqrt, bias=eps_s[0:1, 0:1]).then_inc(sc.h, 1)))
            nrc = dv.bump()
            V(lambda v, w=nsd:
              (v.wait_ge(sc.h, w),
               v.reciprocal(rr_row[0:1, 0:512], rr_row[0:1, 0:512]
                            ).then_inc(dv.h, 1)))
            nbc = pe.bump()
            TN(lambda tn, w=nrc:
               (tn.wait_ge(dv.h, w),
                tn.matmul(pA[:, 0:512], onesr_f, mu_row[0:1, 0:512],
                          start=True, stop=True),
                tn.matmul(pB[:, 0:512], onesr_f, rr_row[0:1, 0:512],
                          start=True, stop=True).then_inc(pe.h, 1)))
            for m in range(ND):
                nap = dv.bump()
                V(lambda v, m=m, tt=tt, w=nbc,
                  tn_=og2_tap_n if (m == 0 and tt == 0) else None:
                  (v.wait_ge(pe.h, w),
                   v.wait_ge(dbg_sem[0].h, tn_) if tn_ is not None else None,
                   v.scalar_tensor_tensor(
                       scrf_f[:, 0:512], pA[:, 0:512], -1.0,
                       og_b[:, 2048 * m + 512 * tt:2048 * m + 512 * (tt + 1)],
                       ALU.mult, ALU.add),
                   v.tensor_mul(scrf_f[:, 512:1024], scrf_f[:, 0:512],
                                pB[:, 0:512]),
                   v.tensor_scalar(og_b[:, 2048 * m + 512 * tt:
                                        2048 * m + 512 * (tt + 1)],
                                   scrf_f[:, 512:1024],
                                   lnw_s[:, m:m + 1], lnb_s[:, m:m + 1],
                                   ALU.mult, ALU.add).then_inc(dv.h, 1)))
            prev_stats_dv[0] = dv.n
        y_dv = dv.n
        tap("y", og_b, wait=(lambda q, w=y_dv: q.wait_ge(dv.h, w)))

        plan.mark("p7")
        for tt in range(NT):
            sl = sld[tt % 2]
            sp = ssp[tt % 2]
            sp_guard = sp.n
            x_tgt = sl.bump(16)
            Y(lambda q, tt=tt, yd=y_dv, spg=sp_guard, sl=sl, sp=sp,
              nv=dv.n:
              (q.wait_ge(dv.h, yd) if tt == 0 else None,
               q.wait_ge(dv.h, nv) if tt == 1 else None,
               q.wait_ge(sp.h, spg) if tt >= 2 else None,
               q.dma_start(out=stg_f[tt % 2],
                           in_=x_h[128 * tt:128 * (tt + 1), :]
                           ).then_inc(sl.h, 16)))
            wo_pes = []
            for hh in range(2):
                ps = (pA, pB)[hh]
                wo_pes.append(pe.bump())
                TN(lambda tn, tt=tt, hh=hh, ps=ps, yd=y_dv, dvw=dv.n:
                   (tn.wait_ge(dv.h, yd),
                    tn.wait_ge(dv.h, dvw) if tt > 0 else None,
                    [tn.matmul(ps[:, 0:512],
                               og_b[:, 2048 * m + 128 * tt:
                                    2048 * m + 128 * (tt + 1)],
                               wo_b[:, 1024 * m + 512 * hh:
                                    1024 * m + 512 * (hh + 1)],
                               start=(m == 0), stop=(m == ND - 1))
                     for m in range(ND)][-1].then_inc(pe.h, 1)))
            nev = dv.bump()
            V(lambda v, tt=tt, w=wo_pes, xt=x_tgt, sl=sl:
              (v.wait_ge(pe.h, w[1]),
               v.wait_ge(sl.h, xt),
               v.scalar_tensor_tensor(scrf_f[:, 0:512], pA[:, 0:512], 0.0,
                                      stg_f[tt % 2][:, 0:512],
                                      ALU.bypass, ALU.add),
               v.scalar_tensor_tensor(scrf_f[:, 512:1024], pB[:, 0:512], 0.0,
                                      stg_f[tt % 2][:, 512:1024],
                                      ALU.bypass, ALU.add),
               v.reduce_sum(st1_s[:, 0:1], scrf_f[:, 0:1024], AX),
               v.tensor_mul(scr1_f[:, 0:1024], scrf_f[:, 0:1024],
                            scrf_f[:, 0:1024]),
               v.reduce_sum(st1_s[:, 1:2], scr1_f[:, 0:1024], AX),
               [(v.tensor_scalar(st1_s[:, 2:3], st1_s[:, 0:1], 1.0 / D,
                                 None, ALU.mult),
                 v.tensor_scalar(st1_s[:, 3:4], st1_s[:, 1:2], 1.0 / D,
                                 None, ALU.mult),
                 v.tensor_mul(st1_s[:, 4:5], st1_s[:, 2:3], st1_s[:, 2:3]),
                 v.tensor_sub(st1_s[:, 7:8], st1_s[:, 3:4], st1_s[:, 4:5]))
                for _r in range(2)][-1][-1].then_inc(dv.h, 1)))
            if tt == 0 and _TAP in ("p8s", "p8x"):
                ds = dbg_sem[0]
                ds.bump(16)
                Y(lambda q, w=nev, ds=ds:
                  (q.wait_ge(dv.h, w),
                   q.dma_start(out=dbg_h[:, 0:1024].bitcast(F32),
                               in_=(scrf_f[:, 0:1024] if _TAP == "p8s"
                                    else stg_f[0][:, 0:1024])
                               ).then_inc(ds.h, 16)))
            nsd = sc.bump()
            S(lambda s, w=nev:
              (s.wait_ge(dv.h, w),
               s.activation(st1_s[:, 5:6], st1_s[:, 7:8], AF.Sqrt,
                            bias=eps_s).then_inc(sc.h, 1)))
            nap = dv.bump()
            V(lambda v, tt=tt, w=nsd, dbt=(tt == 0 and _TAP in ("p8s", "p8x")):
              (v.wait_ge(sc.h, w),
               v.wait_ge(dbg_sem[0].h, dbg_sem[0].n) if dbt else None,
               [v.memset(scr1_f[:, 0:1], 0.0) for _p in range(4)][-1],
               v.reciprocal(st1_s[:, 6:7], st1_s[:, 5:6]),
               v.reciprocal(st1_s[:, 6:7], st1_s[:, 5:6]),
               v.memset(scr1_f[:, 0:1], 0.0),
               v.memset(scr1_f[:, 0:1], 0.0),
               v.tensor_scalar(stg_f[tt % 2][:, 0:1024], scrf_f[:, 0:1024],
                               st1_s[:, 2:3], st1_s[:, 6:7],
                               ALU.subtract, ALU.mult),
               v.tensor_reduce(mx_acc[:, tt:tt + 1], stg_f[tt % 2][:, 0:1024],
                               AX, ALU.max, apply_absolute_value=True
                               ).then_inc(dv.h, 1)))
            sp.bump(16)
            S(lambda s, tt=tt, w=nap, sp=sp:
              (s.wait_ge(dv.h, w),
               s.dma_start(out=x1_spill[128 * tt:128 * (tt + 1), :],
                           in_=stg_f[tt % 2]).then_inc(sp.h, 16)))
        x1_dv = dv.n
        x1_sp = (ssp[0].n, ssp[1].n)
        if _TAP == "x1f":
            ds = dbg_sem[0]
            ds.bump(16)
            Y(lambda q, a=ssp[0].n, b=ssp[1].n, ds=ds:
              (q.wait_ge(ssp[0].h, a), q.wait_ge(ssp[1].h, b),
               q.dma_start(
                   out=dbg_h.rearrange("p (a d) -> a p d", d=1024),
                   in_=x1_spill.rearrange("(a p) d -> a p d", p=128),
               ).then_inc(ds.h, 16)))
        tap("x1", None, wait=(lambda q, a=ssp[0].n, b=ssp[1].n:
                              (q.wait_ge(ssp[0].h, a), q.wait_ge(ssp[1].h, b))),
            dram_src=x1_spill[0:128, :])

        plan.mark("p8")
        def barrier(bidx, srcs, lane_base):
            nred = None
            for i, (acc, K) in enumerate(srcs):
                n1b = dv.bump()
                V(lambda v, acc=acc, K=K:
                  (v.tensor_reduce(acc[:, 31:32], acc[:, 0:K], AX, ALU.max,
                                   apply_absolute_value=True),
                   v.tensor_reduce(acc[:, 31:32], acc[:, 0:K], AX, ALU.max,
                                   apply_absolute_value=True
                                   ).then_inc(dv.h, 1)))
                ldg.bump(16)
                G(lambda q, acc=acc, w=n1b, t=ldg.n - 16:
                  (q.wait_ge(dv.h, w),
                   q.wait_ge(ldg.h, t),
                   q.dma_start(out=row_s, in_=acc[:, 31:32]).then_inc(ldg.h, 16)))
                nred = dv.bump()
                V(lambda v, i=i, w=ldg.n:
                  (v.wait_ge(ldg.h, w),
                   v.tensor_reduce(sc_s[0:1, i:i + 1], row_s, AX, ALU.max
                                   ).then_inc(dv.h, 1)))
            ldg.bump(16)
            G(lambda q, b=bidx, w=nred:
              (q.wait_ge(dv.h, w),
               q.dma_start(out=bar_in[b], in_=sc_s[0:1, 0:8]).then_inc(ldg.h, 16)))
            cc.bump()
            G(lambda q, b=bidx, w=ldg.n:
              (q.wait_ge(ldg.h, w),
               q.collective_compute(
                   "AllReduce", ALU.max, replica_groups=[list(range(8))],
                   ins=[bar_in[b].opt()], outs=[bar_out[b].opt()]
               ).then_inc(cc.h, 1)))
            ldg.bump(16)
            G(lambda q, b=bidx, w=cc.n:
              (q.wait_ge(cc.h, w),
               q.dma_start(out=res_s, in_=bar_out[b]).then_inc(ldg.h, 16)))
            nsc_ = dv.bump()
            V(lambda v, ns=len(srcs), w=ldg.n:
              (v.wait_ge(ldg.h, w),
               [v.tensor_scalar(tmp1_s[0:1, 2 * i:2 * i + 1],
                                res_s[0:1, i:i + 1], 1.0 / QMAX, 1e-12,
                                ALU.mult, ALU.max) for i in range(ns)],
               [v.memset(tmp1_s[0:1, 7:8], 0.0) for _p in range(4)],
               [(v.reciprocal(tmp1_s[0:1, 2 * i + 1:2 * i + 2],
                              tmp1_s[0:1, 2 * i:2 * i + 1]),
                 v.reciprocal(tmp1_s[0:1, 2 * i + 1:2 * i + 2],
                              tmp1_s[0:1, 2 * i:2 * i + 1]))
                for i in range(ns)][-1][1].then_inc(dv.h, 1)))
            nbc = pe.bump()
            TN(lambda tn, ns=len(srcs), w=nsc_:
               (tn.wait_ge(dv.h, w),
                tn.matmul(pC[:, 0:2 * ns], onesr_f, tmp1_s[0:1, 0:2 * ns],
                          start=True, stop=True).then_inc(pe.h, 1)))
            nfin = dv.bump()
            V(lambda v, ns=len(srcs), lb=lane_base, w=nbc:
              (v.wait_ge(pe.h, w),
               v.tensor_copy(scl_bc[:, lb:lb + 2 * ns], pC[:, 0:2 * ns]
                             ).then_inc(dv.h, 1)))
            return nfin

        pre_bar_dv = dv.n
        bar1_dv = barrier(0, [(mx_acc, NT)], 0)
        if _TAP == "bar1d":
            ds = dbg_sem[0]
            ds.bump(96)
            Y(lambda q, w=bar1_dv, ds=ds:
              (q.wait_ge(dv.h, w),
               q.dma_start(out=dbg_h[:, 0:32].bitcast(F32),
                           in_=mx_acc[:, 0:32]).then_inc(ds.h, 16),
               q.dma_start(out=dbg_h[0:1, 32:160].bitcast(F32),
                           in_=row_s).then_inc(ds.h, 16),
               q.dma_start(out=dbg_h[0:1, 160:168].bitcast(F32),
                           in_=sc_s).then_inc(ds.h, 16),
               q.dma_start(out=dbg_h[0:1, 192:200].bitcast(F32),
                           in_=res_s).then_inc(ds.h, 16),
               q.dma_start(out=dbg_h[0:1, 224:232].bitcast(F32),
                           in_=tmp1_s).then_inc(ds.h, 16),
               q.dma_start(out=dbg_h[:, 256:268].bitcast(F32),
                           in_=scl_bc[:, 0:12]).then_inc(ds.h, 16)))

        plan.mark("bar1")

        plan.mark("wprep")
        p9_dv0 = pre_bar_dv
        for tt in range(NT):
            sx = sxq[tt % 2]
            sq_ = stq[tt % 2]
            tr_guard = sq_.n
            x_tgt = sx.bump(16)
            Y(lambda q, tt=tt, d0=p9_dv0, spx=x1_sp, sx=sx,
              pv=(dv.n - 1 if tt >= 2 else None):
              ((q.wait_ge(ssp[0].h, spx[0]), q.wait_ge(ssp[1].h, spx[1]))
               if tt == 0 else None,
               q.wait_ge(dv.h, d0) if tt < 2 else q.wait_ge(dv.h, pv),
               q.dma_start(out=stg_f[tt % 2],
                           in_=x1_spill[128 * tt:128 * (tt + 1), :]
                           ).then_inc(sx.h, 16)))
            nq = dv.bump()
            V(lambda v, tt=tt, xt=x_tgt, b1=bar1_dv, sx=sx, sq_=sq_,
              tg=tr_guard:
              (v.wait_ge(sx.h, xt),
               v.wait_ge(dv.h, b1) if tt == 0 else None,
               v.wait_ge(sq_.h, tg) if tt >= 2 else None,
               v.tensor_scalar(scrf_i[:, 0:1024], stg_f[tt % 2],
                               scl_bc[:, 1:2], None, ALU.mult),
               v.tensor_scalar(xqt_st[tt % 2], scrf_i[:, 0:1024],
                               scl_bc[:, 0:1], None, ALU.mult).then_inc(dv.h, 1)))
            sq_.bump(16)
            Y(lambda q, tt=tt, w=nq, lp=loop_pe, sq_=sq_:
              (q.wait_ge(dv.h, w),
               q.dma_start_transpose(
                   out=xqT_b[:].rearrange("p (j t) -> p j t", t=2048
                                          )[:, :, 128 * tt:128 * (tt + 1)],
                   in_=xqt_st[tt % 2].rearrange("t (j h) -> t j h", h=128),
               ).then_inc(sq_.h, 16)))
        xq_tr = (stq[0].n, stq[1].n)
        tap("xqT", xqT_b, wait=(lambda q, a=stq[0].n, b=stq[1].n:
                                (q.wait_ge(stq[0].h, a), q.wait_ge(stq[1].h, b))))
        tap("sclx", scl_bc, wait=(lambda q, w=bar1_dv: q.wait_ge(dv.h, w)))

        plan.mark("p9")
        pe_at = pe.n
        sc_at = sc.n
        p9_last = dv.n
        evac_sc0 = sc.n
        prep_prev_dv = [p9_last, p9_last]

        def prep_chunk(ci, src_h, dst_b, tsem, half):
            sl = sld[half]
            tgt = sl.bump(16)
            Y(lambda q, w=prep_prev_dv[half], s=src_h, ci=ci, sl=sl:
              (q.wait_ge(dv.h, w),
               q.dma_start(out=stg_f[half],
                           in_=s[128 * ci:128 * (ci + 1), :]
                           ).then_inc(sl.h, 16)))
            ncast = dv.bump()
            prep_prev_dv[half] = ncast
            wn0 = 2048 * half + 1024 * (ci % 2)
            V(lambda v, ci=ci, tgt=tgt, sl=sl, wn0=wn0, tsem=tsem,
              tg=16 * (ci - 1), first=(ci == 0), half=half:
              (v.wait_ge(sl.h, tgt),
               (v.wait_ge(pe.h, pe_at), v.wait_ge(sc.h, sc_at))
               if first else None,
               v.wait_ge(tsem.h, tg) if ci >= 2 else None,
               v.tensor_reduce(
                   bm_s[:, 0:32],
                   stg_f[half].rearrange("p (b c) -> p b c", c=32),
                   AX, ALU.max, apply_absolute_value=True),
               v.tensor_scalar(bm_s[:, 32:64], bm_s[:, 0:32],
                               1.0 / 7.0, 1e-12, ALU.mult, ALU.max),
               v.tensor_scalar(bm_s[:, 32:64], bm_s[:, 0:32],
                               1.0 / 7.0, 1e-12, ALU.mult, ALU.max),
               [v.memset(bm_s[:, 64:65], 0.0) for _p in range(4)][-1],
               v.reciprocal(br_s[:, 0:32], bm_s[:, 32:64]),
               v.reciprocal(br_s[:, 0:32], bm_s[:, 32:64]),
               v.memset(bm_s[:, 64:65], 0.0),
               v.memset(bm_s[:, 64:65], 0.0),
               v.tensor_tensor(
                   scrf_i[:, 0:1024].rearrange("p (b c) -> p b c", c=32),
                   stg_f[half].rearrange("p (b c) -> p b c", c=32),
                   br_s[:, 0:32].unsqueeze(2).broadcast_to([128, 32, 32]),
                   ALU.mult),
               v.scalar_tensor_tensor(
                   wn6_b[:, wn0:wn0 + 1024].rearrange(
                       "p (b c) -> p b c", c=32),
                   scrf_i[:, 0:1024].rearrange("p (b c) -> p b c", c=32),
                   0.0,
                   bm_s[:, 32:64].unsqueeze(2).broadcast_to([128, 32, 32]),
                   ALU.bypass, ALU.mult).then_inc(dv.h, 1)))
            tsem.bump(16)
            Y(lambda q, ci=ci, w=ncast, d=dst_b, wn0=wn0, tsem=tsem,
              first=(ci == 0):
              (q.wait_ge(dv.h, w),
               (q.wait_ge(pe.h, pe_at), q.wait_ge(sc.h, sc_at))
               if first else None,
               q.dma_start_transpose(
                   out=d[:].rearrange("p (hc i) -> p hc i", i=3072
                                      )[:, :, 128 * ci:128 * (ci + 1)],
                   in_=wn6_b[:, wn0:wn0 + 1024].rearrange(
                       "i (hc h) -> i hc h", h=128),
               ).then_inc(tsem.h, 16)))

        gcnt = [0]

        def f23_m(m, wT_b, acc, stgc, spsem, tsem, spill):
            mgs = []
            mpes = []
            for tt in range(4):
                g = gcnt[0]
                gcnt[0] += 1
                mgs.append(g)
                ps = rot3[g % 3]
                mpes.append(pe.bump())
                TN(lambda tn, m=m, tt=tt, g=g, ps=ps, wT=wT_b, tsem=tsem:
                   (tn.wait_ge(tsem.h, 16 * (m + 1)),
                    (tn.wait_ge(stq[0].h, xq_tr[0]),
                     tn.wait_ge(stq[1].h, xq_tr[1])) if g == 0 else None,
                    tn.wait_ge(sc.h, evac_sc0 + g - 2) if g >= 3 else None,
                    [tn.matmul(ps[:, 0:512],
                               wT[:, 3072 * k + 128 * m:
                                  3072 * k + 128 * (m + 1)],
                               xqT_b[:, 2048 * k + 512 * tt:
                                     2048 * k + 512 * (tt + 1)],
                               start=(k == 0), stop=(k == ND - 1))
                     for k in range(ND)][-1].then_inc(pe.h, 1)))
            spg = spsem.n
            for tt in range(4):
                sc.bump()
                S(lambda s, m=m, tt=tt, g=mgs[tt], w=mpes[tt], stgc=stgc,
                  sp=spsem, spg=spg:
                  (s.wait_ge(pe.h, w),
                   s.wait_ge(sp.h, spg) if (tt == 0 and m >= 1) else None,
                   s.copy(stgc[:, 512 * tt:512 * (tt + 1)],
                          rot3[g % 3][:, 0:512]).then_inc(sc.h, 1)))
            nmx = dv.bump()
            V(lambda v, m=m, w=sc.n, stgc=stgc, acc=acc:
              (v.wait_ge(sc.h, w),
               v.tensor_reduce(acc[:, m % 24:m % 24 + 1], stgc, AX,
                               ALU.max, apply_absolute_value=True
                               ).then_inc(dv.h, 1)))
            spsem.bump(16)
            S(lambda s, m=m, w=nmx, stgc=stgc, spd=spill, sp=spsem:
              (s.wait_ge(dv.h, w),
               s.dma_start(out=spd[128 * m:128 * (m + 1), :], in_=stgc
                           ).then_inc(sp.h, 16)))

        for m in range(NF):
            prep_chunk(m, wgate_h, wgT_b, swg, 0)
            prep_chunk(m, wup_h, wuT_b, swu, 1)
            f23_m(m, wgT_b, gm_acc, scr1_b, ssp[0], swg, g_spill)
            f23_m(m, wuT_b, um_acc, e_sb, ssp[1], swu, u_spill)
        wgT_tr = swg.n
        wuT_tr = swu.n
        fmm_sp = (ssp[0].n, ssp[1].n)
        tap("graw", None, wait=(lambda q, a=ssp[0].n, b=ssp[1].n:
                                (q.wait_ge(ssp[0].h, a), q.wait_ge(ssp[1].h, b))),
            dram_src=g_spill[0:128, :])
        tap("uraw", None, wait=(lambda q, a=ssp[0].n, b=ssp[1].n:
                                (q.wait_ge(ssp[0].h, a), q.wait_ge(ssp[1].h, b))),
            dram_src=u_spill[0:128, :])
        tap("wgT", wgT_b, wait=(lambda q, w=wgT_tr: q.wait_ge(swg.h, w)))

        bar2_dv = barrier(1, [(gm_acc, 24), (um_acc, 24)], 2)

        plan.mark("f23")
        wd_pe0 = pe.n
        wd_prev = [dv.n, dv.n]

        def wd_item(i):
            hc, th = divmod(i, 3)
            sl = sld[i % 2]
            tgt = sl.bump(16)
            Y(lambda q, i=i, hc=hc, th=th, w=wd_prev[i % 2], fs=fmm_sp,
              sl=sl:
              ((q.wait_ge(ssp[0].h, fs[0]), q.wait_ge(ssp[1].h, fs[1]))
               if i == 0 else None,
               q.wait_ge(dv.h, w),
               q.dma_start(out=stg_f[i % 2],
                           in_=wdown_h[128 * hc:128 * (hc + 1),
                                       1024 * th:1024 * (th + 1)]
                           ).then_inc(sl.h, 16)))
            ndq = dv.bump()
            wd_prev[i % 2] = ndq
            V(lambda v, i=i, tgt=sl.n, sl=sl, tg=16 * (i - 1):
              (v.wait_ge(sl.h, tgt),
               v.wait_ge(swd.h, tg) if i >= 2 else None,
               v.tensor_reduce(
                   bm_s[:, 0:32],
                   stg_f[i % 2].rearrange("p (b c) -> p b c", c=32),
                   AX, ALU.max, apply_absolute_value=True),
               v.tensor_scalar(bm_s[:, 32:64], bm_s[:, 0:32], 1.0 / 7.0,
                               1e-12, ALU.mult, ALU.max),
               v.tensor_scalar(bm_s[:, 32:64], bm_s[:, 0:32], 1.0 / 7.0,
                               1e-12, ALU.mult, ALU.max),
               [v.memset(bm_s[:, 64:65], 0.0) for _p in range(4)][-1],
               v.reciprocal(br_s[:, 0:32], bm_s[:, 32:64]),
               v.reciprocal(br_s[:, 0:32], bm_s[:, 32:64]),
               v.memset(bm_s[:, 64:65], 0.0),
               v.memset(bm_s[:, 64:65], 0.0),
               v.tensor_tensor(
                   scrf_i[:, 0:1024].rearrange("p (b c) -> p b c", c=32),
                   stg_f[i % 2].rearrange("p (b c) -> p b c", c=32),
                   br_s[:, 0:32].unsqueeze(2).broadcast_to([128, 32, 32]),
                   ALU.mult),
               v.scalar_tensor_tensor(
                   wd_st[i % 2][:].rearrange("p (b c) -> p b c", c=32),
                   scrf_i[:, 0:1024].rearrange("p (b c) -> p b c", c=32),
                   0.0,
                   bm_s[:, 32:64].unsqueeze(2).broadcast_to([128, 32, 32]),
                   ALU.bypass, ALU.mult).then_inc(dv.h, 1)))
            swd.bump(16)
            Y(lambda q, i=i, hc=hc, th=th, w=ndq, pa=wd_pe0:
              (q.wait_ge(dv.h, w),
               q.wait_ge(pe.h, pa) if i == 0 else None,
               q.dma_start_transpose(
                   out=wgT_b[:, 0:24576].rearrange(
                       "p (j F) -> p j F", F=1024
                   )[:, 8 * th:8 * (th + 1), 128 * hc:128 * (hc + 1)],
                   in_=wd_st[i % 2][:].rearrange("h (j i2) -> h j i2",
                                                 i2=128),
               ).then_inc(swd.h, 16)))

        plan.mark("wdprep")
        for m in range(NF):
            wd_item(m)
            stgc = (scr1_b, e_sb)[m % 2]
            sr = srb[m % 2]
            sp = ssp[m % 2]
            wprev = dv.n
            tgt = sr.bump(16)
            Y(lambda q, m=m, b2=bar2_dv, w=wprev, stgc=stgc, sr=sr:
              (q.wait_ge(dv.h, b2) if m < 2 else q.wait_ge(dv.h, w - 1),
               q.dma_start(out=stgc,
                           in_=g_spill[128 * m:128 * (m + 1), :]
                           ).then_inc(sr.h, 16)))
            n1 = dv.bump()
            V(lambda v, m=m, tgt=tgt, stgc=stgc, sr=sr:
              (v.wait_ge(sr.h, tgt),
               [(v.tensor_scalar(scrf_i[:, 0:1024],
                                 stgc[:, 1024 * hf:1024 * (hf + 1)],
                                 scl_bc[:, 3:4], None, ALU.mult),
                 v.tensor_scalar(gq_c[:, 1024 * hf:1024 * (hf + 1)],
                                 scrf_i[:, 0:1024], scl_bc[:, 2:3],
                                 None, ALU.mult)) for hf in range(2)
                ][-1][1].then_inc(dv.h, 1)))
            ns = sc.bump()
            S(lambda s, w=n1:
              (s.wait_ge(dv.h, w),
               s.activation(sig_c, gq_c, AF.Sigmoid).then_inc(sc.h, 1)))
            sp_guard = sp.n
            n2 = dv.bump()
            V(lambda v, m=m, w=ns, sp=sp, spg=sp_guard:
              (v.wait_ge(sc.h, w),
               v.wait_ge(sp.h, spg) if m >= 2 else None,
               [(v.tensor_scalar(scrf_i[:, 0:1024],
                                 sig_c[:, 1024 * hf:1024 * (hf + 1)],
                                 65536.0, None, ALU.mult),
                 v.tensor_scalar(sgq_c[:, 1024 * hf:1024 * (hf + 1)],
                                 scrf_i[:, 0:1024], 65535.0,
                                 1.0 / 65536.0, ALU.min, ALU.mult))
                for hf in range(2)],
               v.tensor_mul(g2st[m % 2], gq_c, sgq_c),
               v.tensor_reduce(gm_acc[:, m % 24:m % 24 + 1],
                               g2st[m % 2], AX,
                               ALU.max, apply_absolute_value=True
                               ).then_inc(dv.h, 1)))
            sp.bump(16)
            S(lambda s, m=m, w=n2, sp=sp:
              (s.wait_ge(dv.h, w),
               s.dma_start(out=g_spill[128 * m:128 * (m + 1), :],
                           in_=g2st[m % 2]).then_inc(sp.h, 16)))
        f4_sp = (ssp[0].n, ssp[1].n)
        tap("g2", None, wait=(lambda q, a=ssp[0].n, b=ssp[1].n:
                              (q.wait_ge(ssp[0].h, a), q.wait_ge(ssp[1].h, b))),
            dram_src=g_spill[0:128, :])

        wdT_tr = swd.n

        bar3_dv = barrier(2, [(gm_acc, 24)], 6)

        plan.mark("f4")
        for m in range(NF):
            stgc = (scr1_b, e_sb)[m % 2]
            sr = srb[m % 2]
            sp = ssp[m % 2]
            wprev = dv.n
            tgt0 = sr.n
            sr.bump(32)
            Y(lambda q, m=m, b3=bar3_dv, w=wprev, stgc=stgc, sr=sr,
              fs=f4_sp:
              ((q.wait_ge(dv.h, b3),
                q.wait_ge(ssp[0].h, fs[0]), q.wait_ge(ssp[1].h, fs[1]))
               if m < 2 else q.wait_ge(dv.h, w - 1),
               q.dma_start(out=stgc,
                           in_=u_spill[128 * m:128 * (m + 1), :]
                           ).then_inc(sr.h, 16),
               q.dma_start(out=g2st[m % 2],
                           in_=g_spill[128 * m:128 * (m + 1), :]
                           ).then_inc(sr.h, 16)))
            sp_guard = sp.n
            n2 = dv.bump()
            V(lambda v, m=m, tgt=sr.n, stgc=stgc, sr=sr, sp=sp, spg=sp_guard:
              (v.wait_ge(sr.h, tgt),
               v.wait_ge(sp.h, spg) if m >= 2 else None,
               [(v.tensor_scalar(scrf_i[:, 0:1024],
                                 stgc[:, 1024 * hf:1024 * (hf + 1)],
                                 scl_bc[:, 5:6], None, ALU.mult),
                 v.tensor_scalar(sgq_c[:, 1024 * hf:1024 * (hf + 1)],
                                 scrf_i[:, 0:1024], scl_bc[:, 4:5],
                                 None, ALU.mult),
                 v.tensor_scalar(scrf_i[:, 0:1024],
                                 g2st[m % 2][:, 1024 * hf:1024 * (hf + 1)],
                                 scl_bc[:, 7:8], None, ALU.mult),
                 v.tensor_scalar(gq_c[:, 1024 * hf:1024 * (hf + 1)],
                                 scrf_i[:, 0:1024], scl_bc[:, 6:7],
                                 None, ALU.mult)) for hf in range(2)],
               v.tensor_mul(oost[m % 2], gq_c, sgq_c),
               v.tensor_reduce(um_acc[:, m % 24:m % 24 + 1],
                               oost[m % 2], AX,
                               ALU.max, apply_absolute_value=True
                               ).then_inc(dv.h, 1)))
            sp.bump(16)
            S(lambda s, m=m, w=n2, sp=sp:
              (s.wait_ge(dv.h, w),
               s.dma_start(out=u_spill[128 * m:128 * (m + 1), :],
                           in_=oost[m % 2]).then_inc(sp.h, 16)))
        f5_sp = (ssp[0].n, ssp[1].n)
        tap("o", None, wait=(lambda q, a=ssp[0].n, b=ssp[1].n:
                             (q.wait_ge(ssp[0].h, a), q.wait_ge(ssp[1].h, b))),
            dram_src=u_spill[0:128, :])

        bar4_dv = barrier(3, [(um_acc, 24)], 8)

        plan.mark("f5")
        f6_quant = []
        f6_out = []
        for tt in range(NT):
            so_ = sot[tt % 2]
            sx = sxq[tt % 2]
            su = sou[tt % 2]
            ot_tgt = so_.bump(16)
            Y(lambda q, tt=tt, b4=bar4_dv, fs=f5_sp, so_=so_,
              pv=(f6_quant[tt - 2] if tt >= 2 else None):
              ((q.wait_ge(ssp[0].h, fs[0]), q.wait_ge(ssp[1].h, fs[1]),
                q.wait_ge(dv.h, b4)) if tt == 0 else None,
               q.wait_ge(dv.h, pv) if pv is not None else None,
               q.dma_start(
                   out=o_tile[tt % 2][:].rearrange("p (ci f) -> p ci f", f=128),
                   in_=u_spill.rearrange("(ci p) t -> p ci t", p=128
                                         )[:, :, 128 * tt:128 * (tt + 1)],
               ).then_inc(so_.h, 16)))
            out_guard = su.n
            x_tgt = sx.bump(16)
            Y(lambda q, tt=tt, su=su, sx=sx, og=out_guard:
              (q.wait_ge(su.h, og) if tt >= 2 else None,
               q.dma_start(out=stg_f[tt % 2],
                           in_=x1_spill[128 * tt:128 * (tt + 1), :]
                           ).then_inc(sx.h, 16)))
            nq = dv.bump()
            f6_quant.append(nq)
            V(lambda v, tt=tt, ot=ot_tgt, b4=bar4_dv, so_=so_:
              (v.wait_ge(so_.h, ot),
               v.wait_ge(dv.h, b4) if tt == 0 else None,
               [(v.tensor_scalar(scrf_i[:, 0:1024],
                                 o_tile[tt % 2][:, 1024 * s3:1024 * (s3 + 1)],
                                 scl_bc[:, 9:10], None, ALU.mult),
                 v.tensor_scalar(o_tile[tt % 2][:, 1024 * s3:1024 * (s3 + 1)],
                                 scrf_i[:, 0:1024], scl_bc[:, 8:9],
                                 None, ALU.mult)) for s3 in range(3)
                ][-1][1].then_inc(dv.h, 1)))
            dn_pes = []
            for hh in range(2):
                ps = (pA, pB)[hh]
                dn_pes.append(pe.bump())
                TN(lambda tn, tt=tt, hh=hh, ps=ps, w=nq, dvw=dv.n:
                   (tn.wait_ge(dv.h, w),
                    tn.wait_ge(swd.h, wdT_tr) if tt == 0 else None,
                    tn.wait_ge(dv.h, dvw) if tt > 0 else None,
                    [tn.matmul(ps[:, 0:512],
                               o_tile[tt % 2][:, 128 * ci:128 * (ci + 1)],
                               wgT_b[:, 1024 * ci + 512 * hh:
                                     1024 * ci + 512 * (hh + 1)],
                               start=(ci == 0), stop=(ci == NF - 1))
                     for ci in range(NF)][-1].then_inc(pe.h, 1)))
            nev = dv.bump()
            V(lambda v, tt=tt, w=dn_pes, xt=x_tgt, sx=sx:
              (v.wait_ge(pe.h, w[1]),
               v.wait_ge(sx.h, xt),
               v.scalar_tensor_tensor(scrf_f[:, 0:512], pA[:, 0:512], 0.0,
                                      stg_f[tt % 2][:, 0:512],
                                      ALU.bypass, ALU.add),
               v.scalar_tensor_tensor(scrf_f[:, 512:1024], pB[:, 0:512], 0.0,
                                      stg_f[tt % 2][:, 512:1024],
                                      ALU.bypass, ALU.add),
               v.reduce_sum(st1_s[:, 0:1], scrf_f[:, 0:1024], AX),
               v.tensor_mul(scr1_f[:, 0:1024], scrf_f[:, 0:1024],
                            scrf_f[:, 0:1024]),
               v.reduce_sum(st1_s[:, 1:2], scr1_f[:, 0:1024], AX),
               [(v.tensor_scalar(st1_s[:, 2:3], st1_s[:, 0:1], 1.0 / D,
                                 None, ALU.mult),
                 v.tensor_scalar(st1_s[:, 3:4], st1_s[:, 1:2], 1.0 / D,
                                 None, ALU.mult),
                 v.tensor_mul(st1_s[:, 4:5], st1_s[:, 2:3], st1_s[:, 2:3]),
                 v.tensor_sub(st1_s[:, 7:8], st1_s[:, 3:4], st1_s[:, 4:5]))
                for _r in range(2)][-1][-1].then_inc(dv.h, 1)))
            nsd = sc.bump()
            S(lambda s, w=nev:
              (s.wait_ge(dv.h, w),
               s.activation(st1_s[:, 5:6], st1_s[:, 7:8], AF.Sqrt,
                            bias=eps_s).then_inc(sc.h, 1)))
            nap = dv.bump()
            V(lambda v, tt=tt, w=nsd:
              (v.wait_ge(sc.h, w),
               [v.memset(scr1_f[:, 0:1], 0.0) for _p in range(4)][-1],
               v.reciprocal(st1_s[:, 6:7], st1_s[:, 5:6]),
               v.reciprocal(st1_s[:, 6:7], st1_s[:, 5:6]),
               v.memset(scr1_f[:, 0:1], 0.0),
               v.memset(scr1_f[:, 0:1], 0.0),
               v.tensor_scalar(stg_f[tt % 2][:, 0:1024], scrf_f[:, 0:1024],
                               st1_s[:, 2:3], st1_s[:, 6:7],
                               ALU.subtract, ALU.mult).then_inc(dv.h, 1)))
            su.bump(16)
            Y(lambda q, tt=tt, w=nap, su=su:
              (q.wait_ge(dv.h, w),
               q.dma_start(out=out_h[128 * tt:128 * (tt + 1), :],
                           in_=stg_f[tt % 2]).then_inc(su.h, 16)))
            f6_out.append(su.n)

        Y(lambda q, a=sou[0].n, b=sou[1].n:
          (q.wait_ge(sou[0].h, a), q.wait_ge(sou[1].h, b),
           q.wait_ge(dbg_sem[0].h, dbg_sem[0].n) if _TAP is not None else None))

        globals()["_DBGS"] = dict(
            sld0=(sld[0].h, sld[0].n), sld1=(sld[1].h, sld[1].n),
            ld=(ld.h, ld.n), strx=(strx.h, strx.n), skt=(skt.h, skt.n),
            stq0=(stq[0].h, stq[0].n), stq1=(stq[1].h, stq[1].n),
            swg=(swg.h, swg.n), swu=(swu.h, swu.n), swd=(swd.h, swd.n),
            srb0=(srb[0].h, srb[0].n), srb1=(srb[1].h, srb[1].n),
            sxq0=(sxq[0].h, sxq[0].n), sxq1=(sxq[1].h, sxq[1].n),
            ssp0=(ssp[0].h, ssp[0].n), ssp1=(ssp[1].h, ssp[1].n),
            pe=(pe.h, pe.n), dv=(dv.h, dv.n), sc=(sc.h, sc.n), cc=(cc.h, cc.n),
            ldg=(ldg.h, ldg.n),
        )
        if _STOP is not None:
            cut = plan.marks[_STOP]
            for e in ENGS:
                plan.ops[e] = plan.ops[e][:cut[e]]

        def _phase_bounds(e):
            return sorted((v[e], k) for k, v in plan.marks.items())

        _esuf = {"sync": "y", "tensor": "t", "vector": "v",
                 "scalar": "c", "gpsimd": "g"}

        def _emit(eng_dev, e):
            bounds = _phase_bounds(e)
            bi = 0
            for i, fn in enumerate(plan.ops[e]):
                while bi < len(bounds) and i >= bounds[bi][0]:
                    bi += 1
                ph = bounds[bi][1] if bi < len(bounds) else "tail"
                with nc.named_scope(ph + "." + _esuf[e]):
                    fn(eng_dev)

        block = est.enter_context(nc.Block())

        @block.sync
        def _(q):
            _emit(q, "sync")

        @block.tensor
        def _(tn):
            _emit(tn, "tensor")

        @block.vector
        def _(v):
            _emit(v, "vector")

        @block.scalar
        def _(s):
            _emit(s, "scalar")

        @block.gpsimd
        def _(q):
            _emit(q, "gpsimd")

    return nc
def _tap_out(dbg_h, s, d):
    ap = d if d is not None else s
    if len(ap.shape) == 2 and ap.shape[0] <= 128:
        rows, cols = ap.shape
        eltb = {mybir.dt.float32: 4, mybir.dt.bfloat16: 2,
                mybir.dt.int32: 4}[ap.dtype]
        ncols = cols * eltb // 4
        return dbg_h[0:rows, 0:ncols].bitcast(ap.dtype)
    raise ValueError(ap.shape)


_NC = None


def _constants(core):
    b, h = divmod(core, 2)
    pos = np.arange(2048 * h, 2048 * (h + 1), dtype=np.float64)
    inv = 1.0 / (10000.0 ** (np.arange(0, 64, 2, dtype=np.float64) / 64.0))
    ang = pos[:, None] * inv[None, :]
    cosv = np.cos(ang).astype(np.float32)
    sinv = np.sin(ang).astype(np.float32)
    rows = np.arange(128) % 64 % 32
    cos_b = np.ascontiguousarray(cosv[:, rows].T).astype(ml_dtypes.bfloat16)
    sin_b = np.ascontiguousarray(sinv[:, rows].T).astype(ml_dtypes.bfloat16)
    kt = np.arange(128)[:, None]
    qt = np.arange(128)[None, :]
    maskt = (kt <= qt).astype(np.float32).astype(ml_dtypes.bfloat16)
    p2l = np.zeros((128, 128), np.float32)
    for base in (0, 64):
        for i in range(32):
            p2l[base + i + 32, base + i] = -1.0
            p2l[base + i, base + i + 32] = 1.0
    p2l = p2l.astype(ml_dtypes.bfloat16)
    onesc = np.ones((128, 1), ml_dtypes.bfloat16)
    onesr = np.ones((1, 128), np.float32)
    eo = np.zeros((128, 2), np.float32)
    eo[:, 0] = 1.0 if h == 0 else 0.0
    eo[:, 1] = 1.0 - eo[:, 0]
    return dict(cosb=cos_b, sinb=sin_b, maskt=maskt, p2l=p2l, onesc=onesc,
                onesr=onesr, eo=eo)


def kernel(L0_attn_x, L0_attn_Wq, L0_attn_Wk, L0_attn_Wv, L0_attn_Wo,
           L0_attn_Wg1, L0_attn_Wg2, L0_attn_ln_w, L0_attn_ln_b,
           L0_ffn_w_gate, L0_ffn_w_up, L0_ffn_w_down):
    global _NC
    if _NC is None:
        _NC = build_nc()
    x = np.ascontiguousarray(np.asarray(L0_attn_x, np.float32))
    lnw = np.ascontiguousarray(
        np.asarray(L0_attn_ln_w, np.float32).reshape(8, 128).T)
    lnb = np.ascontiguousarray(
        np.asarray(L0_attn_ln_b, np.float32).reshape(8, 128).T)
    base = dict(
        wq=np.ascontiguousarray(L0_attn_Wq, dtype=np.float32),
        wk=np.ascontiguousarray(L0_attn_Wk, dtype=np.float32),
        wv=np.ascontiguousarray(L0_attn_Wv, dtype=np.float32),
        wo=np.ascontiguousarray(L0_attn_Wo, dtype=np.float32),
        wg1=np.ascontiguousarray(L0_attn_Wg1, dtype=np.float32),
        wg2=np.ascontiguousarray(L0_attn_Wg2, dtype=np.float32),
        lnw=lnw, lnb=lnb,
        wgate=np.ascontiguousarray(L0_ffn_w_gate, dtype=np.float32),
        wup=np.ascontiguousarray(L0_ffn_w_up, dtype=np.float32),
        wdown=np.ascontiguousarray(L0_ffn_w_down, dtype=np.float32),
    )
    in_maps = []
    for c in range(8):
        b, h = divmod(c, 2)
        m = dict(base)
        m["x"] = np.ascontiguousarray(x[b, 2048 * h:2048 * (h + 1), :])
        m.update(_constants(c))
        in_maps.append(m)
    res = run_bass_kernel_spmd(_NC, in_maps, core_ids=list(range(8)))
    out = np.empty((4, 4096, 1024), np.float32)
    for c in range(8):
        b, h = divmod(c, 2)
        out[b, 2048 * h:2048 * (h + 1), :] = res.results[c]["out"]
    return out


if __name__ == "__main__":
    build_nc()
    print("build ok")


# version 11
# speedup vs baseline: 1.1190x; 1.0189x over previous
import numpy as np
import ml_dtypes
from contextlib import ExitStack

import concourse.bass as bass
import concourse.mybir as mybir
from concourse.bass_utils import run_bass_kernel_spmd

F32 = mybir.dt.float32
BF16 = mybir.dt.bfloat16
I32 = mybir.dt.int32
U8 = mybir.dt.uint8
ALU = mybir.AluOpType
AF = mybir.ActivationFunctionType
AX = mybir.AxisListType.X

T, NT, D, ND = 2048, 16, 1024, 8
NH, NHP, HD = 16, 8, 64
FF, NF = 3072, 24
C, NCH, SCT, NSC = 128, 16, 512, 4
QMAX = 32767.0
EPS = 1e-5
ARENA_B = 211456

ENGS = ("sync", "tensor", "vector", "scalar", "gpsimd")

_TAP = None
_STOP = None



class Sem:
    def __init__(self, st, nc, name):
        self.h = st.enter_context(nc.semaphore(name))
        self.n = 0

    def bump(self, k=1):
        self.n += k
        return self.n


class Plan:
    def __init__(self):
        self.ops = {e: [] for e in ENGS}
        self.marks = {}

    def mark(self, name):
        self.marks[name] = {e: len(self.ops[e]) for e in ENGS}


def build_nc():
    nc = bass.Bass("TRN2", target_bir_lowering=False)

    def inp(name, shape, dt=F32):
        return nc.dram_tensor(name, shape, dt, kind="ExternalInput").ap()

    x_h = inp("x", [T, D])
    wq_h, wk_h, wv_h, wo_h = (inp(n, [D, D]) for n in ("wq", "wk", "wv", "wo"))
    wg1_h, wg2_h = inp("wg1", [D, HD]), inp("wg2", [HD, D])
    lnw_h, lnb_h = inp("lnw", [128, ND]), inp("lnb", [128, ND])
    wgate_h, wup_h = inp("wgate", [FF, D]), inp("wup", [FF, D])
    wdown_h = inp("wdown", [D, FF])
    cos_h, sin_h = inp("cosb", [128, T], BF16), inp("sinb", [128, T], BF16)
    maskT_h = inp("maskt", [128, 128], BF16)
    p2l_h = inp("p2l", [128, 128], BF16)
    onesc_h = inp("onesc", [128, 1], BF16)
    onesr_h = inp("onesr", [1, 128])
    eo_h = inp("eo", [128, 2])
    out_h = nc.dram_tensor("out", [T, D], F32, kind="ExternalOutput").ap()

    st_in = nc.dram_tensor("st_in", [128, 512], F32).ap()
    st_out = nc.dram_tensor("st_out", [128, 512], F32).ap()
    bar_in = [nc.dram_tensor(f"bar{i}_in", [1, 8], F32).ap() for i in range(4)]
    bar_out = [nc.dram_tensor(f"bar{i}_out", [1, 8], F32).ap() for i in range(4)]
    g_spill = nc.dram_tensor("g_spill", [FF, T], BF16).ap()
    u_spill = nc.dram_tensor("u_spill", [FF, T], BF16).ap()
    x1_spill = nc.dram_tensor("x1_spill", [T, D], F32).ap()

    with ExitStack() as est:
        arena = est.enter_context(nc.sbuf_tensor("arena", [128, ARENA_B], U8))
        cur = [0]

        def carve(nbytes, align=64):
            off = (cur[0] + align - 1) // align * align
            cur[0] = off + nbytes
            assert cur[0] <= ARENA_B, f"arena overflow {cur[0]}"
            return off

        def vvw(off, nbytes, dt, rows=128, r0=0):
            return arena[r0:r0 + rows, off:off + nbytes].bitcast(dt)

        cos_b = vvw(carve(4096), 4096, BF16)
        sin_b = vvw(carve(4096), 4096, BF16)
        maskT_b = vvw(carve(256), 256, BF16)
        p2l_b = vvw(carve(256), 256, BF16)
        onesc_b = vvw(carve(64), 2, BF16)
        onesr_f = vvw(carve(512), 512, F32, rows=1)
        lnw_s = vvw(carve(64), 32, F32)
        lnb_s = vvw(carve(64), 32, F32)
        eo_s = vvw(carve(64), 8, F32)
        wg1_b = vvw(carve(1024), 1024, BF16)
        o_g1 = carve(4096)
        g1_b = vvw(o_g1, 4096, BF16, rows=64)
        mx_acc = vvw(carve(128), 128, F32)
        gm_acc = vvw(carve(128), 128, F32)
        um_acc = vvw(carve(128), 128, F32)
        row_s = vvw(carve(512), 512, F32, rows=1)
        sc_s = vvw(carve(64), 32, F32, rows=1)
        res_s = vvw(carve(64), 32, F32, rows=1)
        scl_bc = vvw(carve(64), 48, F32)
        st1_s = vvw(carve(64), 32, F32)
        o_mu = carve(2048)
        mu_row = vvw(o_mu, 2048, F32, rows=1)
        o_rr = carve(2048)
        rr_row = vvw(o_rr, 2048, F32, rows=1)
        tmp1_s = vvw(carve(64), 32, F32, rows=1)
        eps_s = vvw(carve(64), 4, F32)
        kf_j = vvw(carve(2048), 2048, BF16)
        v_j = vvw(carve(2048), 2048, BF16)
        o_scr1 = carve(4096)
        scr1_b = vvw(o_scr1, 4096, BF16)
        scr1_f = vvw(o_scr1, 4096, F32)
        kt_j = vvw(o_scr1 + 2048, 2048, BF16)
        e_sb = vvw(carve(4096), 4096, BF16)
        o_sz = carve(2048)
        s_z = vvw(o_sz, 2048, BF16)
        s_iz = s_z
        o_scrf = carve(4096)
        scrf_f = vvw(o_scrf, 4096, F32)
        scrf_i = vvw(o_scrf, 4096, I32)
        s_fin = vvw(o_scrf + 2048, 2048, F32)
        o_stg = carve(8192)
        stg_f = [vvw(o_stg, 4096, F32), vvw(o_stg + 4096, 4096, F32)]
        kf_z = [vvw(o_stg, 2048, BF16), vvw(o_stg + 2048, 2048, BF16)]
        wd_st = None
        wd_st = [vvw(o_mu, 2048, BF16), vvw(o_rr, 2048, BF16)]
        bm_s = vvw(carve(512), 384, F32)
        br_s = vvw(carve(512), 384, F32)
        globals()["_DBG"] = dict(scrf=o_scrf, stg=o_stg)

        o_biga = carve(32768)
        o_bigb = carve(32768)
        o_bige = carve(32768)
        o_wk = carve(16384)
        o_wv = carve(16384)
        o_wo = carve(16384)
        o_wcur = carve(16384)
        assert o_wk == o_bige + 32768 and o_wo == o_wv + 16384
        assert o_wcur == o_wo + 16384 and o_bigb == o_biga + 32768

        xT_b = vvw(o_biga, 32768, BF16)
        xqT_b = xT_b
        q_b = vvw(o_bigb, 32768, BF16)
        x_tmb = vvw(o_bige, 32768, BF16)
        og_b = vvw(o_bige, 32768, BF16)
        wk_b = vvw(o_wk, 16384, BF16)
        wv_b = vvw(o_wv, 16384, BF16)
        wo_b = vvw(o_wo, 16384, BF16)
        wq_b = vvw(o_wcur, 16384, BF16)
        wg2_b = vvw(o_wcur, 2048, BF16, rows=64)
        wgT_b = vvw(o_bige, 49152, BF16)
        wuT_b = vvw(o_wv, 49152, BF16)
        g2st = [vvw(o_bigb, 4096, BF16), vvw(o_bigb + 4096, 4096, BF16)]
        oost = [vvw(o_bigb + 8192, 4096, BF16),
                vvw(o_bigb + 12288, 4096, BF16)]
        o_tile = [vvw(o_biga, 6144, BF16), vvw(o_biga + 6144, 6144, BF16)]
        gq_c = vvw(o_bigb + 16384, 4096, BF16)
        sig_c = vvw(o_bigb + 16384 + 4096, 4096, BF16)
        sgq_c = vvw(o_bigb + 16384 + 8192, 4096, BF16)
        gq2 = [vvw(o_bigb + 16384, 4096, BF16),
               vvw(o_bigb + 20480, 4096, BF16)]
        sig2 = [vvw(o_bigb + 24576, 4096, BF16),
                vvw(o_bigb + 28672, 4096, BF16)]
        sgq_f4 = vvw(o_bigb + 8192, 4096, BF16)
        xqt_st = [vvw(o_bigb + 16384 + 12288, 2048, BF16),
                  vvw(o_bigb + 16384 + 14336, 2048, BF16)]
        wn6_b = vvw(o_bigb + 16384, 12288, BF16)

        pA = est.enter_context(nc.psum_tensor("pA", [128, 512], F32))
        pB = est.enter_context(nc.psum_tensor("pB", [128, 512], F32))
        pC = est.enter_context(nc.psum_tensor("pC", [128, 512], F32))
        pE1 = est.enter_context(nc.psum_tensor("pE1", [128, 512], F32))
        pE2 = est.enter_context(nc.psum_tensor("pE2", [128, 512], F32))
        pO = est.enter_context(nc.psum_tensor("pO", [128, 1024], F32))
        pS = est.enter_context(nc.psum_tensor("pS", [128, 512], F32))
        rot3 = (pA, pB, pC)

        ld = Sem(est, nc, "ld")
        sld = [Sem(est, nc, "sld0"), Sem(est, nc, "sld1")]
        strx = Sem(est, nc, "strx")
        skt = Sem(est, nc, "skt")
        stq = [Sem(est, nc, "stq0"), Sem(est, nc, "stq1")]
        swg = Sem(est, nc, "swg")
        swu = Sem(est, nc, "swu")
        swd = Sem(est, nc, "swd")
        sot = [Sem(est, nc, "sot0"), Sem(est, nc, "sot1")]
        srb = [Sem(est, nc, "srb0"), Sem(est, nc, "srb1")]
        sxq = [Sem(est, nc, "sxq0"), Sem(est, nc, "sxq1")]
        ssp = [Sem(est, nc, "ssp0"), Sem(est, nc, "ssp1")]
        sou = [Sem(est, nc, "sou0"), Sem(est, nc, "sou1")]
        ldg = Sem(est, nc, "ldg")
        pe = Sem(est, nc, "pe")
        dv = Sem(est, nc, "dv")
        sc = Sem(est, nc, "sc")
        cc = Sem(est, nc, "cc")

        plan = Plan()

        dbg_h = None
        dbg_sem = [None]
        if _TAP is not None:
            dbg_h = nc.dram_tensor("dbg", [128, 16384], F32,
                                   kind="ExternalOutput").ap()
            dbg_sem[0] = Sem(est, nc, "dbgsem")

        def Y(fn): plan.ops["sync"].append(fn)
        def TN(fn): plan.ops["tensor"].append(fn)

        def tap(name, src_ap, wait=None, dram_src=None):
            if _TAP != name:
                return
            ds = dbg_sem[0]
            ds.bump(16)
            Y(lambda q, s=src_ap, w=wait, d=dram_src:
              (w(q) if w else None,
               q.dma_start(out=dbg_h[:, 0:(d if d else s).free_size()
                               ] if False else _tap_out(dbg_h, s, d),
                           in_=(d if d is not None else s)).then_inc(ds.h, 16)))

        def V(fn): plan.ops["vector"].append(fn)
        def S(fn): plan.ops["scalar"].append(fn)
        def G(fn): plan.ops["gpsimd"].append(fn)

        dv.bump()
        V(lambda v: (v.memset(eps_s, EPS),
                     v.memset(sc_s, 0.0),
                     v.memset(res_s, 0.0),
                     v.memset(tmp1_s, 0.0),
                     v.memset(mx_acc, 0.0),
                     v.memset(gm_acc, 0.0),
                     v.memset(um_acc, 0.0),
                     v.memset(scl_bc, 0.0).then_inc(dv.h, 1)))

        for dst, src in ((cos_b, cos_h), (sin_b, sin_h), (maskT_b, maskT_h),
                         (p2l_b, p2l_h), (onesc_b, onesc_h), (onesr_f, onesr_h),
                         (lnw_s, lnw_h), (lnb_s, lnb_h), (eo_s, eo_h)):
            ld.bump(16)
            Y(lambda q, d=dst, s=src: q.dma_start(out=d, in_=s).then_inc(ld.h, 16))
        consts_ld = ld.n

        def load_cast(dst_b, src_h, nchunks, width, wslot):
            for k in range(nchunks):
                sl = sld[k % 2]
                wprev = dv.n - 1
                tgt = sl.bump(16)
                Y(lambda q, k=k, w=wprev, s=src_h, wd=width, sl=sl:
                  (q.wait_ge(dv.h, max(0, w)),
                   q.dma_start(out=stg_f[k % 2][:, :wd],
                               in_=s[128 * k:128 * (k + 1), :]
                               ).then_inc(sl.h, 16)))
                dv.bump()
                V(lambda v, k=k, tgt=tgt, d=dst_b, wd=width, ws=wslot, sl=sl:
                  (v.wait_ge(sl.h, tgt),
                   v.tensor_copy(d[:, ws * k:ws * k + wd],
                                 stg_f[k % 2][:, :wd]).then_inc(dv.h, 1)))

        load_cast(x_tmb, x_h, NT, 1024, 1024)
        x_cast_dv = dv.n

        for j in range(NT):
            strx.bump(16)
            Y(lambda q, j=j, w=x_cast_dv:
              (q.wait_ge(dv.h, w),
               q.dma_start_transpose(
                   out=xT_b[:].rearrange("p (k t) -> p k t", t=2048
                                         )[:, :, 128 * j:128 * (j + 1)],
                   in_=x_tmb[:, 1024 * j:1024 * (j + 1)].rearrange(
                       "t (k h) -> t k h", h=128),
               ).then_inc(strx.h, 16)))
        xT_tr = strx.n
        tap("xT", xT_b, wait=(lambda q: q.wait_ge(strx.h, xT_tr)))

        load_cast(wq_b, wq_h, ND, 1024, 1024)
        load_cast(wk_b, wk_h, ND, 1024, 1024)
        load_cast(wv_b, wv_h, ND, 1024, 1024)
        load_cast(wo_b, wo_h, ND, 1024, 1024)
        load_cast(wg1_b, wg1_h, ND, 64, 64)
        allw_dv = dv.n
        tap("wq", wq_b, wait=(lambda q, w=allw_dv: q.wait_ge(dv.h, w)))

        plan.mark("p0")
        def mm_group(ps_ap, pairs, pre=None):
            n = pe.bump()
            def f(tn, P=list(pairs), pre=pre, ps=ps_ap):
                if pre:
                    pre(tn)
                for i, (l, r) in enumerate(P):
                    ins = tn.matmul(ps, l, r,
                                    start=(i == 0), stop=(i == len(P) - 1))
                ins.then_inc(pe.h, 1)
            TN(f)
            return n

        g1_sc0 = sc.n
        g1_pes = []
        for tt in range(4):
            g1_pes.append(mm_group(
                rot3[tt % 3][0:64, 0:512],
                [(wg1_b[:, 64 * k:64 * (k + 1)],
                  xT_b[:, 2048 * k + 512 * tt:2048 * k + 512 * (tt + 1)])
                 for k in range(ND)],
                pre=(lambda tn, t=tt:
                     (tn.wait_ge(strx.h, xT_tr), tn.wait_ge(dv.h, allw_dv),
                      tn.wait_ge(ld.h, consts_ld),
                      tn.wait_ge(sc.h, g1_sc0 + t - 2) if t >= 3 else None)))
            )
        for tt in range(4):
            sc.bump()
            S(lambda s, t=tt, w=g1_pes[tt]:
              (s.wait_ge(pe.h, w),
               s.copy(g1_b[0:64, 512 * t:512 * (t + 1)],
                      rot3[t % 3][0:64, 0:512]).then_inc(sc.h, 1)))
        g1_sc = sc.n

        q_sc0 = sc.n
        for i in range(32):
            m, tt = divmod(i, 4)
            mm_group(
                rot3[i % 3][:, 0:512],
                [(wq_b[:, 1024 * k + 128 * m:1024 * k + 128 * (m + 1)],
                  xT_b[:, 2048 * k + 512 * tt:2048 * k + 512 * (tt + 1)])
                 for k in range(ND)],
                pre=(lambda tn, i=i:
                     (tn.wait_ge(sc.h, g1_sc),
                      tn.wait_ge(sc.h, q_sc0 + i - 2) if i >= 3 else None)))
            sc.bump()
            S(lambda s, i=i, m=m, tt=tt, w=pe.n:
              (s.wait_ge(pe.h, w),
               s.activation(q_b[:, 2048 * m + 512 * tt:2048 * m + 512 * (tt + 1)],
                            rot3[i % 3][:, 0:512], AF.Silu).then_inc(sc.h, 1)))
        q_sc = sc.n
        tap("qsilu", q_b, wait=(lambda q, w=q_sc: q.wait_ge(sc.h, w)))

        q_pe_done = pe.n
        sl = sld[0]
        wprev = dv.n - 1
        tgt = sl.bump(16)
        Y(lambda q, w=wprev, sl=sl, qp=q_pe_done:
          (q.wait_ge(dv.h, max(0, w)),
           q.wait_ge(pe.h, qp),
           q.dma_start(out=stg_f[0][0:64, :], in_=wg2_h).then_inc(sl.h, 16)))
        dv.bump()
        V(lambda v, tgt=tgt, sl=sl:
          (v.wait_ge(sl.h, tgt),
           v.tensor_copy(wg2_b[0:64, :], stg_f[0][0:64, :]).then_inc(dv.h, 1)))
        wg2_dv = dv.n

        def rope(buf, m_off, tok0, ntok, v_pre=None):
            u = scr1_b[:, 0:ntok]
            cw = cos_b[:, tok0:tok0 + ntok]
            sw = sin_b[:, tok0:tok0 + ntok]
            pe_at = pe.n
            n1 = dv.bump()
            V(lambda v, vp=v_pre, pa=pe_at:
              (vp(v) if vp else None,
               v.wait_ge(pe.h, pa),
               v.tensor_mul(u, buf[:, m_off:m_off + ntok], sw),
               v.tensor_mul(buf[:, m_off:m_off + ntok],
                            buf[:, m_off:m_off + ntok], cw).then_inc(dv.h, 1)))
            tap("u%d" % (m_off // 2048), u,
                wait=(lambda q, w=n1: q.wait_ge(dv.h, w)))
            nj = (ntok + 511) // 512
            add_dvs = []
            for j in range(nj):
                wdt = min(512, ntok - 512 * j)
                ps = (pA, pB)[j % 2]
                pn = pe.bump()
                TN(lambda tn, j=j, ps=ps, wdt=wdt, n1=n1,
                   ag=(add_dvs[j - 2] if j >= 2 else None):
                   (tn.wait_ge(dv.h, n1),
                    tn.wait_ge(dv.h, ag) if ag is not None else None,
                    tn.matmul(ps[:, 0:wdt], p2l_b, u[:, 512 * j:512 * j + wdt],
                              start=True, stop=True).then_inc(pe.h, 1)))
                add_dvs.append(dv.bump())
                V(lambda v, j=j, ps=ps, wdt=wdt, pn=pn, mo=m_off:
                  (v.wait_ge(pe.h, pn),
                   v.tensor_add(buf[:, mo + 512 * j:mo + 512 * j + wdt],
                                buf[:, mo + 512 * j:mo + 512 * j + wdt],
                                ps[:, 0:wdt]).then_inc(dv.h, 1)))

        for m in range(ND):
            rope(q_b, 2048 * m, 0, 2048,
                 v_pre=(lambda v, m=m, s0=q_sc - 32:
                        v.wait_ge(sc.h, s0 + 4 * (m + 1))))
        qrope_dv = dv.n
        tap("q", q_b, wait=(lambda q, w=qrope_dv: q.wait_ge(dv.h, w)))

        plan.mark("p1")
        dv.bump()
        V(lambda v:
          (v.memset(kf_z[0][64:128, :], 0.0),
           v.memset(kf_z[1][0:64, :], 0.0),
           v.memset(s_z[:].rearrange("p (hp c) -> p hp c", c=128)
                    [64:128, :, 0:64], 0.0),
           v.memset(s_z[:].rearrange("p (hp c) -> p hp c", c=128)
                    [0:64, :, 64:128], 0.0).then_inc(dv.h, 1)))
        og_sc_prev = [sc.n]
        for ci in range(NCH):
            tok0 = C * ci
            ksc0 = sc.n
            kq_dv = dv.n
            for bank in range(2):
                ps = (pA, pB)[bank]
                n = pe.bump()
                def kproj(tn, bank=bank, ps=ps, kq=kq_dv, tok0=tok0):
                    if bank == 0:
                        tn.wait_ge(dv.h, kq)
                    first = True
                    for mm in range(4):
                        m = 4 * bank + mm
                        for k in range(ND):
                            ins = tn.matmul(
                                ps[:, 128 * mm:128 * (mm + 1)],
                                wk_b[:, 1024 * k + 128 * m:
                                     1024 * k + 128 * (m + 1)],
                                xT_b[:, 2048 * k + tok0:2048 * k + tok0 + 128],
                                start=first,
                                stop=(mm == 3 and k == ND - 1))
                            first = False
                    ins.then_inc(pe.h, 1)
                TN(kproj)
                sc.bump()
                S(lambda s, bank=bank, w=pe.n, ps=ps:
                  (s.wait_ge(pe.h, w),
                   s.activation(kf_j[:, 512 * bank:512 * (bank + 1)],
                                ps[:, 0:512], AF.Silu).then_inc(sc.h, 1)))
            ksil_sc = sc.n
            if ci == 0:
                plan.mark("c0a")
            cw = cos_b[:, tok0:tok0 + 128].unsqueeze(1).broadcast_to(
                [128, ND, 128])
            sw = sin_b[:, tok0:tok0 + 128].unsqueeze(1).broadcast_to(
                [128, ND, 128])
            kf3 = kf_j[:].rearrange("p (j t) -> p j t", t=128)
            u3 = scr1_b[:, 0:1024].rearrange("p (j t) -> p j t", t=128)
            n1 = dv.bump()
            V(lambda v, w=ksil_sc, pa=pe.n, cw=cw, sw=sw, kf3=kf3, u3=u3:
              (v.wait_ge(sc.h, w),
               v.wait_ge(pe.h, pa),
               v.tensor_tensor(u3, kf3, sw, ALU.mult),
               v.tensor_tensor(kf3, kf3, cw, ALU.mult).then_inc(dv.h, 1)))
            if ci == 0:
                plan.mark("c0b")
            prev_add = [None]
            for g2 in range(2):
                pn = pe.bump()
                TN(lambda tn, g2=g2, n1=n1, ag=prev_add[0]:
                   (tn.wait_ge(dv.h, n1),
                    tn.wait_ge(dv.h, ag) if ag is not None else None,
                    [tn.matmul(pC[:, 128 * mm:128 * (mm + 1)], p2l_b,
                               scr1_b[:, 512 * g2 + 128 * mm:
                                      512 * g2 + 128 * (mm + 1)],
                               start=(mm == 0), stop=(mm == 3))
                     for mm in range(4)][-1].then_inc(pe.h, 1)))
                prev_add[0] = dv.bump()
                V(lambda v, g2=g2, w=pn:
                  (v.wait_ge(pe.h, w),
                   v.tensor_add(kf_j[:, 512 * g2:512 * (g2 + 1)],
                                kf_j[:, 512 * g2:512 * (g2 + 1)],
                                pC[:, 0:512]).then_inc(dv.h, 1)))
            krope_dv = dv.n
            kz_dv = dv.bump()
            V(lambda v:
              (v.tensor_copy(kf_z[0][0:64, :], kf_j[0:64, :]),
               v.tensor_copy(kf_z[1][64:128, :], kf_j[64:128, :]
                             ).then_inc(dv.h, 1)))
            if ci == 0:
                plan.mark("c0c")
            kt_tgt = skt.bump(16)
            Y(lambda q, w=krope_dv, pa=pe.n:
              (q.wait_ge(dv.h, w),
               q.wait_ge(pe.h, pa),
               q.dma_start_transpose(
                   out=kt_j[:].rearrange("t (j F) -> t j F", F=128),
                   in_=kf_j[:].rearrange("f (j t) -> f j t", t=128),
               ).then_inc(skt.h, 16)))
            if ci == 0:
                plan.mark("c0d")
            vsc0 = sc.n
            for hh in range(2):
                ps = (pA, pB)[hh]
                mm_group(
                    ps[:, 0:512],
                    [(xT_b[:, 2048 * k + tok0:2048 * k + tok0 + 128],
                      wv_b[:, 1024 * k + 512 * hh:1024 * k + 512 * (hh + 1)])
                     for k in range(ND)],
                    pre=(lambda tn, hh=hh, kq=krope_dv:
                         tn.wait_ge(dv.h, kq) if hh == 0 else None))
                sc.bump()
                S(lambda s, hh=hh, w=pe.n, ps=ps:
                  (s.wait_ge(pe.h, w),
                   s.copy(v_j[:, 512 * hh:512 * (hh + 1)],
                          ps[:, 0:512]).then_inc(sc.h, 1)))
            if ci == 0:
                plan.mark("c0e")
            v_sc = sc.n

            for gidx in range(4):
                psb = (pE1, pE2)[gidx % 2]
                dv_guard = dv.n
                n = pe.bump()
                def ef(tn, gidx=gidx, psb=psb, ci=ci,
                       ktw=kt_tgt, vsc=v_sc, dg=dv_guard, kz=kz_dv):
                    tn.wait_ge(sc.h, vsc)
                    tn.wait_ge(skt.h, ktw)
                    tn.wait_ge(dv.h, kz)
                    if gidx >= 2:
                        tn.wait_ge(dv.h, dg)
                    for hh in range(4):
                        h = 4 * gidx + hh
                        m, par = divmod(h, 2)
                        ins = tn.matmul(
                            psb[:, 128 * hh:128 * (hh + 1)],
                            kf_z[par][:, 128 * m:128 * (m + 1)],
                            q_b[:, 2048 * m + 128 * ci:
                                2048 * m + 128 * (ci + 1)],
                            start=(hh == 0), stop=(hh == 3))
                    ins.then_inc(pe.h, 1)
                TN(ef)
                dv.bump()
                V(lambda v, gidx=gidx, psb=psb, n=n:
                  (v.wait_ge(pe.h, n),
                   v.tensor_tensor(
                       e_sb[:, 512 * gidx:512 * (gidx + 1)].rearrange(
                           "p (j f) -> p j f", f=128),
                       psb[:, 0:512].rearrange("p (j f) -> p j f", f=128),
                       maskT_b.unsqueeze(1).broadcast_to([128, 4, 128]),
                       ALU.mult).then_inc(dv.h, 1)))
            if ci == 0:
                plan.mark("c0f")
            eev_dv = dv.n
            av_pe = pe.bump()
            def avf(tn, ci=ci, ev=eev_dv, ogp=og_sc_prev[0]):
                tn.wait_ge(dv.h, ev)
                tn.wait_ge(sc.h, ogp)
                seen = set()
                plan_mm = []
                for h in range(NH):
                    hp, par = divmod(h, 2)
                    plan_mm.append(("av", h, hp, par, (hp // 4, par)))
                if ci > 0 and not globals().get('_NO_INTER'):
                    for h in range(NH):
                        hp, par = divmod(h, 2)
                        plan_mm.append(("in", h, hp, par, (hp // 4, par)))
                lastidx = {}
                for idx, (_, _, _, _, grp) in enumerate(plan_mm):
                    lastidx[grp] = idx
                for idx, (kind, h, hp, par, grp) in enumerate(plan_mm):
                    first = grp not in seen
                    seen.add(grp)
                    stop = (lastidx[grp] == idx)
                    if kind == "av":
                        ins = tn.matmul(
                            pO[64 * par:64 * (par + 1), 128 * hp:128 * (hp + 1)],
                            v_j[:, 64 * h:64 * (h + 1)],
                            e_sb[:, 128 * h:128 * (h + 1)],
                            start=first, stop=stop,
                            skip_group_check=(par == 1),
                            tile_position=(0, 64 * par))
                    else:
                        ins = tn.matmul(
                            pO[64 * par:64 * (par + 1), 128 * hp:128 * (hp + 1)],
                            s_z[:, 64 * h:64 * (h + 1)],
                            q_b[:, 2048 * hp + 128 * ci:
                                2048 * hp + 128 * (ci + 1)],
                            start=first, stop=stop,
                            skip_group_check=(par == 1),
                            tile_position=(0, 64 * par))
                ins.then_inc(pe.h, 1)
            TN(avf)
            if ci == 0:
                plan.mark("c0g")
            st_pe = pe.bump()
            def stf(tn, ci=ci, sd=dv.n):
                tn.wait_ge(dv.h, sd)
                for h in range(NH):
                    hp, par = divmod(h, 2)
                    ins = tn.matmul(
                        pS[64 * par:64 * (par + 1), 64 * hp:64 * (hp + 1)],
                        kt_j[:, 64 * h:64 * (h + 1)],
                        v_j[:, 64 * h:64 * (h + 1)],
                        start=(h < 2), stop=(h >= NH - 2),
                        skip_group_check=(par == 1),
                        tile_position=(0, 64 * par))
                ins.then_inc(pe.h, 1)
            TN(stf)
            dv.bump()
            if ci == 0:
                V(lambda v, w=st_pe:
                  (v.wait_ge(pe.h, w),
                   v.tensor_copy(s_fin, pS[:, 0:512]),
                   v.tensor_copy(
                       s_z[:].rearrange("p (hp c) -> p hp c", c=128)
                       [0:64, :, 0:64],
                       s_fin[0:64, :].rearrange("p (hp c) -> p hp c", c=64)),
                   v.tensor_copy(
                       s_z[:].rearrange("p (hp c) -> p hp c", c=128)
                       [64:128, :, 64:128],
                       s_fin[64:128, :].rearrange("p (hp c) -> p hp c", c=64)
                   ).then_inc(dv.h, 1)))
            else:
                V(lambda v, w=st_pe:
                  (v.wait_ge(pe.h, w),
                   v.tensor_add(s_fin, s_fin, pS[:, 0:512]),
                   v.tensor_copy(
                       s_z[:].rearrange("p (hp c) -> p hp c", c=128)
                       [0:64, :, 0:64],
                       s_fin[0:64, :].rearrange("p (hp c) -> p hp c", c=64)),
                   v.tensor_copy(
                       s_z[:].rearrange("p (hp c) -> p hp c", c=128)
                       [64:128, :, 64:128],
                       s_fin[64:128, :].rearrange("p (hp c) -> p hp c", c=64)
                   ).then_inc(dv.h, 1)))
            for gg in range(2):
                sc.bump()
                S(lambda s, gg=gg, ci=ci, w=av_pe:
                  (s.wait_ge(pe.h, w),
                   s.copy(og_b[:].rearrange("p (hp t) -> p hp t", t=2048
                                            )[:, 4 * gg:4 * (gg + 1),
                                              128 * ci:128 * (ci + 1)],
                          pO[:, 512 * gg:512 * (gg + 1)].rearrange(
                              "p (j f) -> p j f", f=128)).then_inc(sc.h, 1)))
            og_sc_prev[0] = sc.n
            plan.mark(f"p3c{ci}")
        loop_sc = sc.n
        loop_pe = pe.n
        tap("og0", og_b, wait=(lambda q, w=loop_sc: q.wait_ge(sc.h, w)))
        tap("S", s_fin, wait=(lambda q, w=dv.n: q.wait_ge(dv.h, w)))

        plan.mark("p3")
        n1 = dv.bump()
        V(lambda v, w=loop_pe, dvp=dv.n - 1:
          (v.tensor_scalar(scrf_f[:, 0:512], s_fin, eo_s[:, 0:1], None,
                           ALU.mult).then_inc(dv.h, 1)))
        ldg.bump(16)
        G(lambda q, w=n1, t=ldg.n:
          (q.wait_ge(dv.h, w),
           q.dma_start(out=st_in, in_=scrf_f[:, 0:512]).then_inc(ldg.h, 16)))
        cc.bump()
        G(lambda q, w=ldg.n, t=cc.n:
          (q.wait_ge(ldg.h, w),
           q.collective_compute(
               "AllReduce", ALU.add,
               replica_groups=[[0, 1], [2, 3], [4, 5], [6, 7]],
               ins=[st_in.opt()], outs=[st_out.opt()]).then_inc(cc.h, 1)))
        ldg.bump(16)
        G(lambda q, w=cc.n, t=ldg.n:
          (q.wait_ge(cc.h, w),
           q.dma_start(out=scrf_f[:, 0:512], in_=st_out).then_inc(ldg.h, 16)))
        sini_dv = dv.bump()
        V(lambda v, w=ldg.n:
          (v.wait_ge(ldg.h, w),
           v.tensor_scalar(
               s_iz[:].rearrange("p (hp c) -> p hp c", c=128)[0:64, :, 0:64],
               scrf_f[0:64, 0:512].rearrange("p (hp c) -> p hp c", c=64),
               eo_s[0:64, 1:2], None, ALU.mult),
           v.tensor_scalar(
               s_iz[:].rearrange("p (hp c) -> p hp c", c=128)
               [64:128, :, 64:128],
               scrf_f[64:128, 0:512].rearrange("p (hp c) -> p hp c", c=64),
               eo_s[64:128, 1:2], None, ALU.mult).then_inc(dv.h, 1)))

        plan.mark("p4")
        first_pref = [True]
        for ci in range(NCH):
            for gg in range(2):
                psb = (pE1, pE2)[gg]
                dv_guard = dv.n
                n = pe.bump()
                def pf(tn, ci=ci, gg=gg, psb=psb, dg=dv_guard, fp=first_pref[0]):
                    tn.wait_ge(dv.h, sini_dv)
                    if fp:
                        tn.wait_ge(sc.h, loop_sc)
                    else:
                        tn.wait_ge(dv.h, dg)
                    for hh in range(4):
                        hp = 4 * gg + hh
                        for par in range(2):
                            h = 2 * hp + par
                            ins = tn.matmul(
                                psb[64 * par:64 * (par + 1),
                                    128 * hh:128 * (hh + 1)],
                                s_iz[:, 64 * h:64 * (h + 1)],
                                q_b[:, 2048 * hp + 128 * ci:
                                    2048 * hp + 128 * (ci + 1)],
                                start=(hh == 0), stop=(hh == 3),
                                skip_group_check=(par == 1),
                                tile_position=(0, 64 * par))
                    ins.then_inc(pe.h, 1)
                TN(pf)
                first_pref[0] = False
                dv.bump()
                V(lambda v, ci=ci, gg=gg, psb=psb, n=n:
                  (v.wait_ge(pe.h, n),
                   v.tensor_add(
                       og_b[:].rearrange("p (hp t) -> p hp t", t=2048
                                         )[:, 4 * gg:4 * (gg + 1),
                                           128 * ci:128 * (ci + 1)],
                       og_b[:].rearrange("p (hp t) -> p hp t", t=2048
                                         )[:, 4 * gg:4 * (gg + 1),
                                           128 * ci:128 * (ci + 1)],
                       psb[:, 0:512].rearrange("p (j f) -> p j f", f=128)
                   ).then_inc(dv.h, 1)))
        prefix_dv = dv.n
        tap("og1", og_b, wait=(lambda q, w=prefix_dv: q.wait_ge(dv.h, w)))
        og1_tap_n = dbg_sem[0].n if _TAP == "og1" else None
        tap("sini", s_iz, wait=(lambda q, w=sini_dv: q.wait_ge(dv.h, w)))

        plan.mark("p5")
        gate_dv_prev = [prefix_dv]
        for m in range(ND):
            gsc0 = sc.n
            gpes = []
            for tt in range(4):
                ps = rot3[tt % 3]
                gpes.append(pe.bump())
                TN(lambda tn, m=m, tt=tt, ps=ps, w=prefix_dv, g0=gsc0,
                   w2=wg2_dv,
                   scw={0: gsc0, 1: gsc0 - 2, 2: gsc0 - 1,
                        3: gsc0 + 1}[tt]:
                   (tn.wait_ge(dv.h, max(w, w2)),
                    tn.wait_ge(sc.h, max(0, scw)),
                    tn.matmul(ps[:, 0:512],
                              wg2_b[0:64, 128 * m:128 * (m + 1)],
                              g1_b[0:64, 512 * tt:512 * (tt + 1)],
                              start=True, stop=True, tile_position=(0, 0)
                              ).then_inc(pe.h, 1)))
            for tt in range(4):
                sc.bump()
                S(lambda s, m=m, tt=tt, w=gpes[tt], gd=gate_dv_prev[0]:
                  (s.wait_ge(pe.h, w),
                   s.wait_ge(dv.h, gd) if tt == 0 else None,
                   s.activation(scr1_b[:, 512 * tt:512 * (tt + 1)],
                                rot3[tt % 3][:, 0:512], AF.Sigmoid
                                ).then_inc(sc.h, 1)))
            gate_dv_prev[0] = dv.bump()
            V(lambda v, m=m, w=sc.n, tn_=og1_tap_n if m == 0 else None:
              (v.wait_ge(sc.h, w),
               v.wait_ge(dbg_sem[0].h, tn_) if tn_ is not None else None,
               v.tensor_mul(og_b[:, 2048 * m:2048 * (m + 1)],
                            og_b[:, 2048 * m:2048 * (m + 1)],
                            scr1_b).then_inc(dv.h, 1)))
        gated_dv = dv.n
        tap("og2", og_b, wait=(lambda q, w=gated_dv: q.wait_ge(dv.h, w)))
        og2_tap_n = dbg_sem[0].n if _TAP == "og2" else None

        plan.mark("p6")
        prev_stats_dv = [gated_dv]
        prev_nsq = [0]
        for tt in range(4):
            n_sum = pe.bump()
            TN(lambda tn, tt=tt, w=gated_dv, pv=prev_stats_dv[0]:
               (tn.wait_ge(dv.h, max(w, pv)),
                [tn.matmul(pE1[0:1, 0:512], onesc_b,
                           og_b[:, 2048 * m + 512 * tt:2048 * m + 512 * (tt + 1)],
                           start=(m == 0), stop=(m == ND - 1))
                 for m in range(ND)][-1].then_inc(pe.h, 1)))
            sq_scs = []
            for m in range(ND):
                sq_scs.append(sc.bump())
                S(lambda s, m=m, tt=tt, w=gated_dv,
                  pq=prev_nsq[0], dst=(scr1_b, e_sb)[m // 4]:
                  (s.wait_ge(dv.h, w),
                   s.wait_ge(pe.h, pq) if pq else None,
                   s.activation(dst[:, 512 * (m % 4):512 * (m % 4 + 1)],
                                og_b[:, 2048 * m + 512 * tt:
                                     2048 * m + 512 * (tt + 1)],
                                AF.Square).then_inc(sc.h, 1)))
            n_sq = pe.bump()
            TN(lambda tn, tt=tt, sqs=list(sq_scs), pv=prev_stats_dv[0]:
               (tn.wait_ge(dv.h, pv),
                [(tn.wait_ge(sc.h, sqs[m]),
                  tn.matmul(pE2[0:1, 0:512], onesc_b,
                            (scr1_b, e_sb)[m // 4][:,
                                512 * (m % 4):512 * (m % 4 + 1)],
                            start=(m == 0), stop=(m == ND - 1)))
                 for m in range(ND)][-1][1].then_inc(pe.h, 1)))
            prev_nsq[0] = n_sq
            nst = dv.bump()
            V(lambda v, w2=n_sq:
              (v.wait_ge(pe.h, w2),
               v.tensor_scalar(mu_row[0:1, 0:512], pE1[0:1, 0:512],
                               1.0 / D, None, ALU.mult),
               v.tensor_scalar(rr_row[0:1, 0:512], pE2[0:1, 0:512],
                               1.0 / D, None, ALU.mult),
               v.tensor_mul(scrf_f[0:1, 0:512], mu_row[0:1, 0:512],
                            mu_row[0:1, 0:512]),
               v.tensor_sub(rr_row[0:1, 0:512], rr_row[0:1, 0:512],
                            scrf_f[0:1, 0:512]).then_inc(dv.h, 1)))
            prev_stats_dv[0] = nst
            nsd = sc.bump()
            S(lambda s, w=nst:
              (s.wait_ge(dv.h, w),
               s.activation(rr_row[0:1, 0:512], rr_row[0:1, 0:512],
                            AF.Sqrt, bias=eps_s[0:1, 0:1]).then_inc(sc.h, 1)))
            nrc = dv.bump()
            V(lambda v, w=nsd:
              (v.wait_ge(sc.h, w),
               v.reciprocal(rr_row[0:1, 0:512], rr_row[0:1, 0:512]
                            ).then_inc(dv.h, 1)))
            nbc = pe.bump()
            TN(lambda tn, w=nrc:
               (tn.wait_ge(dv.h, w),
                tn.matmul(pA[:, 0:512], onesr_f, mu_row[0:1, 0:512],
                          start=True, stop=True),
                tn.matmul(pB[:, 0:512], onesr_f, rr_row[0:1, 0:512],
                          start=True, stop=True).then_inc(pe.h, 1)))
            for m in range(ND):
                nap = dv.bump()
                V(lambda v, m=m, tt=tt, w=nbc,
                  tn_=og2_tap_n if (m == 0 and tt == 0) else None:
                  (v.wait_ge(pe.h, w),
                   v.wait_ge(dbg_sem[0].h, tn_) if tn_ is not None else None,
                   v.scalar_tensor_tensor(
                       scrf_f[:, 0:512], pA[:, 0:512], -1.0,
                       og_b[:, 2048 * m + 512 * tt:2048 * m + 512 * (tt + 1)],
                       ALU.mult, ALU.add),
                   v.tensor_mul(scrf_f[:, 512:1024], scrf_f[:, 0:512],
                                pB[:, 0:512]),
                   v.tensor_scalar(og_b[:, 2048 * m + 512 * tt:
                                        2048 * m + 512 * (tt + 1)],
                                   scrf_f[:, 512:1024],
                                   lnw_s[:, m:m + 1], lnb_s[:, m:m + 1],
                                   ALU.mult, ALU.add).then_inc(dv.h, 1)))
            prev_stats_dv[0] = dv.n
        y_dv = dv.n
        tap("y", og_b, wait=(lambda q, w=y_dv: q.wait_ge(dv.h, w)))

        plan.mark("p7")
        for tt in range(NT):
            sl = sld[tt % 2]
            sp = ssp[tt % 2]
            sp_guard = sp.n
            x_tgt = sl.bump(16)
            Y(lambda q, tt=tt, yd=y_dv, spg=sp_guard, sl=sl, sp=sp,
              nv=dv.n:
              (q.wait_ge(dv.h, yd) if tt == 0 else None,
               q.wait_ge(dv.h, nv) if tt == 1 else None,
               q.wait_ge(sp.h, spg) if tt >= 2 else None,
               q.dma_start(out=stg_f[tt % 2],
                           in_=x_h[128 * tt:128 * (tt + 1), :]
                           ).then_inc(sl.h, 16)))
            wo_pes = []
            for hh in range(2):
                ps = (pA, pB)[hh]
                wo_pes.append(pe.bump())
                TN(lambda tn, tt=tt, hh=hh, ps=ps, yd=y_dv, dvw=dv.n:
                   (tn.wait_ge(dv.h, yd),
                    tn.wait_ge(dv.h, dvw) if tt > 0 else None,
                    [tn.matmul(ps[:, 0:512],
                               og_b[:, 2048 * m + 128 * tt:
                                    2048 * m + 128 * (tt + 1)],
                               wo_b[:, 1024 * m + 512 * hh:
                                    1024 * m + 512 * (hh + 1)],
                               start=(m == 0), stop=(m == ND - 1))
                     for m in range(ND)][-1].then_inc(pe.h, 1)))
            nev = dv.bump()
            V(lambda v, tt=tt, w=wo_pes, xt=x_tgt, sl=sl:
              (v.wait_ge(pe.h, w[1]),
               v.wait_ge(sl.h, xt),
               v.scalar_tensor_tensor(scrf_f[:, 0:512], pA[:, 0:512], 0.0,
                                      stg_f[tt % 2][:, 0:512],
                                      ALU.bypass, ALU.add),
               v.scalar_tensor_tensor(scrf_f[:, 512:1024], pB[:, 0:512], 0.0,
                                      stg_f[tt % 2][:, 512:1024],
                                      ALU.bypass, ALU.add),
               v.reduce_sum(st1_s[:, 0:1], scrf_f[:, 0:1024], AX),
               v.tensor_mul(scr1_f[:, 0:1024], scrf_f[:, 0:1024],
                            scrf_f[:, 0:1024]),
               v.reduce_sum(st1_s[:, 1:2], scr1_f[:, 0:1024], AX),
               [(v.tensor_scalar(st1_s[:, 2:3], st1_s[:, 0:1], 1.0 / D,
                                 None, ALU.mult),
                 v.tensor_scalar(st1_s[:, 3:4], st1_s[:, 1:2], 1.0 / D,
                                 None, ALU.mult),
                 v.tensor_mul(st1_s[:, 4:5], st1_s[:, 2:3], st1_s[:, 2:3]),
                 v.tensor_sub(st1_s[:, 7:8], st1_s[:, 3:4], st1_s[:, 4:5]))
                for _r in range(2)][-1][-1].then_inc(dv.h, 1)))
            if tt == 0 and _TAP in ("p8s", "p8x"):
                ds = dbg_sem[0]
                ds.bump(16)
                Y(lambda q, w=nev, ds=ds:
                  (q.wait_ge(dv.h, w),
                   q.dma_start(out=dbg_h[:, 0:1024].bitcast(F32),
                               in_=(scrf_f[:, 0:1024] if _TAP == "p8s"
                                    else stg_f[0][:, 0:1024])
                               ).then_inc(ds.h, 16)))
            nsd = sc.bump()
            S(lambda s, w=nev:
              (s.wait_ge(dv.h, w),
               s.activation(st1_s[:, 5:6], st1_s[:, 7:8], AF.Sqrt,
                            bias=eps_s).then_inc(sc.h, 1)))
            nap = dv.bump()
            V(lambda v, tt=tt, w=nsd, dbt=(tt == 0 and _TAP in ("p8s", "p8x")):
              (v.wait_ge(sc.h, w),
               v.wait_ge(dbg_sem[0].h, dbg_sem[0].n) if dbt else None,
               [v.memset(scr1_f[:, 0:1], 0.0) for _p in range(4)][-1],
               v.reciprocal(st1_s[:, 6:7], st1_s[:, 5:6]),
               v.reciprocal(st1_s[:, 6:7], st1_s[:, 5:6]),
               v.memset(scr1_f[:, 0:1], 0.0),
               v.memset(scr1_f[:, 0:1], 0.0),
               v.tensor_scalar(stg_f[tt % 2][:, 0:1024], scrf_f[:, 0:1024],
                               st1_s[:, 2:3], st1_s[:, 6:7],
                               ALU.subtract, ALU.mult),
               v.tensor_reduce(mx_acc[:, tt:tt + 1], stg_f[tt % 2][:, 0:1024],
                               AX, ALU.max, apply_absolute_value=True
                               ).then_inc(dv.h, 1)))
            sp.bump(16)
            S(lambda s, tt=tt, w=nap, sp=sp:
              (s.wait_ge(dv.h, w),
               s.dma_start(out=x1_spill[128 * tt:128 * (tt + 1), :],
                           in_=stg_f[tt % 2]).then_inc(sp.h, 16)))
        x1_dv = dv.n
        x1_sp = (ssp[0].n, ssp[1].n)
        if _TAP == "x1f":
            ds = dbg_sem[0]
            ds.bump(16)
            Y(lambda q, a=ssp[0].n, b=ssp[1].n, ds=ds:
              (q.wait_ge(ssp[0].h, a), q.wait_ge(ssp[1].h, b),
               q.dma_start(
                   out=dbg_h.rearrange("p (a d) -> a p d", d=1024),
                   in_=x1_spill.rearrange("(a p) d -> a p d", p=128),
               ).then_inc(ds.h, 16)))
        tap("x1", None, wait=(lambda q, a=ssp[0].n, b=ssp[1].n:
                              (q.wait_ge(ssp[0].h, a), q.wait_ge(ssp[1].h, b))),
            dram_src=x1_spill[0:128, :])

        plan.mark("p8")
        def barrier(bidx, srcs, lane_base):
            nred = None
            for i, (acc, K) in enumerate(srcs):
                n1b = dv.bump()
                V(lambda v, acc=acc, K=K:
                  (v.tensor_reduce(acc[:, 31:32], acc[:, 0:K], AX, ALU.max,
                                   apply_absolute_value=True),
                   v.tensor_reduce(acc[:, 31:32], acc[:, 0:K], AX, ALU.max,
                                   apply_absolute_value=True
                                   ).then_inc(dv.h, 1)))
                ldg.bump(16)
                G(lambda q, acc=acc, w=n1b, t=ldg.n - 16:
                  (q.wait_ge(dv.h, w),
                   q.wait_ge(ldg.h, t),
                   q.dma_start(out=row_s, in_=acc[:, 31:32]).then_inc(ldg.h, 16)))
                nred = dv.bump()
                V(lambda v, i=i, w=ldg.n:
                  (v.wait_ge(ldg.h, w),
                   v.tensor_reduce(sc_s[0:1, i:i + 1], row_s, AX, ALU.max
                                   ).then_inc(dv.h, 1)))
            ldg.bump(16)
            G(lambda q, b=bidx, w=nred:
              (q.wait_ge(dv.h, w),
               q.dma_start(out=bar_in[b], in_=sc_s[0:1, 0:8]).then_inc(ldg.h, 16)))
            cc.bump()
            G(lambda q, b=bidx, w=ldg.n:
              (q.wait_ge(ldg.h, w),
               q.collective_compute(
                   "AllReduce", ALU.max, replica_groups=[list(range(8))],
                   ins=[bar_in[b].opt()], outs=[bar_out[b].opt()]
               ).then_inc(cc.h, 1)))
            ldg.bump(16)
            G(lambda q, b=bidx, w=cc.n:
              (q.wait_ge(cc.h, w),
               q.dma_start(out=res_s, in_=bar_out[b]).then_inc(ldg.h, 16)))
            nsc_ = dv.bump()
            V(lambda v, ns=len(srcs), w=ldg.n:
              (v.wait_ge(ldg.h, w),
               [v.tensor_scalar(tmp1_s[0:1, 2 * i:2 * i + 1],
                                res_s[0:1, i:i + 1], 1.0 / QMAX, 1e-12,
                                ALU.mult, ALU.max) for i in range(ns)],
               [v.memset(tmp1_s[0:1, 7:8], 0.0) for _p in range(4)],
               [(v.reciprocal(tmp1_s[0:1, 2 * i + 1:2 * i + 2],
                              tmp1_s[0:1, 2 * i:2 * i + 1]),
                 v.reciprocal(tmp1_s[0:1, 2 * i + 1:2 * i + 2],
                              tmp1_s[0:1, 2 * i:2 * i + 1]))
                for i in range(ns)][-1][1].then_inc(dv.h, 1)))
            nbc = pe.bump()
            TN(lambda tn, ns=len(srcs), w=nsc_:
               (tn.wait_ge(dv.h, w),
                tn.matmul(pC[:, 0:2 * ns], onesr_f, tmp1_s[0:1, 0:2 * ns],
                          start=True, stop=True).then_inc(pe.h, 1)))
            nfin = dv.bump()
            V(lambda v, ns=len(srcs), lb=lane_base, w=nbc:
              (v.wait_ge(pe.h, w),
               v.tensor_copy(scl_bc[:, lb:lb + 2 * ns], pC[:, 0:2 * ns]
                             ).then_inc(dv.h, 1)))
            return nfin

        pre_bar_dv = dv.n
        bar1_dv = barrier(0, [(mx_acc, NT)], 0)
        if _TAP == "bar1d":
            ds = dbg_sem[0]
            ds.bump(96)
            Y(lambda q, w=bar1_dv, ds=ds:
              (q.wait_ge(dv.h, w),
               q.dma_start(out=dbg_h[:, 0:32].bitcast(F32),
                           in_=mx_acc[:, 0:32]).then_inc(ds.h, 16),
               q.dma_start(out=dbg_h[0:1, 32:160].bitcast(F32),
                           in_=row_s).then_inc(ds.h, 16),
               q.dma_start(out=dbg_h[0:1, 160:168].bitcast(F32),
                           in_=sc_s).then_inc(ds.h, 16),
               q.dma_start(out=dbg_h[0:1, 192:200].bitcast(F32),
                           in_=res_s).then_inc(ds.h, 16),
               q.dma_start(out=dbg_h[0:1, 224:232].bitcast(F32),
                           in_=tmp1_s).then_inc(ds.h, 16),
               q.dma_start(out=dbg_h[:, 256:268].bitcast(F32),
                           in_=scl_bc[:, 0:12]).then_inc(ds.h, 16)))

        plan.mark("bar1")

        plan.mark("wprep")
        p9_dv0 = pre_bar_dv
        for tt in range(NT):
            sx = sxq[tt % 2]
            sq_ = stq[tt % 2]
            tr_guard = sq_.n
            x_tgt = sx.bump(16)
            Y(lambda q, tt=tt, d0=p9_dv0, spx=x1_sp, sx=sx,
              pv=(dv.n - 1 if tt >= 2 else None):
              ((q.wait_ge(ssp[0].h, spx[0]), q.wait_ge(ssp[1].h, spx[1]))
               if tt == 0 else None,
               q.wait_ge(dv.h, d0) if tt < 2 else q.wait_ge(dv.h, pv),
               q.dma_start(out=stg_f[tt % 2],
                           in_=x1_spill[128 * tt:128 * (tt + 1), :]
                           ).then_inc(sx.h, 16)))
            nq = dv.bump()
            V(lambda v, tt=tt, xt=x_tgt, b1=bar1_dv, sx=sx, sq_=sq_,
              tg=tr_guard:
              (v.wait_ge(sx.h, xt),
               v.wait_ge(dv.h, b1) if tt == 0 else None,
               v.wait_ge(sq_.h, tg) if tt >= 2 else None,
               v.tensor_scalar(scrf_i[:, 0:1024], stg_f[tt % 2],
                               scl_bc[:, 1:2], None, ALU.mult),
               v.tensor_scalar(xqt_st[tt % 2], scrf_i[:, 0:1024],
                               scl_bc[:, 0:1], None, ALU.mult).then_inc(dv.h, 1)))
            sq_.bump(16)
            Y(lambda q, tt=tt, w=nq, lp=loop_pe, sq_=sq_:
              (q.wait_ge(dv.h, w),
               q.dma_start_transpose(
                   out=xqT_b[:].rearrange("p (j t) -> p j t", t=2048
                                          )[:, :, 128 * tt:128 * (tt + 1)],
                   in_=xqt_st[tt % 2].rearrange("t (j h) -> t j h", h=128),
               ).then_inc(sq_.h, 16)))
        xq_tr = (stq[0].n, stq[1].n)
        tap("xqT", xqT_b, wait=(lambda q, a=stq[0].n, b=stq[1].n:
                                (q.wait_ge(stq[0].h, a), q.wait_ge(stq[1].h, b))))
        tap("sclx", scl_bc, wait=(lambda q, w=bar1_dv: q.wait_ge(dv.h, w)))

        plan.mark("p9")
        pe_at = pe.n
        sc_at = sc.n
        p9_last = dv.n
        evac_sc0 = sc.n
        prep_prev_dv = [p9_last, p9_last]

        def prep_chunk(ci, src_h, dst_b, tsem, half):
            sl = sld[half]
            tgt = sl.bump(16)
            Y(lambda q, w=prep_prev_dv[half], s=src_h, ci=ci, sl=sl:
              (q.wait_ge(dv.h, w),
               q.dma_start(out=stg_f[half],
                           in_=s[128 * ci:128 * (ci + 1), :]
                           ).then_inc(sl.h, 16)))
            ncast = dv.bump()
            prep_prev_dv[half] = ncast
            wn0 = 2048 * half + 1024 * (ci % 2)
            V(lambda v, ci=ci, tgt=tgt, sl=sl, wn0=wn0, tsem=tsem,
              tg=16 * (ci - 1), first=(ci == 0), half=half:
              (v.wait_ge(sl.h, tgt),
               (v.wait_ge(pe.h, pe_at), v.wait_ge(sc.h, sc_at))
               if first else None,
               v.wait_ge(tsem.h, tg) if ci >= 2 else None,
               v.tensor_reduce(
                   bm_s[:, 0:32],
                   stg_f[half].rearrange("p (b c) -> p b c", c=32),
                   AX, ALU.max, apply_absolute_value=True),
               v.tensor_scalar(bm_s[:, 32:64], bm_s[:, 0:32],
                               1.0 / 7.0, 1e-12, ALU.mult, ALU.max),
               v.tensor_scalar(bm_s[:, 32:64], bm_s[:, 0:32],
                               1.0 / 7.0, 1e-12, ALU.mult, ALU.max),
               [v.memset(bm_s[:, 64:65], 0.0) for _p in range(4)][-1],
               v.reciprocal(br_s[:, 0:32], bm_s[:, 32:64]),
               v.reciprocal(br_s[:, 0:32], bm_s[:, 32:64]),
               v.memset(bm_s[:, 64:65], 0.0),
               v.memset(bm_s[:, 64:65], 0.0),
               v.tensor_tensor(
                   scrf_i[:, 0:1024].rearrange("p (b c) -> p b c", c=32),
                   stg_f[half].rearrange("p (b c) -> p b c", c=32),
                   br_s[:, 0:32].unsqueeze(2).broadcast_to([128, 32, 32]),
                   ALU.mult),
               v.scalar_tensor_tensor(
                   wn6_b[:, wn0:wn0 + 1024].rearrange(
                       "p (b c) -> p b c", c=32),
                   scrf_i[:, 0:1024].rearrange("p (b c) -> p b c", c=32),
                   0.0,
                   bm_s[:, 32:64].unsqueeze(2).broadcast_to([128, 32, 32]),
                   ALU.bypass, ALU.mult).then_inc(dv.h, 1)))
            tsem.bump(16)
            Y(lambda q, ci=ci, w=ncast, d=dst_b, wn0=wn0, tsem=tsem,
              first=(ci == 0):
              (q.wait_ge(dv.h, w),
               (q.wait_ge(pe.h, pe_at), q.wait_ge(sc.h, sc_at))
               if first else None,
               q.dma_start_transpose(
                   out=d[:].rearrange("p (hc i) -> p hc i", i=3072
                                      )[:, :, 128 * ci:128 * (ci + 1)],
                   in_=wn6_b[:, wn0:wn0 + 1024].rearrange(
                       "i (hc h) -> i hc h", h=128),
               ).then_inc(tsem.h, 16)))

        gcnt = [0]

        def f23_m(m, wT_b, acc, stgc, spsem, tsem, spill):
            mgs = []
            mpes = []
            for tt in range(4):
                g = gcnt[0]
                gcnt[0] += 1
                mgs.append(g)
                ps = rot3[g % 3]
                mpes.append(pe.bump())
                TN(lambda tn, m=m, tt=tt, g=g, ps=ps, wT=wT_b, tsem=tsem:
                   (tn.wait_ge(tsem.h, 16 * (m + 1)),
                    (tn.wait_ge(stq[0].h, xq_tr[0]),
                     tn.wait_ge(stq[1].h, xq_tr[1])) if g == 0 else None,
                    tn.wait_ge(sc.h, evac_sc0 + g - 2) if g >= 3 else None,
                    [tn.matmul(ps[:, 0:512],
                               wT[:, 3072 * k + 128 * m:
                                  3072 * k + 128 * (m + 1)],
                               xqT_b[:, 2048 * k + 512 * tt:
                                     2048 * k + 512 * (tt + 1)],
                               start=(k == 0), stop=(k == ND - 1))
                     for k in range(ND)][-1].then_inc(pe.h, 1)))
            spg = spsem.n
            for tt in range(4):
                sc.bump()
                S(lambda s, m=m, tt=tt, g=mgs[tt], w=mpes[tt], stgc=stgc,
                  sp=spsem, spg=spg:
                  (s.wait_ge(pe.h, w),
                   s.wait_ge(sp.h, spg) if (tt == 0 and m >= 1) else None,
                   s.copy(stgc[:, 512 * tt:512 * (tt + 1)],
                          rot3[g % 3][:, 0:512]).then_inc(sc.h, 1)))
            nmx = dv.bump()
            V(lambda v, m=m, w=sc.n, stgc=stgc, acc=acc:
              (v.wait_ge(sc.h, w),
               v.tensor_reduce(acc[:, m % 24:m % 24 + 1], stgc, AX,
                               ALU.max, apply_absolute_value=True
                               ).then_inc(dv.h, 1)))
            spsem.bump(16)
            S(lambda s, m=m, w=nmx, stgc=stgc, spd=spill, sp=spsem:
              (s.wait_ge(dv.h, w),
               s.dma_start(out=spd[128 * m:128 * (m + 1), :], in_=stgc
                           ).then_inc(sp.h, 16)))

        for m in range(NF):
            prep_chunk(m, wgate_h, wgT_b, swg, 0)
            prep_chunk(m, wup_h, wuT_b, swu, 1)
            f23_m(m, wgT_b, gm_acc, scr1_b, ssp[0], swg, g_spill)
            f23_m(m, wuT_b, um_acc, e_sb, ssp[1], swu, u_spill)
        wgT_tr = swg.n
        wuT_tr = swu.n
        fmm_sp = (ssp[0].n, ssp[1].n)
        tap("graw", None, wait=(lambda q, a=ssp[0].n, b=ssp[1].n:
                                (q.wait_ge(ssp[0].h, a), q.wait_ge(ssp[1].h, b))),
            dram_src=g_spill[0:128, :])
        tap("uraw", None, wait=(lambda q, a=ssp[0].n, b=ssp[1].n:
                                (q.wait_ge(ssp[0].h, a), q.wait_ge(ssp[1].h, b))),
            dram_src=u_spill[0:128, :])
        tap("wgT", wgT_b, wait=(lambda q, w=wgT_tr: q.wait_ge(swg.h, w)))

        bar2_dv = barrier(1, [(gm_acc, 24), (um_acc, 24)], 2)

        plan.mark("f23")
        wd_pe0 = pe.n
        wd_prev = [dv.n, dv.n]

        def wd_item(i):
            hc, th = divmod(i, 3)
            sl = sld[i % 2]
            tgt = sl.bump(16)
            Y(lambda q, i=i, hc=hc, th=th, w=wd_prev[i % 2], fs=fmm_sp,
              sl=sl:
              ((q.wait_ge(ssp[0].h, fs[0]), q.wait_ge(ssp[1].h, fs[1]))
               if i == 0 else None,
               q.wait_ge(dv.h, w),
               q.dma_start(out=stg_f[i % 2],
                           in_=wdown_h[128 * hc:128 * (hc + 1),
                                       1024 * th:1024 * (th + 1)]
                           ).then_inc(sl.h, 16)))
            ndq = dv.bump()
            wd_prev[i % 2] = ndq
            V(lambda v, i=i, tgt=sl.n, sl=sl, tg=16 * (i - 1):
              (v.wait_ge(sl.h, tgt),
               v.wait_ge(swd.h, tg) if i >= 2 else None,
               v.tensor_reduce(
                   bm_s[:, 0:32],
                   stg_f[i % 2].rearrange("p (b c) -> p b c", c=32),
                   AX, ALU.max, apply_absolute_value=True),
               v.tensor_scalar(bm_s[:, 32:64], bm_s[:, 0:32], 1.0 / 7.0,
                               1e-12, ALU.mult, ALU.max),
               v.tensor_scalar(bm_s[:, 32:64], bm_s[:, 0:32], 1.0 / 7.0,
                               1e-12, ALU.mult, ALU.max),
               [v.memset(bm_s[:, 64:65], 0.0) for _p in range(4)][-1],
               v.reciprocal(br_s[:, 0:32], bm_s[:, 32:64]),
               v.reciprocal(br_s[:, 0:32], bm_s[:, 32:64]),
               v.memset(bm_s[:, 64:65], 0.0),
               v.memset(bm_s[:, 64:65], 0.0),
               v.tensor_tensor(
                   scrf_i[:, 0:1024].rearrange("p (b c) -> p b c", c=32),
                   stg_f[i % 2].rearrange("p (b c) -> p b c", c=32),
                   br_s[:, 0:32].unsqueeze(2).broadcast_to([128, 32, 32]),
                   ALU.mult),
               v.scalar_tensor_tensor(
                   wd_st[i % 2][:].rearrange("p (b c) -> p b c", c=32),
                   scrf_i[:, 0:1024].rearrange("p (b c) -> p b c", c=32),
                   0.0,
                   bm_s[:, 32:64].unsqueeze(2).broadcast_to([128, 32, 32]),
                   ALU.bypass, ALU.mult).then_inc(dv.h, 1)))
            swd.bump(16)
            Y(lambda q, i=i, hc=hc, th=th, w=ndq, pa=wd_pe0:
              (q.wait_ge(dv.h, w),
               q.wait_ge(pe.h, pa) if i == 0 else None,
               q.dma_start_transpose(
                   out=wgT_b[:, 0:24576].rearrange(
                       "p (j F) -> p j F", F=1024
                   )[:, 8 * th:8 * (th + 1), 128 * hc:128 * (hc + 1)],
                   in_=wd_st[i % 2][:].rearrange("h (j i2) -> h j i2",
                                                 i2=128),
               ).then_inc(swd.h, 16)))

        plan.mark("wdprep")
        f4_n1 = []
        f4_ns = []

        def f4_stage2(mm):
            f4_ns.append(sc.bump())
            S(lambda s, mm=mm, w=f4_n1[mm]:
              (s.wait_ge(dv.h, w),
               s.activation(sig2[mm % 2], gq2[mm % 2],
                            AF.Sigmoid).then_inc(sc.h, 1)))
            sp = ssp[mm % 2]
            sp_guard = sp.n
            n2 = dv.bump()
            V(lambda v, mm=mm, w=f4_ns[mm], sp=sp, spg=sp_guard:
              (v.wait_ge(sc.h, w),
               v.wait_ge(sp.h, spg) if mm >= 2 else None,
               [(v.tensor_scalar(scrf_i[:, 0:1024],
                                 sig2[mm % 2][:, 1024 * hf:1024 * (hf + 1)],
                                 65536.0, None, ALU.mult),
                 v.tensor_scalar(sgq_f4[:, 1024 * hf:1024 * (hf + 1)],
                                 scrf_i[:, 0:1024], 65535.0,
                                 1.0 / 65536.0, ALU.min, ALU.mult))
                for hf in range(2)],
               v.tensor_mul(g2st[mm % 2], gq2[mm % 2], sgq_f4),
               v.tensor_reduce(gm_acc[:, mm % 24:mm % 24 + 1],
                               g2st[mm % 2], AX,
                               ALU.max, apply_absolute_value=True
                               ).then_inc(dv.h, 1)))
            sp.bump(16)
            S(lambda s, mm=mm, w=n2, sp=sp:
              (s.wait_ge(dv.h, w),
               s.dma_start(out=g_spill[128 * mm:128 * (mm + 1), :],
                           in_=g2st[mm % 2]).then_inc(sp.h, 16)))

        for m in range(NF):
            wd_item(m)
            stgc = (scr1_b, e_sb)[m % 2]
            sr = srb[m % 2]
            tgt = sr.bump(16)
            Y(lambda q, m=m, b2=bar2_dv, stgc=stgc, sr=sr,
              w=(f4_n1[m - 2] if m >= 2 else None):
              (q.wait_ge(dv.h, b2) if m < 2 else q.wait_ge(dv.h, w),
               q.dma_start(out=stgc,
                           in_=g_spill[128 * m:128 * (m + 1), :]
                           ).then_inc(sr.h, 16)))
            f4_n1.append(dv.bump())
            V(lambda v, m=m, tgt=tgt, stgc=stgc, sr=sr:
              (v.wait_ge(sr.h, tgt),
               [(v.tensor_scalar(scrf_i[:, 0:1024],
                                 stgc[:, 1024 * hf:1024 * (hf + 1)],
                                 scl_bc[:, 3:4], None, ALU.mult),
                 v.tensor_scalar(gq2[m % 2][:, 1024 * hf:1024 * (hf + 1)],
                                 scrf_i[:, 0:1024], scl_bc[:, 2:3],
                                 None, ALU.mult)) for hf in range(2)
                ][-1][1].then_inc(dv.h, 1)))
            if m >= 1:
                f4_stage2(m - 1)
        f4_stage2(NF - 1)
        f4_sp = (ssp[0].n, ssp[1].n)
        tap("g2", None, wait=(lambda q, a=ssp[0].n, b=ssp[1].n:
                              (q.wait_ge(ssp[0].h, a), q.wait_ge(ssp[1].h, b))),
            dram_src=g_spill[0:128, :])

        wdT_tr = swd.n

        bar3_dv = barrier(2, [(gm_acc, 24)], 6)

        plan.mark("f4")
        for m in range(NF):
            stgc = (scr1_b, e_sb)[m % 2]
            sr = srb[m % 2]
            sp = ssp[m % 2]
            wprev = dv.n
            tgt0 = sr.n
            sr.bump(32)
            Y(lambda q, m=m, b3=bar3_dv, w=wprev, stgc=stgc, sr=sr,
              fs=f4_sp:
              ((q.wait_ge(dv.h, b3),
                q.wait_ge(ssp[0].h, fs[0]), q.wait_ge(ssp[1].h, fs[1]))
               if m < 2 else q.wait_ge(dv.h, w - 1),
               q.dma_start(out=stgc,
                           in_=u_spill[128 * m:128 * (m + 1), :]
                           ).then_inc(sr.h, 16),
               q.dma_start(out=g2st[m % 2],
                           in_=g_spill[128 * m:128 * (m + 1), :]
                           ).then_inc(sr.h, 16)))
            sp_guard = sp.n
            n2 = dv.bump()
            V(lambda v, m=m, tgt=sr.n, stgc=stgc, sr=sr, sp=sp, spg=sp_guard:
              (v.wait_ge(sr.h, tgt),
               v.wait_ge(sp.h, spg) if m >= 2 else None,
               [(v.tensor_scalar(scrf_i[:, 0:1024],
                                 stgc[:, 1024 * hf:1024 * (hf + 1)],
                                 scl_bc[:, 5:6], None, ALU.mult),
                 v.tensor_scalar(sgq_c[:, 1024 * hf:1024 * (hf + 1)],
                                 scrf_i[:, 0:1024], scl_bc[:, 4:5],
                                 None, ALU.mult),
                 v.tensor_scalar(scrf_i[:, 0:1024],
                                 g2st[m % 2][:, 1024 * hf:1024 * (hf + 1)],
                                 scl_bc[:, 7:8], None, ALU.mult),
                 v.tensor_scalar(gq_c[:, 1024 * hf:1024 * (hf + 1)],
                                 scrf_i[:, 0:1024], scl_bc[:, 6:7],
                                 None, ALU.mult)) for hf in range(2)],
               v.tensor_mul(oost[m % 2], gq_c, sgq_c),
               v.tensor_reduce(um_acc[:, m % 24:m % 24 + 1],
                               oost[m % 2], AX,
                               ALU.max, apply_absolute_value=True
                               ).then_inc(dv.h, 1)))
            sp.bump(16)
            S(lambda s, m=m, w=n2, sp=sp:
              (s.wait_ge(dv.h, w),
               s.dma_start(out=u_spill[128 * m:128 * (m + 1), :],
                           in_=oost[m % 2]).then_inc(sp.h, 16)))
        f5_sp = (ssp[0].n, ssp[1].n)
        tap("o", None, wait=(lambda q, a=ssp[0].n, b=ssp[1].n:
                             (q.wait_ge(ssp[0].h, a), q.wait_ge(ssp[1].h, b))),
            dram_src=u_spill[0:128, :])

        bar4_dv = barrier(3, [(um_acc, 24)], 8)

        plan.mark("f5")
        f6_quant = []
        f6_out = []
        for tt in range(NT):
            so_ = sot[tt % 2]
            sx = sxq[tt % 2]
            su = sou[tt % 2]
            ot_tgt = so_.bump(16)
            Y(lambda q, tt=tt, b4=bar4_dv, fs=f5_sp, so_=so_,
              pv=(f6_quant[tt - 2] if tt >= 2 else None):
              ((q.wait_ge(ssp[0].h, fs[0]), q.wait_ge(ssp[1].h, fs[1]),
                q.wait_ge(dv.h, b4)) if tt == 0 else None,
               q.wait_ge(dv.h, pv) if pv is not None else None,
               q.dma_start(
                   out=o_tile[tt % 2][:].rearrange("p (ci f) -> p ci f", f=128),
                   in_=u_spill.rearrange("(ci p) t -> p ci t", p=128
                                         )[:, :, 128 * tt:128 * (tt + 1)],
               ).then_inc(so_.h, 16)))
            out_guard = su.n
            x_tgt = sx.bump(16)
            Y(lambda q, tt=tt, su=su, sx=sx, og=out_guard:
              (q.wait_ge(su.h, og) if tt >= 2 else None,
               q.dma_start(out=stg_f[tt % 2],
                           in_=x1_spill[128 * tt:128 * (tt + 1), :]
                           ).then_inc(sx.h, 16)))
            nq = dv.bump()
            f6_quant.append(nq)
            V(lambda v, tt=tt, ot=ot_tgt, b4=bar4_dv, so_=so_:
              (v.wait_ge(so_.h, ot),
               v.wait_ge(dv.h, b4) if tt == 0 else None,
               [(v.tensor_scalar(scrf_i[:, 0:1024],
                                 o_tile[tt % 2][:, 1024 * s3:1024 * (s3 + 1)],
                                 scl_bc[:, 9:10], None, ALU.mult),
                 v.tensor_scalar(o_tile[tt % 2][:, 1024 * s3:1024 * (s3 + 1)],
                                 scrf_i[:, 0:1024], scl_bc[:, 8:9],
                                 None, ALU.mult)) for s3 in range(3)
                ][-1][1].then_inc(dv.h, 1)))
            dn_pes = []
            for hh in range(2):
                ps = (pA, pB)[hh]
                dn_pes.append(pe.bump())
                TN(lambda tn, tt=tt, hh=hh, ps=ps, w=nq, dvw=dv.n:
                   (tn.wait_ge(dv.h, w),
                    tn.wait_ge(swd.h, wdT_tr) if tt == 0 else None,
                    tn.wait_ge(dv.h, dvw) if tt > 0 else None,
                    [tn.matmul(ps[:, 0:512],
                               o_tile[tt % 2][:, 128 * ci:128 * (ci + 1)],
                               wgT_b[:, 1024 * ci + 512 * hh:
                                     1024 * ci + 512 * (hh + 1)],
                               start=(ci == 0), stop=(ci == NF - 1))
                     for ci in range(NF)][-1].then_inc(pe.h, 1)))
            nev = dv.bump()
            V(lambda v, tt=tt, w=dn_pes, xt=x_tgt, sx=sx:
              (v.wait_ge(pe.h, w[1]),
               v.wait_ge(sx.h, xt),
               v.scalar_tensor_tensor(scrf_f[:, 0:512], pA[:, 0:512], 0.0,
                                      stg_f[tt % 2][:, 0:512],
                                      ALU.bypass, ALU.add),
               v.scalar_tensor_tensor(scrf_f[:, 512:1024], pB[:, 0:512], 0.0,
                                      stg_f[tt % 2][:, 512:1024],
                                      ALU.bypass, ALU.add),
               v.reduce_sum(st1_s[:, 0:1], scrf_f[:, 0:1024], AX),
               v.tensor_mul(scr1_f[:, 0:1024], scrf_f[:, 0:1024],
                            scrf_f[:, 0:1024]),
               v.reduce_sum(st1_s[:, 1:2], scr1_f[:, 0:1024], AX),
               [(v.tensor_scalar(st1_s[:, 2:3], st1_s[:, 0:1], 1.0 / D,
                                 None, ALU.mult),
                 v.tensor_scalar(st1_s[:, 3:4], st1_s[:, 1:2], 1.0 / D,
                                 None, ALU.mult),
                 v.tensor_mul(st1_s[:, 4:5], st1_s[:, 2:3], st1_s[:, 2:3]),
                 v.tensor_sub(st1_s[:, 7:8], st1_s[:, 3:4], st1_s[:, 4:5]))
                for _r in range(2)][-1][-1].then_inc(dv.h, 1)))
            nsd = sc.bump()
            S(lambda s, w=nev:
              (s.wait_ge(dv.h, w),
               s.activation(st1_s[:, 5:6], st1_s[:, 7:8], AF.Sqrt,
                            bias=eps_s).then_inc(sc.h, 1)))
            nap = dv.bump()
            V(lambda v, tt=tt, w=nsd:
              (v.wait_ge(sc.h, w),
               [v.memset(scr1_f[:, 0:1], 0.0) for _p in range(4)][-1],
               v.reciprocal(st1_s[:, 6:7], st1_s[:, 5:6]),
               v.reciprocal(st1_s[:, 6:7], st1_s[:, 5:6]),
               v.memset(scr1_f[:, 0:1], 0.0),
               v.memset(scr1_f[:, 0:1], 0.0),
               v.tensor_scalar(stg_f[tt % 2][:, 0:1024], scrf_f[:, 0:1024],
                               st1_s[:, 2:3], st1_s[:, 6:7],
                               ALU.subtract, ALU.mult).then_inc(dv.h, 1)))
            su.bump(16)
            Y(lambda q, tt=tt, w=nap, su=su:
              (q.wait_ge(dv.h, w),
               q.dma_start(out=out_h[128 * tt:128 * (tt + 1), :],
                           in_=stg_f[tt % 2]).then_inc(su.h, 16)))
            f6_out.append(su.n)

        Y(lambda q, a=sou[0].n, b=sou[1].n:
          (q.wait_ge(sou[0].h, a), q.wait_ge(sou[1].h, b),
           q.wait_ge(dbg_sem[0].h, dbg_sem[0].n) if _TAP is not None else None))

        globals()["_DBGS"] = dict(
            sld0=(sld[0].h, sld[0].n), sld1=(sld[1].h, sld[1].n),
            ld=(ld.h, ld.n), strx=(strx.h, strx.n), skt=(skt.h, skt.n),
            stq0=(stq[0].h, stq[0].n), stq1=(stq[1].h, stq[1].n),
            swg=(swg.h, swg.n), swu=(swu.h, swu.n), swd=(swd.h, swd.n),
            srb0=(srb[0].h, srb[0].n), srb1=(srb[1].h, srb[1].n),
            sxq0=(sxq[0].h, sxq[0].n), sxq1=(sxq[1].h, sxq[1].n),
            ssp0=(ssp[0].h, ssp[0].n), ssp1=(ssp[1].h, ssp[1].n),
            pe=(pe.h, pe.n), dv=(dv.h, dv.n), sc=(sc.h, sc.n), cc=(cc.h, cc.n),
            ldg=(ldg.h, ldg.n),
        )
        if _STOP is not None:
            cut = plan.marks[_STOP]
            for e in ENGS:
                plan.ops[e] = plan.ops[e][:cut[e]]

        def _phase_bounds(e):
            return sorted((v[e], k) for k, v in plan.marks.items())

        _esuf = {"sync": "y", "tensor": "t", "vector": "v",
                 "scalar": "c", "gpsimd": "g"}

        def _emit(eng_dev, e):
            bounds = _phase_bounds(e)
            bi = 0
            for i, fn in enumerate(plan.ops[e]):
                while bi < len(bounds) and i >= bounds[bi][0]:
                    bi += 1
                ph = bounds[bi][1] if bi < len(bounds) else "tail"
                with nc.named_scope(ph + "." + _esuf[e]):
                    fn(eng_dev)

        block = est.enter_context(nc.Block())

        @block.sync
        def _(q):
            _emit(q, "sync")

        @block.tensor
        def _(tn):
            _emit(tn, "tensor")

        @block.vector
        def _(v):
            _emit(v, "vector")

        @block.scalar
        def _(s):
            _emit(s, "scalar")

        @block.gpsimd
        def _(q):
            _emit(q, "gpsimd")

    return nc
def _tap_out(dbg_h, s, d):
    ap = d if d is not None else s
    if len(ap.shape) == 2 and ap.shape[0] <= 128:
        rows, cols = ap.shape
        eltb = {mybir.dt.float32: 4, mybir.dt.bfloat16: 2,
                mybir.dt.int32: 4}[ap.dtype]
        ncols = cols * eltb // 4
        return dbg_h[0:rows, 0:ncols].bitcast(ap.dtype)
    raise ValueError(ap.shape)


_NC = None


def _constants(core):
    b, h = divmod(core, 2)
    pos = np.arange(2048 * h, 2048 * (h + 1), dtype=np.float64)
    inv = 1.0 / (10000.0 ** (np.arange(0, 64, 2, dtype=np.float64) / 64.0))
    ang = pos[:, None] * inv[None, :]
    cosv = np.cos(ang).astype(np.float32)
    sinv = np.sin(ang).astype(np.float32)
    rows = np.arange(128) % 64 % 32
    cos_b = np.ascontiguousarray(cosv[:, rows].T).astype(ml_dtypes.bfloat16)
    sin_b = np.ascontiguousarray(sinv[:, rows].T).astype(ml_dtypes.bfloat16)
    kt = np.arange(128)[:, None]
    qt = np.arange(128)[None, :]
    maskt = (kt <= qt).astype(np.float32).astype(ml_dtypes.bfloat16)
    p2l = np.zeros((128, 128), np.float32)
    for base in (0, 64):
        for i in range(32):
            p2l[base + i + 32, base + i] = -1.0
            p2l[base + i, base + i + 32] = 1.0
    p2l = p2l.astype(ml_dtypes.bfloat16)
    onesc = np.ones((128, 1), ml_dtypes.bfloat16)
    onesr = np.ones((1, 128), np.float32)
    eo = np.zeros((128, 2), np.float32)
    eo[:, 0] = 1.0 if h == 0 else 0.0
    eo[:, 1] = 1.0 - eo[:, 0]
    return dict(cosb=cos_b, sinb=sin_b, maskt=maskt, p2l=p2l, onesc=onesc,
                onesr=onesr, eo=eo)


def kernel(L0_attn_x, L0_attn_Wq, L0_attn_Wk, L0_attn_Wv, L0_attn_Wo,
           L0_attn_Wg1, L0_attn_Wg2, L0_attn_ln_w, L0_attn_ln_b,
           L0_ffn_w_gate, L0_ffn_w_up, L0_ffn_w_down):
    global _NC
    if _NC is None:
        _NC = build_nc()
    x = np.ascontiguousarray(np.asarray(L0_attn_x, np.float32))
    lnw = np.ascontiguousarray(
        np.asarray(L0_attn_ln_w, np.float32).reshape(8, 128).T)
    lnb = np.ascontiguousarray(
        np.asarray(L0_attn_ln_b, np.float32).reshape(8, 128).T)
    base = dict(
        wq=np.ascontiguousarray(L0_attn_Wq, dtype=np.float32),
        wk=np.ascontiguousarray(L0_attn_Wk, dtype=np.float32),
        wv=np.ascontiguousarray(L0_attn_Wv, dtype=np.float32),
        wo=np.ascontiguousarray(L0_attn_Wo, dtype=np.float32),
        wg1=np.ascontiguousarray(L0_attn_Wg1, dtype=np.float32),
        wg2=np.ascontiguousarray(L0_attn_Wg2, dtype=np.float32),
        lnw=lnw, lnb=lnb,
        wgate=np.ascontiguousarray(L0_ffn_w_gate, dtype=np.float32),
        wup=np.ascontiguousarray(L0_ffn_w_up, dtype=np.float32),
        wdown=np.ascontiguousarray(L0_ffn_w_down, dtype=np.float32),
    )
    in_maps = []
    for c in range(8):
        b, h = divmod(c, 2)
        m = dict(base)
        m["x"] = np.ascontiguousarray(x[b, 2048 * h:2048 * (h + 1), :])
        m.update(_constants(c))
        in_maps.append(m)
    res = run_bass_kernel_spmd(_NC, in_maps, core_ids=list(range(8)))
    out = np.empty((4, 4096, 1024), np.float32)
    for c in range(8):
        b, h = divmod(c, 2)
        out[b, 2048 * h:2048 * (h + 1), :] = res.results[c]["out"]
    return out


if __name__ == "__main__":
    build_nc()
    print("build ok")
